# version 14
# baseline (speedup 1.0000x reference)
import os
import numpy as np
import ml_dtypes
import concourse.bass as bass
import concourse.mybir as mybir
from concourse.bass_types import AP

F32 = mybir.dt.float32
BF16 = mybir.dt.bfloat16
AF = mybir.ActivationFunctionType
OP = mybir.AluOpType
AX = mybir.AxisListType.X

NCORES = 8
TC = 1024
T = 8192
D = 2048
NH = 16
RK = 32
HR = NH * RK
HPC = 2
EE = 32
SCALE = 1.0 / float(np.sqrt(RK))
EPS = 1e-5
BIG = float(np.finfo(np.float32).max)
MSAFE = 1.03

N_G = 4
N_QT = 2
QW = 512
N_KC = 64
GKC = N_QT * N_G * N_KC
PREP_SZ = 3 * D * 64 + 64 * D


def build(debug=False):
    nc = bass.Bass("TRN2", target_bir_lowering=False, debug=debug)

    din = lambda name, shape, dt: nc.dram_tensor(name, shape, dt, kind="ExternalInput").ap()
    d_xs = din("xs", [TC, D], F32)
    d_wq = din("wq_r", [D, D], F32)
    d_wk = din("wk_r", [D, D], F32)
    d_wv = din("wv_r", [D, D], F32)
    d_wo = din("wo_c", [D, D], F32)
    d_wdn = din("wdown", [128, RK], F32)
    d_wup = din("wup", [RK, 128], F32)
    d_sim = din("sim", [D, EE], F32)
    d_gates = din("gates", [1, EE], F32)
    d_wvf = din("wvf", [EE, 8 * D], F32)
    d_owf = din("owf", [EE, 8 * D], F32)
    d_idb = din("identb", [128, 128], BF16)
    d_idf = din("identf", [128, 128], F32)
    d_lt = din("ltmask", [EE, EE], F32)
    d_exp4 = din("expander", [4, 128], F32)
    d_bdg = din("bdiag", [128, 4], BF16)
    d_out = nc.dram_tensor("out", [TC, D], F32, kind="ExternalOutput").ap()

    d_woe = nc.dram_tensor("woe_d", [HR, D], BF16)
    d_ksh = nc.dram_tensor("ksh", [HR * TC + 16], BF16)
    d_kg = nc.dram_tensor("kg", [NCORES * (HR * TC + 16)], BF16)
    d_vsh = nc.dram_tensor("vsh", [TC, HR], BF16)
    d_vg = nc.dram_tensor("vg", [NCORES * TC, HR], BF16)
    d_seqp = nc.dram_tensor("seqp", [1, D], F32)
    d_seqs = nc.dram_tensor("seqs", [1, D], F32)
    d_ffnp = nc.dram_tensor("ffnp", [16, D], BF16)
    d_ffng = nc.dram_tensor("ffng", [NCORES * 16, D], BF16)

    from contextlib import ExitStack
    stack = ExitStack()
    SB = lambda name, shape, dt: stack.enter_context(nc.sbuf_tensor(name, shape, dt))
    PST = lambda name, shape, dt: stack.enter_context(nc.psum_tensor(name, shape, dt))
    SEM = lambda name: stack.enter_context(nc.semaphore(name))

    k_sb = [SB(f"k{g}", [128, T], BF16) for g in range(N_G)]
    v_sb = SB("v", [128, N_KC * HR], BF16)
    q_sb = [SB(f"q{g}", [128, TC], BF16) for g in range(N_G)]
    y_sb = [SB(f"y{g}", [128, TC], BF16) for g in range(N_G)]
    exp_sb = SB("exps", [128, 8 * 1024], BF16)
    identb = SB("identb_s", [128, 128], BF16)
    identf = SB("identf_s", [128, 128], F32)
    onesb = SB("onesb", [128, 1], BF16)
    onesf = SB("onesf", [128, 1], F32)
    zerob = SB("zerob", [128, 128], BF16)
    epsb = SB("epsb", [128, 1], F32)
    negMb = SB("negMb", [128, 1], F32)
    bdiag = SB("bdiag_s", [128, 4], BF16)
    exp4 = SB("exp4_s", [4, 128], F32)
    ltm = SB("ltm_s", [EE, EE], F32)
    wdnf = SB("wdnf", [128, RK], F32)
    wupf = SB("wupf", [RK, 128], F32)
    nrm = SB("nrm", [NH, 4], F32)
    nrmb = SB("nrmb", [NH, 1], BF16)
    kn8 = SB("kn8", [NH, NCORES], BF16)
    bn6 = SB("bn6", [128, 24], F32)
    mv2 = SB("mv2", [128, 8], F32)
    rstd = SB("rstd", [128, 4], F32)
    negm = SB("negm", [128, 4], F32)
    dsb = SB("dsb", [1, 4 * QW], F32)
    nrm4 = SB("nrm4", [4, 2], F32)
    bcsb = SB("bcsb", [128, QW], F32)
    gsc = SB("gsc", [EE, 16], F32)
    gmx = SB("gmx", [EE, EE], F32)
    gmx2 = SB("gmx2", [EE, EE], F32)
    gtmp = SB("gtmp", [EE, EE], F32)
    grow = SB("grow", [1, 256], F32)
    routb = SB("routb", [EE, 1], BF16)
    simsb = SB("simsb", [128, 512], F32)
    sim2s = SB("sim2s", [128, 512], F32)
    seqcol = SB("seqcol", [128, 16], F32)
    gatesb = SB("gatesb", [1, EE], F32)
    ffst = SB("ffst", [128, 512], BF16)

    def view(t, dt, byte_off, part, cols):
        a = t.ap().bitcast(dt)
        esz = 4 if dt == F32 else 2
        assert byte_off % esz == 0
        return a[0:part, byte_off // esz: byte_off // esz + cols]

    ln1T = ([view(k_sb[0], BF16, 2048 * i, 128, TC) for i in range(8)]
            + [view(k_sb[1], BF16, 2048 * i, 128, TC) for i in range(8)])
    A_q = [view(k_sb[2], BF16, 1024 * i, 128, 512) for i in range(16)]
    A_k = [view(k_sb[3], BF16, 1024 * i, 128, 512) for i in range(16)]
    A_v = [view(v_sb, BF16, 1024 * i, 128, 512) for i in range(16)]
    xst = [view(v_sb, F32, 16384 + 8192 * i, 128, D) for i in range(2)]
    ln1st = [view(v_sb, BF16, 32768 + 4096 * i, 128, D) for i in range(2)]
    wqst = [view(v_sb, F32, 40960 + 8192 * i, 128, D) for i in range(2)]
    wost = view(v_sb, F32, 57344, 128, D)
    wotb = [view(exp_sb, BF16, 4096 * i, 128, D) for i in range(2)]
    wupt = view(exp_sb, BF16, 8192, 128, RK)
    prep_st = view(exp_sb, BF16, 8320, 128, 1024)
    kst = prep_st
    sqst = view(exp_sb, BF16, 10368, 128, 512)
    vstg = view(y_sb[2], BF16, 0, 128, 512)
    woest_h = [view(y_sb[i], BF16, 0, 64, 1024) for i in range(2)]

    ln2n = [view(k_sb[tt // 4], BF16, 4096 * (tt % 4), 128, D) for tt in range(8)]
    ln3T = ([view(k_sb[2], BF16, 2048 * i, 128, TC) for i in range(8)]
            + [view(k_sb[3], BF16, 2048 * i, 128, TC) for i in range(8)])
    woe_v = [view(v_sb, BF16, 4096 * g, 128, D) for g in range(N_G)]
    xnst = [view(v_sb, F32, 16384 + 8192 * i, 128, D) for i in range(2)]
    ln3stg = [view(v_sb, BF16, 32768 + 4096 * i, 128, D) for i in range(2)]
    wvstage = view(v_sb, F32, 57344, 32, D)
    seqsb = view(exp_sb, F32, 0, 1, D)
    wvb = ([view(exp_sb, BF16, 1024 * i, 32, 512) for i in range(16)]
           + [view(v_sb, BF16, 40960 + 1024 * i, 32, 512) for i in range(16)])
    assert len(wvb) == 32
    wvgsb = view(v_sb, BF16, 32768, 64, D)
    wodsb = view(v_sb, BF16, 36864, 64, D)
    wveffT = [view(exp_sb, BF16, 128 * i, 128, 64) for i in range(16)]
    combT = view(exp_sb, BF16, 2048, 64, TC)
    xrest = [view(exp_sb, F32, 4096, 128, D), view(v_sb, F32, 57344, 128, D)]
    ffn_t = [view(v_sb, F32, 0, 128, D), view(v_sb, F32, 8192, 128, D),
             view(v_sb, F32, 40960, 128, D), view(v_sb, F32, 49152, 128, D),
             view(k_sb[2], F32, 0, 128, D), view(k_sb[2], F32, 8192, 128, D),
             view(k_sb[3], F32, 0, 128, D), view(k_sb[3], F32, 8192, 128, D)]

    p6 = PST("p6", [128, 3072], F32)
    pav = PST("pav", [128, 512], F32)
    pd = PST("pd", [128, 512], F32)

    def psview(ps, dt, byte_off, part, cols):
        a = ps.ap().bitcast(dt)
        esz = 4 if dt == F32 else 2
        return a[0:part, byte_off // esz: byte_off // esz + cols]

    pA = p6.ap()[:, 0:512]
    pAB = p6.ap()[:, 512:1024]
    pqn = p6.ap()[0:4, 1536:2048]
    ptr_b1 = psview(p6, BF16, 8192, 128, 1024)
    ptr_b1b = psview(p6, BF16, 2048, 128, 1024)
    pq2 = [p6.ap()[:, 2048 + 512 * i: 2048 + 512 * (i + 1)] for i in range(2)]
    ptr_b8 = psview(pav, BF16, 0, 128, 1024)
    pg = pd.ap()

    s, C = {}, {}
    for n in ["cload", "xload0", "xload1", "xre0", "xre1", "xre2", "xre3", "prep_s", "woedma", "wload0", "wload1", "lnv", "lns", "tr1", "wload", "prep_t", "prep_v",
              "prep_dma", "ag1", "aload", "qkv_t", "qkv_v", "shdma", "ag2",
              "kvload", "kvn", "mh_t", "mh_v", "mh_s", "tS", "act", "cons",
              "sgv1", "sgt2", "sgv2", "sgy", "woeload", "op_t", "op_v",
              "p8v", "p8s", "tr3", "seq_t", "seq_v", "seqdma", "ar",
              "seqload", "g_t", "g_v", "g_s", "simload", "ffwload",
              "ffw_v", "ffw_t", "ffw_e", "ffpdma", "ag3", "ffgload",
              "fftr_t", "fftr_v", "ffc_t", "ffc_v", "ffo_t", "ffo_v",
              "p11v", "p11s", "store"]:
        s[n] = SEM("s_" + n)
        C[n] = 0

    _CUT = [False]

    class _Q(list):
        def append(self, x):
            if not _CUT[0]:
                super().append(x)

    ops = {e: _Q() for e in "tvsqg"}
    tq, vq, sq, qq, gq = (ops[e] for e in "tvsqg")
    MK_CUT = os.environ.get("MK_CUT", "")

    def cutpoint(name):
        if MK_CUT == name and not _CUT[0]:
            for ttg in range(8):
                bump("store", 16)
                qq.append(lambda E, ttg=ttg: E.dma_start(
                    out=d_out[128 * ttg:128 * (ttg + 1), 0:128],
                    in_=identf[:, :]).then_inc(s["store"], 16))
            v = C["store"]
            gq.append(lambda E, v=v: E.wait_ge(s["store"], v))
            _CUT[0] = True

    def bump(name, amt=1):
        C[name] += amt
        return C[name]

    def dma(eq, sem_name, out, in_, **kw):
        bump(sem_name, 16)
        eq.append(lambda E, out=out, in_=in_, kw=kw, sn=sem_name:
                  E.dma_start(out=out, in_=in_, **kw).then_inc(s[sn], 16))

    def wait(eq, sem_name, val=None):
        v = C[sem_name] if val is None else val
        eq.append(lambda E, sn=sem_name, v=v: E.wait_ge(s[sn], v))

    def dinc(eq, sem_name):
        bump(sem_name)
        eq.append(lambda E, sn=sem_name: E.drain().then_inc(s[sn], 1))

    dma(qq, "cload", identb[:, :], d_idb)
    dma(qq, "cload", identf[:, :], d_idf)
    dma(qq, "cload", ltm[:, :], d_lt)
    dma(qq, "cload", exp4[:, :], d_exp4)
    dma(qq, "cload", bdiag[:, :], d_bdg)
    dma(qq, "cload", gatesb[:, :], d_gates)
    dma(qq, "cload", wdnf[:, :], d_wdn)
    dma(qq, "cload", wupf[:, :], d_wup)

    def f_memset(E):
        E.memset(onesb[:, :], 1.0)
        E.memset(onesf[:, :], 1.0)
        E.memset(zerob[:, :], 0.0)
        E.memset(epsb[:, :], EPS)
        E.memset(grow[:, :], 1.0)
        E.memset(nrm[:, :], 0.0)
        E.memset(nrmb[:, :], 0.0)
    vq.append(f_memset)
    dinc(vq, "cload")
    CLOAD = C["cload"]

    def ln_stats(E, src, b):
        for ii in range(4):
            E.bn_stats(out=bn6[:, 6 * ii:6 * ii + 6], in_=src[:, 512 * ii:512 * (ii + 1)])
        E.drain()
        E.bn_aggr(out=mv2[:, 2 * b:2 * b + 2], in_=bn6[:, :])
        E.drain()
        E.tensor_scalar(out=negm[:, b:b + 1], in0=mv2[:, 2 * b:2 * b + 1],
                        scalar1=-1.0, scalar2=None, op0=OP.mult)

    def emit_ln_s(sem_wait_v, sem_s, dst, src_, b):
        wait(sq, sem_wait_v, C[sem_wait_v])

        def f1(E, dst=dst, src_=src_, b=b):
            E.activation(out=rstd[:, b:b + 1], in_=mv2[:, 2 * b + 1:2 * b + 2],
                         func=AF.Ln, bias=epsb[:, 0:1], scale=1.0)
            E.drain()
            E.activation(out=rstd[:, b:b + 1], in_=rstd[:, b:b + 1],
                         func=AF.Exp, scale=-0.5)
            E.drain()
            E.activation(out=dst, in_=src_, func=AF.Identity,
                         bias=negm[:, b:b + 1], scale=1.0)
            E.drain()
            E.activation(out=dst, in_=dst, func=AF.Identity,
                         scale=rstd[:, b:b + 1])
        sq.append(f1)
        dinc(sq, sem_s)

    for tt in range(8):
        b = tt % 2
        if tt >= 2:
            wait(qq, "tr1", C["tr1"])
        dma(qq, "xload" + str(b), xst[b][:, :], d_xs[128 * tt:128 * (tt + 1), :])
        wait(vq, "xload" + str(b), C["xload" + str(b)])
        if tt == 0:
            wait(vq, "cload", CLOAD)
        vq.append(lambda E, b=b: ln_stats(E, xst[b], b))
        dinc(vq, "lnv")
        emit_ln_s("lnv", "lns", ln1st[b][:, :], xst[b][:, :], b)
        for half in range(2):
            wait(tq, "lns", C["lns"])
            if tt or half:
                wait(tq, "tr1", C["tr1"])

            def f_tr(E, b=b, half=half):
                for j in range(8):
                    dc = 8 * half + j
                    i2 = E.transpose(ptr_b1[:, 128 * j:128 * (j + 1)],
                                     ln1st[b][:, 128 * dc:128 * (dc + 1)], identb[:, :])
                    if j == 7:
                        i2.then_inc(s["tr1"], 1)
            tq.append(f_tr)
            bump("tr1")
            wait(vq, "tr1", C["tr1"])

            def f_ev(E, half=half, tt=tt):
                for j in range(8):
                    dc = 8 * half + j
                    i2 = E.tensor_copy(out=ln1T[dc][:, 128 * tt:128 * (tt + 1)],
                                       in_=ptr_b1[:, 128 * j:128 * (j + 1)])
                    if j == 7:
                        i2.then_inc(s["tr1"], 1)
            vq.append(f_ev)
            bump("tr1")
    TR1_ALL = C["tr1"]

    cutpoint('P1')
    woest32 = view(exp_sb, BF16, 12288, 32, D)

    def f_wdncast(E):
        E.tensor_copy(out=wupt[:, 0:RK], in_=wdnf[:, :])
    vq.append(f_wdncast)
    dinc(vq, "prep_v")
    WDN_V = C["prep_v"]
    wait(sq, "cload", CLOAD)
    wait(sq, "prep_v", WDN_V)
    prepdef = [(d_wq, A_q), (d_wk, A_k), (d_wv, A_v)]
    for wi, (dsrc, Adst) in enumerate(prepdef):
        for h in range(NH):
            hb = h % 2
            it = wi * NH + h
            if it >= 2:
                wait(qq, "prep_s", C["prep_s"] - 1)
            dma(qq, "wload" + str(hb), wqst[hb][:, :], dsrc[128 * h:128 * (h + 1), :])
            wait(sq, "wload" + str(hb), C["wload" + str(hb)])
            if it >= 2:
                wait(sq, "prep_t", C["prep_t"] - 1)

            def f_cast(E, hb=hb):
                E.activation(out=wotb[hb][:, :], in_=wqst[hb][:, :],
                             func=AF.Copy, scale=1.0)
            sq.append(f_cast)
            dinc(sq, "prep_s")
            wait(tq, "prep_s", C["prep_s"])
            if it >= 2:
                wait(tq, "prep_v", C["prep_v"] - 1)
            if it == 0:
                wait(tq, "prep_v", WDN_V)

            def f_amm(E, hb=hb, it=it):
                pbk = pA if it % 2 == 0 else pAB
                for dc in range(16):
                    i2 = E.matmul(pbk[:, 32 * dc:32 * dc + 32],
                                  wotb[hb][:, 128 * dc:128 * (dc + 1)], wupt[:, 0:RK],
                                  start=True, stop=True, skip_group_check=True)
                    if dc == 15:
                        i2.then_inc(s["prep_t"], 1)
            tq.append(f_amm)
            bump("prep_t")
            wait(vq, "prep_t", C["prep_t"])

            def f_aev(E, h=h, it=it, Adst=Adst):
                pbk = pA if it % 2 == 0 else pAB
                t0 = Adst[0]
                outap = AP(tensor=t0.tensor, offset=t0.offset + 32 * h,
                           ap=[[t0.ap[0][0], 128], [512, 16], [1, 32]])
                E.tensor_copy(out=outap, in_=pbk[:, 0:512]).then_inc(s["prep_v"], 1)
            vq.append(f_aev)
            bump("prep_v")
    PREP_A_V = C["prep_v"]

    wait(tq, "prep_v", C["prep_v"])
    tq.append(lambda E: E.transpose(pA[0:128, 0:32], wupf[:, :], identf[0:RK, 0:RK])
              .then_inc(s["prep_t"], 1))
    bump("prep_t")
    wait(vq, "prep_t", C["prep_t"])
    vq.append(lambda E: E.tensor_copy(out=wupt[:, 0:RK], in_=pA[0:128, 0:32])
              .then_inc(s["prep_v"], 1))
    bump("prep_v")
    wait(tq, "prep_v", C["prep_v"])

    def emit_wo_head(h):
        wait(qq, "prep_s", C["prep_s"])
        wo_r = AP(tensor=d_wo.tensor, offset=128 * h,
                  ap=[[D, 128], [128 * D, 16], [1, 128]])
        dma(qq, "wload0", wost[:, :], wo_r)
        wait(sq, "wload0", C["wload0"])
        wait(sq, "prep_t", C["prep_t"])

        def f_wocast(E):
            E.activation(out=wotb[0][:, :], in_=wost[:, :], func=AF.Copy, scale=1.0)
        sq.append(f_wocast)
        dinc(sq, "prep_s")
        wait(tq, "prep_s", C["prep_s"])
        wait(tq, "prep_v", C["prep_v"])
        for half in range(2):
            def f_wotr(E, half=half):
                for j in range(8):
                    dc = 8 * half + j
                    i2 = E.transpose(ptr_b1b[:, 128 * j:128 * (j + 1)],
                                     wotb[0][:, 128 * dc:128 * (dc + 1)], identb[:, :])
                    if j == 7:
                        i2.then_inc(s["prep_t"], 1)
            tq.append(f_wotr)
            bump("prep_t")
            wait(vq, "prep_t", C["prep_t"])

            def f_wotev(E, half=half):
                E.tensor_copy(out=wotb[1][:, 1024 * half:1024 * (half + 1)],
                              in_=ptr_b1b[:, 0:1024]).then_inc(s["prep_v"], 1)
            vq.append(f_wotev)
            bump("prep_v")
            wait(tq, "prep_v", C["prep_v"])
        wait(vq, "woedma", C["woedma"])
        for cc in range(4):
            def f_woemm(E, cc=cc):
                E.matmul(pA[0:32, 0:512], wupt[:, 0:RK],
                         wotb[1][:, 512 * cc:512 * (cc + 1)],
                         start=True, stop=True, skip_group_check=True
                         ).then_inc(s["prep_t"], 1)
            tq.append(f_woemm)
            bump("prep_t")
            wait(vq, "prep_t", C["prep_t"])

            def f_woeev(E, cc=cc):
                E.tensor_copy(out=woest32[0:32, 512 * cc:512 * (cc + 1)],
                              in_=pA[0:32, 0:512]).then_inc(s["prep_v"], 1)
            vq.append(f_woeev)
            bump("prep_v")
            wait(tq, "prep_v", C["prep_v"])
        wait(qq, "prep_v", C["prep_v"])
        dma(qq, "woedma", d_woe[32 * h:32 * (h + 1), :], woest32[0:32, :])
    cutpoint('P3')
    for g in range(N_G):
        if g == 0:
            wait(tq, "prep_v", PREP_A_V)
            wait(tq, "tr1", TR1_ALL)
        for th in range(2):
            if g or th:
                wait(tq, "qkv_v", C["qkv_v"])
            if g:
                wait(vq, "shdma", C["shdma"])

            def f_kmm(E, g=g, th=th):
                for dc in range(16):
                    i2 = E.matmul(pq2[th][:, :], A_k[dc][:, 128 * g:128 * (g + 1)],
                                  ln1T[dc][:, 512 * th:512 * (th + 1)],
                                  start=(dc == 0), stop=(dc == 15))
                    if dc == 15:
                        i2.then_inc(s["qkv_t"], 1)
            tq.append(f_kmm)
            bump("qkv_t")
            wait(vq, "qkv_t", C["qkv_t"])

            def f_kev(E, th=th):
                E.tensor_copy(out=kst[:, 512 * th:512 * (th + 1)],
                              in_=pq2[th][:, :]).then_inc(s["qkv_v"], 1)
            vq.append(f_kev)
            bump("qkv_v")
        wait(qq, "qkv_v", C["qkv_v"])
        kshout = AP(tensor=d_ksh.ap().tensor, offset=128 * g * TC,
                    ap=[[TC, 128], [1, TC]])
        dma(qq, "shdma", kshout, kst[:, 0:1024])
        for th in range(2):
            def f_ksq(E, th=th):
                E.tensor_tensor(out=sqst[:, :], in0=kst[:, 512 * th:512 * (th + 1)],
                                in1=kst[:, 512 * th:512 * (th + 1)], op=OP.mult)
            vq.append(f_ksq)
            dinc(vq, "qkv_v")
            wait(tq, "qkv_v", C["qkv_v"])
            tq.append(lambda E: E.matmul(pqn, bdiag[:, :], sqst[:, :], start=True,
                                         stop=True, skip_group_check=True)
                      .then_inc(s["qkv_t"], 1))
            bump("qkv_t")
            wait(vq, "qkv_t", C["qkv_t"])

            def f_knred(E, g=g):
                E.tensor_reduce(out=nrm4[0:4, 0:1], in_=pqn, axis=AX, op=OP.max)
                E.drain()
                E.tensor_tensor(out=nrm[0:4, 1:2], in0=nrm[0:4, 1:2],
                                in1=nrm4[0:4, 0:1], op=OP.max)
            vq.append(f_knred)
            dinc(vq, "qkv_v")
            wait(tq, "qkv_v", C["qkv_v"])
    wait(tq, "qkv_v", C["qkv_v"])
    tq.append(lambda E: E.transpose(pqn[0:1, 0:4], nrm[0:4, 1:2], identf[0:4, 0:4])
              .then_inc(s["qkv_t"], 1))
    bump("qkv_t")
    wait(vq, "qkv_t", C["qkv_t"])

    def f_knfin(E):
        E.tensor_reduce(out=nrm[0:1, 1:2], in_=pqn[0:1, 0:4], axis=AX, op=OP.max)
        E.drain()
        E.tensor_copy(out=nrmb[0:1, 0:1], in_=nrm[0:1, 1:2])
    vq.append(f_knfin)
    dinc(vq, "qkv_v")
    wait(qq, "qkv_v", C["qkv_v"])
    knap = AP(tensor=d_ksh.ap().tensor, offset=HR * TC, ap=[[1, 16], [1, 1]])
    dma(qq, "shdma", knap, nrmb[0:16, 0:1], allow_slow_non_contiguous=True)

    wait(gq, "shdma", C["shdma"])
    gq.append(lambda E: E.collective_compute(
        "AllGather", OP.bypass, replica_groups=[list(range(NCORES))],
        ins=[d_ksh.ap().opt()], outs=[d_kg.ap().opt()]).then_inc(s["ag2"], 1))
    bump("ag2")

    for tt in range(8):
        wait(tq, "shdma", C["shdma"])

        def f_vmm(E, tt=tt):
            for dc in range(16):
                i2 = E.matmul(pq2[0][:, :], ln1T[dc][:, 128 * tt:128 * (tt + 1)],
                              A_v[dc][:, :], start=(dc == 0), stop=(dc == 15))
                if dc == 15:
                    i2.then_inc(s["qkv_t"], 1)
        tq.append(f_vmm)
        bump("qkv_t")
        wait(vq, "qkv_t", C["qkv_t"])
        wait(vq, "shdma", C["shdma"])
        vq.append(lambda E: E.tensor_copy(out=vstg[:, :], in_=pq2[0][:, :])
                  .then_inc(s["qkv_v"], 1))
        bump("qkv_v")
        wait(qq, "qkv_v", C["qkv_v"])
        dma(qq, "shdma", d_vsh[128 * tt:128 * (tt + 1), :], vstg[:, :])

    wait(gq, "shdma", C["shdma"])
    gq.append(lambda E: E.collective_compute(
        "AllGather", OP.bypass, replica_groups=[list(range(NCORES))],
        ins=[d_vsh.ap().opt()], outs=[d_vg.ap().opt()]).then_inc(s["ag2"], 1))
    bump("ag2")

    for g in range(N_G):
        for th in range(2):
            wait(tq, "qkv_v", C["qkv_v"])

            def f_qmm(E, g=g, th=th):
                for dc in range(16):
                    i2 = E.matmul(pq2[1][:, :], A_q[dc][:, 128 * g:128 * (g + 1)],
                                  ln1T[dc][:, 512 * th:512 * (th + 1)],
                                  start=(dc == 0), stop=(dc == 15))
                    if dc == 15:
                        i2.then_inc(s["qkv_t"], 1)
            tq.append(f_qmm)
            bump("qkv_t")
            wait(vq, "qkv_t", C["qkv_t"])

            def f_qev(E, g=g, th=th):
                E.tensor_copy(out=q_sb[g].ap()[:, 512 * th:512 * (th + 1)],
                              in_=pq2[1][:, :])
                E.drain()
                E.tensor_tensor(out=sqst[:, :],
                                in0=q_sb[g].ap()[:, 512 * th:512 * (th + 1)],
                                in1=q_sb[g].ap()[:, 512 * th:512 * (th + 1)], op=OP.mult)
            vq.append(f_qev)
            dinc(vq, "qkv_v")
            wait(tq, "qkv_v", C["qkv_v"])
            tq.append(lambda E: E.matmul(pqn, bdiag[:, :], sqst[:, :], start=True,
                                         stop=True, skip_group_check=True)
                      .then_inc(s["qkv_t"], 1))
            bump("qkv_t")
            wait(vq, "qkv_t", C["qkv_t"])

            def f_qnred(E, g=g):
                E.tensor_reduce(out=nrm4[0:4, 0:1], in_=pqn, axis=AX, op=OP.max)
                E.drain()
                E.tensor_tensor(out=nrm[0:4, 0:1], in0=nrm[0:4, 0:1],
                                in1=nrm4[0:4, 0:1], op=OP.max)
            vq.append(f_qnred)
            dinc(vq, "qkv_v")
            wait(tq, "qkv_v", C["qkv_v"])
    QKV_T_ALL = C["qkv_t"]
    QKV_V_ALL = C["qkv_v"]

    for h in range(NH):
        emit_wo_head(h)

    cutpoint('P4')
    wait(gq, "ag2", C["ag2"])
    wait(gq, "qkv_t", QKV_T_ALL)
    wait(gq, "qkv_v", QKV_V_ALL)
    wait(gq, "woedma", C["woedma"])
    knin = AP(tensor=d_kg.ap().tensor, offset=HR * TC,
              ap=[[1, 1], [HR * TC + 16, 8]])
    dma(gq, "kvn", kn8[0:1, 0:NCORES], knin, allow_slow_non_contiguous=True)
    for r in range(NCORES):
        for g in range(N_G):
            inap = AP(tensor=d_kg.ap().tensor,
                      offset=r * (HR * TC + 16) + 128 * g * TC,
                      ap=[[TC, 128], [1, TC]])
            outap = AP(tensor=k_sb[g].ap().tensor, offset=r * TC,
                       ap=[[T, 128], [1, TC]])
            dma(gq, "kvload", outap, inap)
        inap = AP(tensor=d_vg.ap().tensor, offset=r * TC * HR,
                  ap=[[HR, 128], [128 * HR, 8], [1, HR]])
        outap = AP(tensor=v_sb.ap().tensor, offset=r * 8 * 512,
                   ap=[[N_KC * HR, 128], [512, 8], [1, 512]])
        dma(gq, "kvload", outap, inap)
        wait(gq, "kvload", C["kvload"])
    KVLOAD = C["kvload"]

    def kv_need(kc):
        return 16 * 5 * (kc // 8 + 1)

    wait(vq, "kvn", 16)
    wait(tq, "qkv_v", QKV_V_ALL)
    tq.append(lambda E: E.transpose(pqn[0:1, 8:12], nrm[0:4, 0:1],
                                    identf[0:4, 0:4]).then_inc(s["mh_t"], 1))
    bump("mh_t")
    wait(vq, "mh_t", C["mh_t"])

    def f_mh1(E):
        E.tensor_reduce(out=nrm[0:1, 0:1], in_=pqn[0:1, 8:12], axis=AX, op=OP.max)
        E.drain()
        E.tensor_reduce(out=nrm[0:1, 1:2], in_=kn8[0:1, 0:NCORES], axis=AX, op=OP.max)
    vq.append(f_mh1)
    dinc(vq, "mh_v")
    wait(sq, "mh_v", C["mh_v"])

    def f_mh2(E):
        E.tensor_tensor(out=nrm[0:1, 2:3], in0=nrm[0:1, 0:1], in1=nrm[0:1, 1:2],
                        op=OP.mult)
    vq.append(f_mh2)
    dinc(vq, "mh_v")
    wait(sq, "mh_v", C["mh_v"])

    def f_mhs(E):
        E.activation(out=nrm[0:1, 2:3], in_=nrm[0:1, 2:3], func=AF.Ln, scale=1.0)
        E.drain()
        E.activation(out=nrm[0:1, 2:3], in_=nrm[0:1, 2:3], func=AF.Exp, scale=0.5)
    sq.append(f_mhs)
    dinc(sq, "mh_s")
    wait(vq, "mh_s", C["mh_s"])

    def f_mh3(E):
        E.tensor_scalar(out=grow[0:1, 33:34], in0=nrm[0:1, 2:3],
                        scalar1=float(-SCALE * MSAFE), scalar2=None, op0=OP.mult)
    vq.append(f_mh3)
    dinc(vq, "mh_v")
    wait(tq, "mh_v", C["mh_v"])
    tq.append(lambda E: E.matmul(pg[0:128, 16:17], grow[0:1, 128:256],
                                 grow[0:1, 33:34], start=True, stop=True,
                                 skip_group_check=True).then_inc(s["mh_t"], 1))
    bump("mh_t")
    wait(vq, "mh_t", C["mh_t"])
    vq.append(lambda E: E.tensor_copy(out=negMb[:, :], in_=pg[0:128, 16:17])
              .then_inc(s["mh_v"], 1))
    bump("mh_v")
    MH_DONE = C["mh_v"]

    cutpoint('P5')
    def qt_sg_kc(i):
        return i // (N_G * N_KC), (i // N_KC) % N_G, i % N_KC

    kv_hi = [0]

    def emit_unit(u):
        i = u // 2
        qt, sg, kc = qt_sg_kc(i)
        if u >= 3:
            wait(tq, "act", u - 2)
        if u == 0:
            wait(tq, "mh_t", C["mh_t"])
        if kv_need(kc) > kv_hi[0]:
            kv_hi[0] = kv_need(kc)
            wait(tq, "kvload", kv_hi[0])
        p = u % 3
        hls = (0, 1) if u % 2 == 0 else (2, 3)

        def f_sc(E, qt=qt, sg=sg, kc=kc, p=p, hls=hls):
            for hl in hls:
                i2 = E.matmul(
                    p6.ap()[:, 1024 * p + 512 * (hl % 2): 1024 * p + 512 * (hl % 2) + 512],
                    k_sb[sg].ap()[32 * hl:32 * hl + 32, 128 * kc:128 * (kc + 1)],
                    q_sb[sg].ap()[32 * hl:32 * hl + 32, QW * qt:QW * qt + QW],
                    start=True, stop=True, tile_position=(32 * hl, 0))
                if hl % 2 == 1:
                    i2.then_inc(s["tS"], 1)
        tq.append(f_sc)
        bump("tS")

    def emit_act(u):
        wait(sq, "tS", u + 1)
        if u >= 8:
            wait(sq, "cons", (u - 8) // 2 + 1)
        if u == 0:
            wait(sq, "mh_v", MH_DONE)
        p, slot = u % 3, u % 8

        def f_act(E, p=p, slot=slot):
            E.activation(out=exp_sb.ap()[:, 1024 * slot:1024 * (slot + 1)],
                         in_=p6.ap()[:, 1024 * p:1024 * (p + 1)],
                         func=AF.Exp, bias=negMb[:, 0:1],
                         scale=float(SCALE)).then_inc(s["act"], 1)
        sq.append(f_act)
        bump("act")

    def emit_avd(j):
        qt, sg, kc = qt_sg_kc(j)
        opens = (kc == 0)
        close = (kc == N_KC - 1)
        wait(tq, "act", 2 * j + 2)
        if opens:
            if j > 0:
                wait(tq, "sgv1", C["sgv1"])

                def f_bc(E):
                    for hl in range(4):
                        i2 = E.matmul(pd.ap()[32 * hl:32 * hl + 32, :],
                                      grow[0:1, 128:160], dsb[0:1, QW * hl:QW * hl + QW],
                                      start=True, stop=True, skip_group_check=True,
                                      tile_position=(0, 32 * hl))
                        if hl == 3:
                            i2.then_inc(s["sgt2"], 1)
                tq.append(f_bc)
                bump("sgt2")
            wait(tq, "sgv2", C["sgv2"])

            def f_open(E):
                E.matmul(pav.ap()[:, :], zerob[:, :], k_sb[0].ap()[:, 0:512],
                         start=True, stop=False, skip_group_check=True)
                E.matmul(pd.ap()[:, :], zerob[:, :], k_sb[0].ap()[:, 0:512],
                         start=True, stop=False, skip_group_check=True)
            tq.append(f_open)
        su = 2 * j
        slotA, slotB = su % 8, (su + 1) % 8

        def f_avd(E, sg=sg, kc=kc, slotA=slotA, slotB=slotB, close=close):
            for hl in range(4):
                slot = slotA if hl < 2 else slotB
                rhs = exp_sb.ap()[:, 1024 * slot + 512 * (hl % 2):
                                  1024 * slot + 512 * (hl % 2) + 512]
                h = 4 * sg + hl
                E.matmul(pav.ap()[32 * hl:32 * hl + 32, :],
                         v_sb.ap()[:, 512 * kc + 32 * h: 512 * kc + 32 * h + 32],
                         rhs, start=False, stop=close,
                         tile_position=(0, 32 * hl), skip_group_check=True)
            for hl in range(4):
                slot = slotA if hl < 2 else slotB
                rhs = exp_sb.ap()[:, 1024 * slot + 512 * (hl % 2):
                                  1024 * slot + 512 * (hl % 2) + 512]
                i2 = E.matmul(pd.ap()[32 * hl:32 * hl + 1, :],
                              onesb[:, 0:1], rhs, start=False, stop=close,
                              tile_position=(0, 32 * hl), skip_group_check=True)
                if hl == 3:
                    i2.then_inc(s["cons"], 1)
        tq.append(f_avd)
        bump("cons")

    def emit_sg_end(j):
        qt, sg, kc = qt_sg_kc(j)
        wait(vq, "cons", j + 1)

        def f_e1(E, qt=qt, sg=sg):
            E.tensor_copy(out=y_sb[sg].ap()[:, QW * qt:QW * qt + QW],
                          in_=pav.ap()[:, :])
            for hl in range(4):
                E.tensor_copy(out=dsb[0:1, QW * hl:QW * hl + QW],
                              in_=pd.ap()[32 * hl:32 * hl + 1, :])
        vq.append(f_e1)
        dinc(vq, "sgv1")
        wait(vq, "sgt2", C["sgt2"] + 1)

        def f_e3(E, qt=qt, sg=sg):
            E.tensor_copy(out=bcsb[:, :], in_=pd.ap()[:, :])
        vq.append(f_e3)
        dinc(vq, "sgv2")

        def f_e4(E, qt=qt, sg=sg):
            E.reciprocal(out=bcsb[:, :], in_=bcsb[:, :])
            E.drain()
            E.tensor_tensor(out=y_sb[sg].ap()[:, QW * qt:QW * qt + QW],
                            in0=y_sb[sg].ap()[:, QW * qt:QW * qt + QW],
                            in1=bcsb[:, :], op=OP.mult).then_inc(s["sgy"], 1)
        vq.append(f_e4)
        bump("sgy")

    pend = 0
    for i in range(GKC):
        emit_unit(2 * i)
        emit_unit(2 * i + 1)
        emit_act(2 * i)
        emit_act(2 * i + 1)
        while pend < GKC and pend + (4 if pend % N_KC < 2 else 2) <= i:
            emit_avd(pend)
            if pend % N_KC == N_KC - 1:
                emit_sg_end(pend)
            pend += 1
    while pend < GKC:
        emit_avd(pend)
        if pend % N_KC == N_KC - 1:
            emit_sg_end(pend)
        pend += 1
    wait(tq, "sgv1", C["sgv1"])

    def f_bc_last(E):
        for hl in range(4):
            i2 = E.matmul(pd.ap()[32 * hl:32 * hl + 32, :],
                          grow[0:1, 128:160], dsb[0:1, QW * hl:QW * hl + QW],
                          start=True, stop=True, skip_group_check=True,
                          tile_position=(0, 32 * hl))
            if hl == 3:
                i2.then_inc(s["sgt2"], 1)
    tq.append(f_bc_last)
    bump("sgt2")
    CONS_ALL = C["cons"]
    SGY_ALL = 8

    cutpoint('P6')
    wait(gq, "cons", CONS_ALL - 48)
    for g in range(N_G):
        dma(gq, "woeload", woe_v[g][:, :], d_woe[128 * g:128 * (g + 1), :])
    WOELOAD = C["woeload"]
    wait(tq, "woeload", WOELOAD)
    wait(tq, "sgy", SGY_ALL)
    for qt in range(N_QT):
        for tl in range(4):
            ttg = 4 * qt + tl
            for cc in range(4):
                pb = cc % 2
                if ttg * 4 + cc >= 2:
                    wait(tq, "op_v", C["op_v"] - 1)

                def f_op(E, qt=qt, tl=tl, cc=cc, pb=pb):
                    for g in range(N_G):
                        i2 = E.matmul(pq2[pb][:, :],
                                      y_sb[g].ap()[:, QW * qt + 128 * tl: QW * qt + 128 * tl + 128],
                                      woe_v[g][:, 512 * cc:512 * (cc + 1)],
                                      start=(g == 0), stop=(g == 3))
                        if g == 3:
                            i2.then_inc(s["op_t"], 1)
                tq.append(f_op)
                bump("op_t")
                wait(vq, "op_t", C["op_t"])

                def f_opev(E, ttg=ttg, cc=cc, pb=pb):
                    E.tensor_copy(out=ln2n[ttg][:, 512 * cc:512 * (cc + 1)],
                                  in_=pq2[pb][:, :]).then_inc(s["op_v"], 1)
                vq.append(f_opev)
                bump("op_v")

    cutpoint('P7')
    xn4 = [xnst[0], xnst[1], xrest[0], xrest[1]]
    p8_dcnt = {}

    def p8_A(ttg):
        b = ttg % 4
        if ttg == 0:
            wait(qq, "op_v", 4)
        if ttg >= 4:
            wait(qq, "p8s", p8_dcnt[ttg - 4])
        dma(qq, "xre" + str(b), xn4[b][:, :], d_xs[128 * ttg:128 * (ttg + 1), :])
        wait(vq, "op_v", 4 * (ttg + 1))
        wait(vq, "xre" + str(b), C["xre" + str(b)])
        vq.append(lambda E, ttg=ttg, b=b: ln_stats(E, ln2n[ttg], b))
        dinc(vq, "p8v")

    def p8_B(ttg):
        b = ttg % 4
        emit_ln_s("p8v", "p8s", ln2n[ttg][:, :], ln2n[ttg][:, :], b)

    def p8_C(ttg):
        b = ttg % 4
        wait(vq, "p8s", C["p8s"])

        def f(E, ttg=ttg, b=b):
            E.tensor_tensor(out=xn4[b][:, :], in0=xn4[b][:, :],
                            in1=ln2n[ttg][:, :], op=OP.add)
            E.drain()
            ln_stats(E, xn4[b], b)
        vq.append(f)
        dinc(vq, "p8v")

    def p8_D(ttg):
        b = ttg % 4
        if ttg >= 2:
            wait(sq, "seq_t", C["seq_t"])
        emit_ln_s("p8v", "p8s", ln3stg[ttg % 2][:, :], xn4[b][:, :], b)
        p8_dcnt[ttg] = C["p8s"]

    def p8_EF(ttg):
        b = ttg % 2
        for half in range(2):
            wait(tq, "p8s", C["p8s"])
            if ttg or half:
                wait(tq, "tr3", C["tr3"])

            def f_tr3(E, b=b, half=half):
                for j in range(8):
                    dc = 8 * half + j
                    i2 = E.transpose(ptr_b8[:, 128 * j:128 * (j + 1)],
                                     ln3stg[b][:, 128 * dc:128 * (dc + 1)], identb[:, :])
                    if j == 7:
                        i2.then_inc(s["tr3"], 1)
            tq.append(f_tr3)
            bump("tr3")
            wait(vq, "tr3", C["tr3"])

            def f_tr3e(E, half=half, ttg=ttg):
                for j in range(8):
                    dc = 8 * half + j
                    i2 = E.tensor_copy(out=ln3T[dc][:, 128 * ttg:128 * (ttg + 1)],
                                       in_=ptr_b8[:, 128 * j:128 * (j + 1)])
                    if j == 7:
                        i2.then_inc(s["tr3"], 1)
            vq.append(f_tr3e)
            bump("tr3")
        wait(tq, "p8s", C["p8s"])

        def f_seq(E, ttg=ttg, b=b):
            for dcq in range(4):
                i2 = E.matmul(p6.ap()[0:1, 512 * dcq:512 * (dcq + 1)],
                              onesb[:, 0:1],
                              ln3stg[b][:, 512 * dcq:512 * (dcq + 1)],
                              start=(ttg == 0), stop=(ttg == 7),
                              skip_group_check=True)
                if dcq == 3:
                    i2.then_inc(s["seq_t"], 1)
        tq.append(f_seq)
        bump("seq_t")
        wait(vq, "seq_t", C["seq_t"])

    p8_A(0); p8_B(0); p8_A(1); p8_B(1)
    for t in range(8):
        p8_C(t); p8_D(t)
        if t + 2 < 8:
            p8_A(t + 2); p8_B(t + 2)
        if t >= 1:
            p8_EF(t - 1)
    p8_EF(7)
    wait(vq, "seq_t", C["seq_t"])
    vq.append(lambda E: E.tensor_copy(out=seqsb[0:1, :], in_=p6.ap()[0:1, 0:2048])
              .then_inc(s["seq_v"], 1))
    bump("seq_v")
    wait(qq, "seq_v", C["seq_v"])
    dma(qq, "seqdma", d_seqp.ap(), seqsb[0:1, :])
    wait(gq, "seqdma", C["seqdma"])
    gq.append(lambda E: E.collective_compute(
        "AllReduce", OP.add, replica_groups=[list(range(NCORES))],
        ins=[d_seqp.ap().opt()], outs=[d_seqs.ap().opt()]).then_inc(s["ar"], 1))
    bump("ar")
    wait(gq, "ar", C["ar"])
    dma(gq, "seqload", seqsb[0:1, :], d_seqs.ap())
    for dc in range(16):
        dma(gq, "simload", simsb[:, 32 * dc:32 * (dc + 1)],
            d_sim[128 * dc:128 * (dc + 1), :])
    SIMLOAD = C["simload"]
    wait(tq, "seqload", C["seqload"])
    wait(tq, "tr3", C["tr3"])

    def f_sct(E):
        for j in range(16):
            i2 = E.transpose(pg[0:128, 20 + j:21 + j],
                             seqsb[0:1, 128 * j:128 * (j + 1)], identf[0:1, 0:1],
                             )
            if j == 15:
                i2.then_inc(s["g_t"], 1)
    tq.append(f_sct)
    bump("g_t")
    wait(vq, "g_t", C["g_t"])
    vq.append(lambda E: E.tensor_copy(out=seqcol[:, :], in_=pg[0:128, 20:36])
              .then_inc(s["g_v"], 1))
    bump("g_v")

    cutpoint('P8')
    wait(vq, "simload", SIMLOAD)
    vq.append(lambda E: E.tensor_tensor(out=sim2s[:, :], in0=simsb[:, :],
                                        in1=simsb[:, :], op=OP.mult))
    dinc(vq, "g_v")
    wait(tq, "g_v", C["g_v"])
    wait(tq, "seq_v", C["seq_v"])

    def f_g2(E):
        for dc in range(16):
            E.matmul(pg[0:1, 0:32], onesf[:, 0:1], sim2s[:, 32 * dc:32 * (dc + 1)],
                     start=(dc == 0), stop=(dc == 15), skip_group_check=True)
        for dc in range(16):
            E.matmul(pg[0:1, 64:96], seqcol[:, dc:dc + 1], simsb[:, 32 * dc:32 * (dc + 1)],
                     start=(dc == 0), stop=(dc == 15), skip_group_check=True)
        for dc in range(16):
            i2 = E.matmul(pg[0:1, 96:97], seqcol[:, dc:dc + 1], seqcol[:, dc:dc + 1],
                          start=(dc == 0), stop=(dc == 15), skip_group_check=True)
            if dc == 15:
                i2.then_inc(s["g_t"], 1)
    tq.append(f_g2)
    bump("g_t")
    wait(sq, "g_t", C["g_t"])

    def f_g3(E):
        E.activation(out=grow[0:1, 0:32], in_=pg[0:1, 0:32], func=AF.Ln, scale=1.0)
        E.drain()
        E.activation(out=grow[0:1, 0:32], in_=grow[0:1, 0:32], func=AF.Exp, scale=-0.5)
        E.drain()
        E.activation(out=grow[0:1, 34:35], in_=pg[0:1, 96:97], func=AF.Ln, scale=1.0)
        E.drain()
        E.activation(out=grow[0:1, 34:35], in_=grow[0:1, 34:35], func=AF.Exp, scale=-0.5)
        E.drain()
        E.activation(out=grow[0:1, 64:96], in_=gatesb[0:1, :], func=AF.Exp, scale=-1.0)
        E.drain()
        E.activation(out=grow[0:1, 96:128], in_=pg[0:1, 64:96], func=AF.Copy, scale=1.0)
    sq.append(f_g3)
    dinc(sq, "g_s")
    wait(vq, "g_s", C["g_s"])

    def f_g4(E):
        E.tensor_scalar(out=grow[0:1, 64:96], in0=grow[0:1, 64:96],
                        scalar1=1.0, scalar2=None, op0=OP.add)
        E.drain()
        E.reciprocal(out=grow[0:1, 64:96], in_=grow[0:1, 64:96])
        E.drain()
        E.tensor_scalar(out=grow[0:1, 96:128], in0=grow[0:1, 96:128],
                        scalar1=grow[0:1, 34:35], scalar2=None, op0=OP.mult)
        E.drain()
        E.tensor_tensor(out=grow[0:1, 96:128], in0=grow[0:1, 96:128],
                        in1=grow[0:1, 0:32], op=OP.mult)
        E.drain()
        E.tensor_tensor(out=grow[0:1, 96:128], in0=grow[0:1, 96:128],
                        in1=grow[0:1, 64:96], op=OP.subtract)
    vq.append(f_g4)
    dinc(vq, "g_v")
    wait(tq, "g_v", C["g_v"])

    def f_g5(E):
        E.transpose(pg[0:EE, 37:38], grow[0:1, 96:128], identf[0:1, 0:1])
        E.matmul(pg[0:EE, 128:160], grow[0:1, 128:160], grow[0:1, 96:128],
                 start=True, stop=True, skip_group_check=True)
        E.matmul(pg[0:EE, 160:192], grow[0:1, 96:128], grow[0:1, 128:160],
                 start=True, stop=True, skip_group_check=True).then_inc(s["g_t"], 1)
    tq.append(f_g5)
    bump("g_t")
    wait(vq, "g_t", C["g_t"])

    def f_g6(E):
        E.tensor_copy(out=gsc[0:EE, 0:1], in_=pg[0:EE, 37:38])
        E.tensor_copy(out=gmx[0:EE, :], in_=pg[0:EE, 128:160])
        E.tensor_copy(out=gmx2[0:EE, :], in_=pg[0:EE, 160:192])
        E.drain()
        E.tensor_scalar(out=gsc[0:EE, 1:2], in0=gsc[0:EE, 0:1],
                        scalar1=0.0, scalar2=None, op0=OP.max)
        E.drain()
        E.tensor_scalar(out=gsc[0:EE, 2:3], in0=gsc[0:EE, 1:2],
                        scalar1=0.0, scalar2=None, op0=OP.is_gt)
        E.drain()
        E.tensor_tensor(out=gtmp[0:EE, :], in0=gmx[0:EE, :], in1=gmx2[0:EE, :],
                        op=OP.is_equal)
        E.drain()
        E.tensor_tensor(out=gtmp[0:EE, :], in0=gtmp[0:EE, :], in1=ltm[0:EE, :],
                        op=OP.mult)
        E.drain()
        E.tensor_tensor(out=gmx2[0:EE, :], in0=gmx[0:EE, :], in1=gmx2[0:EE, :],
                        op=OP.is_gt)
        E.drain()
        E.tensor_tensor(out=gmx2[0:EE, :], in0=gmx2[0:EE, :], in1=gtmp[0:EE, :],
                        op=OP.add)
        E.drain()
        E.tensor_reduce(out=gsc[0:EE, 3:4], in_=gmx2[0:EE, :], axis=AX, op=OP.add)
        E.drain()
        E.tensor_scalar(out=gsc[0:EE, 4:5], in0=gsc[0:EE, 3:4],
                        scalar1=16.0, scalar2=None, op0=OP.is_lt)
    vq.append(f_g6)
    dinc(vq, "g_v")
    wait(tq, "g_v", C["g_v"])
    tq.append(lambda E: E.transpose(pg[0:1, 38:70], gsc[0:EE, 2:3],
                                    identf[0:EE, 0:EE]).then_inc(s["g_t"], 1))
    bump("g_t")
    wait(vq, "g_t", C["g_t"])

    def f_g8(E):
        E.tensor_reduce(out=grow[0:1, 200:201], in_=pg[0:1, 38:70], axis=AX, op=OP.add)
        E.drain()
        E.tensor_scalar(out=grow[0:1, 200:201], in0=grow[0:1, 200:201],
                        scalar1=0.0, scalar2=None, op0=OP.is_equal)
    vq.append(f_g8)
    dinc(vq, "g_v")
    wait(tq, "g_v", C["g_v"])
    tq.append(lambda E: E.matmul(pg[0:EE, 36:37], grow[0:1, 128:160],
                                 grow[0:1, 200:201], start=True, stop=True,
                                 skip_group_check=True).then_inc(s["g_t"], 1))
    bump("g_t")
    wait(vq, "g_t", C["g_t"])

    def f_g10(E):
        E.tensor_copy(out=gsc[0:EE, 5:6], in_=pg[0:EE, 36:37])
        E.drain()
        E.tensor_tensor(out=gsc[0:EE, 6:7], in0=gsc[0:EE, 2:3], in1=gsc[0:EE, 4:5],
                        op=OP.max)
        E.drain()
        E.tensor_tensor(out=gsc[0:EE, 6:7], in0=gsc[0:EE, 6:7], in1=gsc[0:EE, 2:3],
                        op=OP.subtract)
        E.drain()
        E.tensor_tensor(out=gsc[0:EE, 6:7], in0=gsc[0:EE, 6:7], in1=gsc[0:EE, 5:6],
                        op=OP.mult)
        E.drain()
        E.tensor_tensor(out=gsc[0:EE, 6:7], in0=gsc[0:EE, 6:7], in1=gsc[0:EE, 2:3],
                        op=OP.add)
        E.drain()
        E.tensor_tensor(out=gsc[0:EE, 7:8], in0=gsc[0:EE, 1:2], in1=gsc[0:EE, 6:7],
                        op=OP.mult)
        E.drain()
        E.tensor_scalar(out=gsc[0:EE, 8:9], in0=gsc[0:EE, 6:7],
                        scalar1=1.0, scalar2=BIG, op0=OP.subtract, op1=OP.mult)
        E.drain()
        E.tensor_tensor(out=gsc[0:EE, 7:8], in0=gsc[0:EE, 7:8], in1=gsc[0:EE, 8:9],
                        op=OP.add)
    vq.append(f_g10)
    dinc(vq, "g_v")
    wait(tq, "g_v", C["g_v"])
    tq.append(lambda E: E.transpose(pg[0:1, 70:102], gsc[0:EE, 7:8],
                                    identf[0:EE, 0:EE]).then_inc(s["g_t"], 1))
    bump("g_t")
    wait(vq, "g_t", C["g_t"])

    def f_g12(E):
        E.tensor_copy(out=grow[0:1, 224:256], in_=pg[0:1, 70:102])
        E.drain()
        E.tensor_reduce(out=grow[0:1, 201:202], in_=grow[0:1, 224:256], axis=AX, op=OP.max)
        E.drain()
        E.tensor_scalar(out=grow[0:1, 202:203], in0=grow[0:1, 201:202],
                        scalar1=-1.0, scalar2=None, op0=OP.mult)
    vq.append(f_g12)
    dinc(vq, "g_v")
    wait(sq, "g_v", C["g_v"])
    sq.append(lambda E: E.activation(out=grow[0:1, 224:256], in_=grow[0:1, 224:256],
                                     func=AF.Exp, bias=grow[0:1, 202:203], scale=1.0))
    dinc(sq, "g_s")
    wait(vq, "g_s", C["g_s"])

    def f_g14(E):
        E.tensor_reduce(out=grow[0:1, 203:204], in_=grow[0:1, 224:256], axis=AX, op=OP.add)
        E.drain()
        E.reciprocal(out=grow[0:1, 203:204], in_=grow[0:1, 203:204])
        E.drain()
        E.tensor_scalar(out=grow[0:1, 224:256], in0=grow[0:1, 224:256],
                        scalar1=grow[0:1, 203:204], scalar2=None, op0=OP.mult)
    vq.append(f_g14)
    dinc(vq, "g_v")
    wait(tq, "g_v", C["g_v"])
    tq.append(lambda E: E.transpose(pg[0:EE, 102:103], grow[0:1, 224:256],
                                    identf[0:1, 0:1]).then_inc(s["g_t"], 1))
    bump("g_t")
    wait(vq, "g_t", C["g_t"])
    vq.append(lambda E: E.tensor_copy(out=routb[0:EE, :], in_=pg[0:EE, 102:103])
              .then_inc(s["g_v"], 1))
    bump("g_v")
    ROUT_DONE = C["g_v"]

    cutpoint('P9')
    for ch in range(8):
        if ch == 0:
            wait(qq, "seqdma", C["seqdma"])
            wait(qq, "op_v", 32)
        else:
            wait(qq, "ffw_v", C["ffw_v"])
        inap = AP(tensor=d_wvf.tensor, offset=ch * D, ap=[[8 * D, 32], [1, D]])
        dma(qq, "ffwload", wvstage[0:32, :], inap)
        wait(vq, "ffwload", C["ffwload"])
        if ch == 0:
            wait(vq, "seqdma", C["seqdma"])

        def f_cast(E, ch=ch):
            for j in range(4):
                E.tensor_copy(out=wvb[4 * ch + j][0:32, :],
                              in_=wvstage[0:32, 512 * j:512 * (j + 1)])
        vq.append(f_cast)
        dinc(vq, "ffw_v")
    WV_CAST = C["ffw_v"]
    wait(tq, "ffw_v", WV_CAST)
    wait(tq, "g_v", ROUT_DONE)

    def f_fvm(E):
        for hh in range(8):
            for cc in range(16):
                fi = (2048 * hh + 128 * cc) // 512
                co = (2048 * hh + 128 * cc) % 512
                i2 = E.matmul(pg[0:128, 128 + 16 * hh + cc:129 + 16 * hh + cc],
                              wvb[fi][0:32, co:co + 128], routb[0:EE, :],
                              start=True, stop=True, skip_group_check=True)
                if hh == 7 and cc == 15:
                    i2.then_inc(s["ffw_t"], 1)
    tq.append(f_fvm)
    bump("ffw_t")
    wait(vq, "ffw_t", C["ffw_t"])
    vq.append(lambda E: E.tensor_copy(out=ffst[:, 0:128], in_=pg[0:128, 128:256])
              .then_inc(s["ffw_e"], 1))
    bump("ffw_e")
    wait(tq, "ffw_e", C["ffw_e"])
    tq.append(lambda E: E.transpose(ptr_b8[:, 0:128], ffst[:, 0:128], identb[:, :])
              .then_inc(s["fftr_t"], 1))
    bump("fftr_t")
    wait(vq, "fftr_t", C["fftr_t"])
    vq.append(lambda E: E.tensor_copy(out=ffst[:, 128:256], in_=ptr_b8[:, 0:128])
              .then_inc(s["ffw_e"], 1))
    bump("ffw_e")
    wait(qq, "ffw_e", C["ffw_e"])
    fvout = AP(tensor=d_ffnp.ap().tensor, offset=0, ap=[[128, 128], [1, 128]])
    dma(qq, "ffpdma", fvout, ffst[:, 128:256])
    for ch in range(8):
        wait(qq, "ffw_v", C["ffw_v"])
        if ch == 0:
            wait(qq, "ffw_t", C["ffw_t"])
        inap = AP(tensor=d_owf.tensor, offset=ch * D, ap=[[8 * D, 32], [1, D]])
        dma(qq, "ffwload", wvstage[0:32, :], inap)
        wait(vq, "ffwload", C["ffwload"])
        if ch == 0:
            wait(vq, "ffw_t", C["ffw_t"])

        def f_cast2(E, ch=ch):
            for j in range(4):
                E.tensor_copy(out=wvb[4 * ch + j][0:32, :],
                              in_=wvstage[0:32, 512 * j:512 * (j + 1)])
        vq.append(f_cast2)
        dinc(vq, "ffw_v")
    wait(tq, "ffw_v", C["ffw_v"])
    wait(tq, "ffw_e", C["ffw_e"])

    def f_fom(E):
        for hh in range(8):
            for cc in range(16):
                fi = (2048 * hh + 128 * cc) // 512
                co = (2048 * hh + 128 * cc) % 512
                i2 = E.matmul(pg[0:128, 256 + 16 * hh + cc:257 + 16 * hh + cc],
                              wvb[fi][0:32, co:co + 128], routb[0:EE, :],
                              start=True, stop=True, skip_group_check=True)
                if hh == 7 and cc == 15:
                    i2.then_inc(s["ffw_t"], 1)
    tq.append(f_fom)
    bump("ffw_t")
    wait(vq, "ffw_t", C["ffw_t"])
    vq.append(lambda E: E.tensor_copy(out=ffst[:, 256:384], in_=pg[0:128, 256:384])
              .then_inc(s["ffw_e"], 1))
    bump("ffw_e")
    wait(tq, "ffw_e", C["ffw_e"])
    tq.append(lambda E: E.transpose(ptr_b8[:, 0:128], ffst[:, 256:384], identb[:, :])
              .then_inc(s["fftr_t"], 1))
    bump("fftr_t")
    wait(vq, "fftr_t", C["fftr_t"])
    vq.append(lambda E: E.tensor_copy(out=ffst[:, 384:512], in_=ptr_b8[:, 0:128])
              .then_inc(s["ffw_e"], 1))
    bump("ffw_e")
    wait(qq, "ffw_e", C["ffw_e"])
    foout = AP(tensor=d_ffnp.ap().tensor, offset=8 * D, ap=[[128, 128], [1, 128]])
    dma(qq, "ffpdma", foout, ffst[:, 384:512])

    wait(gq, "ffpdma", C["ffpdma"])
    gq.append(lambda E: E.collective_compute(
        "AllGather", OP.bypass, replica_groups=[list(range(NCORES))],
        ins=[d_ffnp.ap().opt()], outs=[d_ffng.ap().opt()]).then_inc(s["ag3"], 1))
    bump("ag3")
    wait(gq, "ag3", C["ag3"])
    for r in range(NCORES):
        inap = AP(tensor=d_ffng.ap().tensor, offset=r * 16 * D, ap=[[D, 8], [1, D]])
        dma(gq, "ffgload", wvgsb[8 * r:8 * r + 8, :], inap)
        inap2 = AP(tensor=d_ffng.ap().tensor, offset=r * 16 * D + 8 * D,
                   ap=[[D, 8], [1, D]])
        dma(gq, "ffgload", wodsb[8 * r:8 * r + 8, :], inap2)
    FFG = C["ffgload"]
    wait(tq, "ffgload", FFG)
    for dc in range(16):
        if dc:
            wait(tq, "fftr_v", C["fftr_v"])
        tq.append(lambda E, dc=dc: E.transpose(ptr_b8[:, 0:64],
                                               wvgsb[0:64, 128 * dc:128 * (dc + 1)],
                                               identb[0:64, 0:64]).then_inc(s["fftr_t"], 1))
        bump("fftr_t")
        wait(vq, "fftr_t", C["fftr_t"])
        vq.append(lambda E, dc=dc: E.tensor_copy(out=wveffT[dc][:, :],
                                                 in_=ptr_b8[:, 0:64]).then_inc(s["fftr_v"], 1))
        bump("fftr_v")

    cutpoint('P10')
    for qt in range(N_QT):
        if qt == 0:
            wait(tq, "fftr_v", C["fftr_v"])
        else:
            wait(tq, "ffc_v", C["ffc_v"])

        def f_cmm(E, qt=qt):
            for dc in range(16):
                i2 = E.matmul(pq2[0][0:64, :], wveffT[dc][:, :],
                              ln3T[dc][:, QW * qt:QW * qt + QW],
                              start=(dc == 0), stop=(dc == 15))
                if dc == 15:
                    i2.then_inc(s["ffc_t"], 1)
        tq.append(f_cmm)
        bump("ffc_t")
        wait(vq, "ffc_t", C["ffc_t"])
        vq.append(lambda E, qt=qt: E.tensor_copy(out=combT[0:64, QW * qt:QW * qt + QW],
                                                 in_=pq2[0][0:64, :]).then_inc(s["ffc_v"], 1))
        bump("ffc_v")
    wait(tq, "ffc_v", C["ffc_v"])
    wait(tq, "fftr_v", C["fftr_v"])
    ffo_banks = [[p6.ap()[:, 512 * dcq:512 * (dcq + 1)] for dcq in range(4)],
                 [p6.ap()[:, 2048:2560], p6.ap()[:, 2560:3072],
                  pav.ap()[:, :], pd.ap()[:, :]]]

    def ffo_evac_ap(sel):
        if sel == 0:
            return [p6.ap()[0:128, 0:2048]]
        return [p6.ap()[0:128, 2048:3072], pav.ap()[:, :], pd.ap()[:, :]]

    for ttg in range(8):
        sel = ttg % 2
        if ttg >= 2:
            wait(tq, "ffo_v", C["ffo_v"] - 1)

        def f_fmm(E, ttg=ttg, sel=sel):
            for dcq in range(4):
                i2 = E.matmul(ffo_banks[sel][dcq],
                              combT[0:64, 128 * ttg:128 * (ttg + 1)],
                              wodsb[0:64, 512 * dcq:512 * (dcq + 1)],
                              start=True, stop=True, skip_group_check=True)
                if dcq == 3:
                    i2.then_inc(s["ffo_t"], 1)
        tq.append(f_fmm)
        bump("ffo_t")
        wait(vq, "ffo_t", C["ffo_t"])

        def f_fev(E, ttg=ttg, sel=sel):
            if sel == 0:
                E.tensor_copy(out=ffn_t[ttg][:, :], in_=p6.ap()[0:128, 0:2048])
                E.drain().then_inc(s["ffo_v"], 1)
            else:
                E.tensor_copy(out=ffn_t[ttg][:, 0:1024], in_=p6.ap()[0:128, 2048:3072])
                E.tensor_copy(out=ffn_t[ttg][:, 1024:1536], in_=pav.ap()[:, :])
                E.tensor_copy(out=ffn_t[ttg][:, 1536:2048], in_=pd.ap()[:, :])
                E.drain().then_inc(s["ffo_v"], 1)
        vq.append(f_fev)
        bump("ffo_v")
    FFO_ALL = C["ffo_v"]

    p11_ccnt = {}

    def p11_A(ttg):
        b = ttg % 4
        if ttg == 0:
            wait(qq, "ffw_t", C["ffw_t"])
            wait(qq, "ffw_v", C["ffw_v"])
            wait(qq, "fftr_v", C["fftr_v"])
            wait(qq, "p8s", C["p8s"])
        if ttg >= 4:
            wait(qq, "p11v", p11_ccnt[ttg - 4])
        dma(qq, "xre" + str(b), xn4[b][:, :], d_xs[128 * ttg:128 * (ttg + 1), :])
        wait(vq, "ffo_v", min(FFO_ALL, ttg + 1))
        wait(vq, "xre" + str(b), C["xre" + str(b)])
        vq.append(lambda E, ttg=ttg, b=b: ln_stats(E, ffn_t[ttg], b))
        dinc(vq, "p11v")

    def p11_B(ttg):
        b = ttg % 4
        emit_ln_s("p11v", "p11s", ffn_t[ttg][:, :], ffn_t[ttg][:, :], b)

    def p11_C(ttg):
        b = ttg % 4
        wait(vq, "p11s", C["p11s"])

        def f(E, ttg=ttg, b=b):
            E.tensor_tensor(out=ffn_t[ttg][:, :], in0=ffn_t[ttg][:, :],
                            in1=xn4[b][:, :], op=OP.add)
            E.drain()
            E.tensor_tensor(out=ffn_t[ttg][:, :], in0=ffn_t[ttg][:, :],
                            in1=ln2n[ttg][:, :], op=OP.add)
        vq.append(f)
        dinc(vq, "p11v")
        p11_ccnt[ttg] = C["p11v"]
        wait(qq, "p11v", C["p11v"])
        dma(qq, "store", d_out[128 * ttg:128 * (ttg + 1), :], ffn_t[ttg][:, :])

    p11_A(0); p11_B(0); p11_A(1); p11_B(1)
    for t in range(8):
        p11_C(t)
        if t + 2 < 8:
            p11_A(t + 2); p11_B(t + 2)
    wait(gq, "store", C["store"])

    with nc.Block() as blk:
        @blk.tensor
        def _(E):
            for f in tq:
                f(E)

        @blk.vector
        def _(E):
            for f in vq:
                f(E)

        @blk.scalar
        def _(E):
            for f in sq:
                f(E)

        @blk.sync
        def _(E):
            for f in qq:
                f(E)

        @blk.gpsimd
        def _(E):
            for f in gq:
                f(E)

    stack.close()
    return nc


def shard_inputs(inputs):
    bf = ml_dtypes.bfloat16
    X = np.ascontiguousarray(np.asarray(inputs["L0_attn_x"], np.float32).reshape(T, D))
    Wq = np.asarray(inputs["L0_attn_Wq"], np.float32)
    Wk = np.asarray(inputs["L0_attn_Wk"], np.float32)
    Wv = np.asarray(inputs["L0_attn_Wv"], np.float32)
    Wo = np.asarray(inputs["L0_attn_Wo"], np.float32)
    Wdn = np.ascontiguousarray(np.asarray(inputs["L0_attn_Wdown"], np.float32))
    Wup = np.ascontiguousarray(np.asarray(inputs["L0_attn_Wup"], np.float32))
    sim = np.ascontiguousarray(np.asarray(inputs["L0_ffn_sim_matrix"], np.float32))
    gates = np.ascontiguousarray(np.asarray(inputs["L0_ffn_gates"], np.float32).reshape(1, EE))
    w_v = np.asarray(inputs["L0_ffn_w_v"], np.float32).reshape(EE, 64, D)
    o_w = np.asarray(inputs["L0_ffn_o_weights"], np.float32).reshape(EE, 64, D)

    ident = np.eye(128, dtype=np.float32)
    ltmask = np.tril(np.ones((EE, EE), np.float32), k=-1)
    expander = np.zeros((4, 128), np.float32)
    for hl in range(4):
        expander[hl, 32 * hl:32 * hl + 32] = 1.0
    bdg = np.zeros((128, 4), np.float32)
    for hl in range(4):
        bdg[32 * hl:32 * hl + 32, hl] = 1.0

    maps = []
    for c in range(NCORES):
        maps.append({
            "xs": np.ascontiguousarray(X[TC * c:TC * (c + 1)]),
            "wq_r": Wq, "wk_r": Wk, "wv_r": Wv, "wo_c": Wo,
            "wdown": Wdn, "wup": Wup, "sim": sim, "gates": gates,
            "wvf": np.ascontiguousarray(w_v[:, 8 * c:8 * (c + 1), :].reshape(EE, 8 * D)),
            "owf": np.ascontiguousarray(o_w[:, 8 * c:8 * (c + 1), :].reshape(EE, 8 * D)),
            "identb": ident.astype(bf), "identf": ident,
            "ltmask": ltmask, "expander": expander,
            "bdiag": bdg.astype(bf),
        })
    return maps


def assemble(results):
    return np.concatenate([r["out"] for r in results], axis=0).reshape(1, T, D)



def kernel(**inputs):
    from concourse.bass_utils import run_bass_kernel_spmd
    nc = build(debug=False)
    maps = shard_inputs(inputs)
    res = run_bass_kernel_spmd(nc, maps, core_ids=list(range(NCORES)))
    return assemble(res.results)


# version 15
# speedup vs baseline: 1.0689x; 1.0689x over previous
import os
import numpy as np
import ml_dtypes
import concourse.bass as bass
import concourse.mybir as mybir
from concourse.bass_types import AP

F32 = mybir.dt.float32
BF16 = mybir.dt.bfloat16
AF = mybir.ActivationFunctionType
OP = mybir.AluOpType
AX = mybir.AxisListType.X

NCORES = 8
TC = 1024
T = 8192
D = 2048
NH = 16
RK = 32
HR = NH * RK
HPC = 2
EE = 32
SCALE = 1.0 / float(np.sqrt(RK))
EPS = 1e-5
BIG = float(np.finfo(np.float32).max)
MSAFE = 1.03

N_G = 4
N_QT = 2
QW = 512
N_KC = 64
GKC = N_QT * N_G * N_KC
PREP_SZ = 3 * D * 64 + 64 * D


def build(debug=False):
    nc = bass.Bass("TRN2", target_bir_lowering=False, debug=debug)

    din = lambda name, shape, dt: nc.dram_tensor(name, shape, dt, kind="ExternalInput").ap()
    d_xs = din("xs", [TC, D], F32)
    d_wq = din("wq_r", [HPC * 128, D], F32)
    d_wk = din("wk_r", [HPC * 128, D], F32)
    d_wv = din("wv_r", [HPC * 128, D], F32)
    d_wo = din("wo_c", [D, HPC * 128], F32)
    d_wdn = din("wdown", [128, RK], F32)
    d_wup = din("wup", [RK, 128], F32)
    d_sim = din("sim", [D, EE], F32)
    d_gates = din("gates", [1, EE], F32)
    d_wvf = din("wvf", [EE, 8 * D], F32)
    d_owf = din("owf", [EE, 8 * D], F32)
    d_idb = din("identb", [128, 128], BF16)
    d_idf = din("identf", [128, 128], F32)
    d_lt = din("ltmask", [EE, EE], F32)
    d_exp4 = din("expander", [4, 128], F32)
    d_bdg = din("bdiag", [128, 4], BF16)
    d_out = nc.dram_tensor("out", [TC, D], F32, kind="ExternalOutput").ap()

    d_prep_sh = nc.dram_tensor("prep_sh", [PREP_SZ], BF16)
    d_prep_g = nc.dram_tensor("prep_g", [NCORES * PREP_SZ], BF16)
    d_ksh = nc.dram_tensor("ksh", [HR * TC + 16], BF16)
    d_kg = nc.dram_tensor("kg", [NCORES * (HR * TC + 16)], BF16)
    d_vsh = nc.dram_tensor("vsh", [TC, HR], BF16)
    d_vg = nc.dram_tensor("vg", [NCORES * TC, HR], BF16)
    d_seqp = nc.dram_tensor("seqp", [1, D], F32)
    d_seqs = nc.dram_tensor("seqs", [1, D], F32)
    d_ffnp = nc.dram_tensor("ffnp", [16, D], BF16)
    d_ffng = nc.dram_tensor("ffng", [NCORES * 16, D], BF16)

    from contextlib import ExitStack
    stack = ExitStack()
    SB = lambda name, shape, dt: stack.enter_context(nc.sbuf_tensor(name, shape, dt))
    PST = lambda name, shape, dt: stack.enter_context(nc.psum_tensor(name, shape, dt))
    SEM = lambda name: stack.enter_context(nc.semaphore(name))

    k_sb = [SB(f"k{g}", [128, T], BF16) for g in range(N_G)]
    v_sb = SB("v", [128, N_KC * HR], BF16)
    q_sb = [SB(f"q{g}", [128, TC], BF16) for g in range(N_G)]
    y_sb = [SB(f"y{g}", [128, TC], BF16) for g in range(N_G)]
    exp_sb = SB("exps", [128, 8 * 1024], BF16)
    identb = SB("identb_s", [128, 128], BF16)
    identf = SB("identf_s", [128, 128], F32)
    onesb = SB("onesb", [128, 1], BF16)
    onesf = SB("onesf", [128, 1], F32)
    zerob = SB("zerob", [128, 128], BF16)
    epsb = SB("epsb", [128, 1], F32)
    negMb = SB("negMb", [128, 1], F32)
    bdiag = SB("bdiag_s", [128, 4], BF16)
    exp4 = SB("exp4_s", [4, 128], F32)
    ltm = SB("ltm_s", [EE, EE], F32)
    wdnf = SB("wdnf", [128, RK], F32)
    wupf = SB("wupf", [RK, 128], F32)
    nrm = SB("nrm", [NH, 4], F32)
    nrmb = SB("nrmb", [NH, 1], BF16)
    kn8 = SB("kn8", [NH, NCORES], BF16)
    bn6 = SB("bn6", [128, 24], F32)
    mv2 = SB("mv2", [128, 8], F32)
    rstd = SB("rstd", [128, 4], F32)
    negm = SB("negm", [128, 4], F32)
    dsb = SB("dsb", [1, 4 * QW], F32)
    nrm4 = SB("nrm4", [4, 2], F32)
    bcsb = SB("bcsb", [128, QW], F32)
    gsc = SB("gsc", [EE, 16], F32)
    gmx = SB("gmx", [EE, EE], F32)
    gmx2 = SB("gmx2", [EE, EE], F32)
    gtmp = SB("gtmp", [EE, EE], F32)
    grow = SB("grow", [1, 256], F32)
    routb = SB("routb", [EE, 1], BF16)
    simsb = SB("simsb", [128, 512], F32)
    sim2s = SB("sim2s", [128, 512], F32)
    seqcol = SB("seqcol", [128, 16], F32)
    gatesb = SB("gatesb", [1, EE], F32)
    ffst = SB("ffst", [128, 512], BF16)

    def view(t, dt, byte_off, part, cols):
        a = t.ap().bitcast(dt)
        esz = 4 if dt == F32 else 2
        assert byte_off % esz == 0
        return a[0:part, byte_off // esz: byte_off // esz + cols]

    ln1T = ([view(k_sb[0], BF16, 2048 * i, 128, TC) for i in range(8)]
            + [view(k_sb[1], BF16, 2048 * i, 128, TC) for i in range(8)])
    A_q = [view(k_sb[2], BF16, 1024 * i, 128, 512) for i in range(16)]
    A_k = [view(k_sb[3], BF16, 1024 * i, 128, 512) for i in range(16)]
    A_v = [view(v_sb, BF16, 1024 * i, 128, 512) for i in range(16)]
    xst = [view(v_sb, F32, 16384 + 8192 * i, 128, D) for i in range(2)]
    ln1st = [view(v_sb, BF16, 32768 + 4096 * i, 128, D) for i in range(2)]
    wqst = [view(v_sb, F32, 40960 + 8192 * i, 128, D) for i in range(2)]
    wost = view(v_sb, F32, 57344, 128, D)
    wotb = [view(exp_sb, BF16, 4096 * i, 128, D) for i in range(2)]
    wupt = view(exp_sb, BF16, 8192, 128, RK)
    prep_st = view(exp_sb, BF16, 8320, 128, 1024)
    kst = prep_st
    sqst = view(exp_sb, BF16, 10368, 128, 512)
    vstg = view(y_sb[2], BF16, 0, 128, 512)
    woest_h = [view(y_sb[i], BF16, 0, 64, 1024) for i in range(2)]

    ln2n = [view(k_sb[tt // 4], BF16, 4096 * (tt % 4), 128, D) for tt in range(8)]
    ln3T = ([view(k_sb[2], BF16, 2048 * i, 128, TC) for i in range(8)]
            + [view(k_sb[3], BF16, 2048 * i, 128, TC) for i in range(8)])
    woe_v = [view(v_sb, BF16, 4096 * g, 128, D) for g in range(N_G)]
    xnst = [view(v_sb, F32, 16384 + 8192 * i, 128, D) for i in range(2)]
    ln3stg = [view(v_sb, BF16, 32768 + 4096 * i, 128, D) for i in range(2)]
    wvstage = view(v_sb, F32, 57344, 32, D)
    seqsb = view(exp_sb, F32, 0, 1, D)
    wvb = ([view(exp_sb, BF16, 1024 * i, 32, 512) for i in range(16)]
           + [view(v_sb, BF16, 40960 + 1024 * i, 32, 512) for i in range(16)])
    assert len(wvb) == 32
    wvgsb = view(v_sb, BF16, 32768, 64, D)
    wodsb = view(v_sb, BF16, 36864, 64, D)
    wveffT = [view(exp_sb, BF16, 128 * i, 128, 64) for i in range(16)]
    combT = view(exp_sb, BF16, 2048, 64, TC)
    xrest = [view(exp_sb, F32, 4096, 128, D), view(v_sb, F32, 57344, 128, D)]
    ffn_t = [view(v_sb, F32, 0, 128, D), view(v_sb, F32, 8192, 128, D),
             view(v_sb, F32, 40960, 128, D), view(v_sb, F32, 49152, 128, D),
             view(k_sb[2], F32, 0, 128, D), view(k_sb[2], F32, 8192, 128, D),
             view(k_sb[3], F32, 0, 128, D), view(k_sb[3], F32, 8192, 128, D)]

    p6 = PST("p6", [128, 3072], F32)
    pav = PST("pav", [128, 512], F32)
    pd = PST("pd", [128, 512], F32)

    def psview(ps, dt, byte_off, part, cols):
        a = ps.ap().bitcast(dt)
        esz = 4 if dt == F32 else 2
        return a[0:part, byte_off // esz: byte_off // esz + cols]

    pA = p6.ap()[:, 0:512]
    pAB = p6.ap()[:, 512:1024]
    pqn = p6.ap()[0:4, 1536:2048]
    ptr_b1 = psview(p6, BF16, 8192, 128, 1024)
    pq2 = [p6.ap()[:, 2048 + 512 * i: 2048 + 512 * (i + 1)] for i in range(2)]
    ptr_b8 = psview(pav, BF16, 0, 128, 1024)
    pg = pd.ap()

    s, C = {}, {}
    for n in ["cload", "xload0", "xload1", "xre0", "xre1", "xre2", "xre3", "lnv", "lns", "tr1", "wload", "prep_t", "prep_v",
              "prep_dma", "ag1", "aload", "qkv_t", "qkv_v", "shdma", "ag2",
              "kvload", "kvn", "mh_t", "mh_v", "mh_s", "tS", "act", "cons",
              "sgv1", "sgt2", "sgv2", "sgy", "woeload", "op_t", "op_v",
              "p8v", "p8s", "tr3", "seq_t", "seq_v", "seqdma", "ar",
              "seqload", "g_t", "g_v", "g_s", "simload", "ffwload",
              "ffw_v", "ffw_t", "ffw_e", "ffpdma", "ag3", "ffgload",
              "fftr_t", "fftr_v", "ffc_t", "ffc_v", "ffo_t", "ffo_v",
              "p11v", "p11s", "store"]:
        s[n] = SEM("s_" + n)
        C[n] = 0

    _CUT = [False]

    class _Q(list):
        def append(self, x):
            if not _CUT[0]:
                super().append(x)

    ops = {e: _Q() for e in "tvsqg"}
    tq, vq, sq, qq, gq = (ops[e] for e in "tvsqg")
    MK_CUT = os.environ.get("MK_CUT", "")

    def cutpoint(name):
        if MK_CUT == name and not _CUT[0]:
            for ttg in range(8):
                bump("store", 16)
                qq.append(lambda E, ttg=ttg: E.dma_start(
                    out=d_out[128 * ttg:128 * (ttg + 1), 0:128],
                    in_=identf[:, :]).then_inc(s["store"], 16))
            v = C["store"]
            gq.append(lambda E, v=v: E.wait_ge(s["store"], v))
            _CUT[0] = True

    def bump(name, amt=1):
        C[name] += amt
        return C[name]

    def dma(eq, sem_name, out, in_, **kw):
        bump(sem_name, 16)
        eq.append(lambda E, out=out, in_=in_, kw=kw, sn=sem_name:
                  E.dma_start(out=out, in_=in_, **kw).then_inc(s[sn], 16))

    def wait(eq, sem_name, val=None):
        v = C[sem_name] if val is None else val
        eq.append(lambda E, sn=sem_name, v=v: E.wait_ge(s[sn], v))

    def dinc(eq, sem_name):
        bump(sem_name)
        eq.append(lambda E, sn=sem_name: E.drain().then_inc(s[sn], 1))

    dma(qq, "cload", identb[:, :], d_idb)
    dma(qq, "cload", identf[:, :], d_idf)
    dma(qq, "cload", ltm[:, :], d_lt)
    dma(qq, "cload", exp4[:, :], d_exp4)
    dma(qq, "cload", bdiag[:, :], d_bdg)
    dma(qq, "cload", gatesb[:, :], d_gates)
    dma(qq, "cload", wdnf[:, :], d_wdn)
    dma(qq, "cload", wupf[:, :], d_wup)

    def f_memset(E):
        E.memset(onesb[:, :], 1.0)
        E.memset(onesf[:, :], 1.0)
        E.memset(zerob[:, :], 0.0)
        E.memset(epsb[:, :], EPS)
        E.memset(grow[:, :], 1.0)
        E.memset(nrm[:, :], 0.0)
        E.memset(nrmb[:, :], 0.0)
    vq.append(f_memset)
    dinc(vq, "cload")
    CLOAD = C["cload"]

    def ln_stats(E, src, b):
        for ii in range(4):
            E.bn_stats(out=bn6[:, 6 * ii:6 * ii + 6], in_=src[:, 512 * ii:512 * (ii + 1)])
        E.drain()
        E.bn_aggr(out=mv2[:, 2 * b:2 * b + 2], in_=bn6[:, :])
        E.drain()
        E.tensor_scalar(out=negm[:, b:b + 1], in0=mv2[:, 2 * b:2 * b + 1],
                        scalar1=-1.0, scalar2=None, op0=OP.mult)

    def emit_ln_s(sem_wait_v, sem_s, dst, src_, b):
        wait(sq, sem_wait_v, C[sem_wait_v])

        def f1(E, dst=dst, src_=src_, b=b):
            E.activation(out=rstd[:, b:b + 1], in_=mv2[:, 2 * b + 1:2 * b + 2],
                         func=AF.Ln, bias=epsb[:, 0:1], scale=1.0)
            E.drain()
            E.activation(out=rstd[:, b:b + 1], in_=rstd[:, b:b + 1],
                         func=AF.Exp, scale=-0.5)
            E.drain()
            E.activation(out=dst, in_=src_, func=AF.Identity,
                         bias=negm[:, b:b + 1], scale=1.0)
            E.drain()
            E.activation(out=dst, in_=dst, func=AF.Identity,
                         scale=rstd[:, b:b + 1])
        sq.append(f1)
        dinc(sq, sem_s)

    for tt in range(8):
        b = tt % 2
        if tt >= 2:
            wait(qq, "tr1", C["tr1"])
        dma(qq, "xload" + str(b), xst[b][:, :], d_xs[128 * tt:128 * (tt + 1), :])
        wait(vq, "xload" + str(b), C["xload" + str(b)])
        if tt == 0:
            wait(vq, "cload", CLOAD)
        vq.append(lambda E, b=b: ln_stats(E, xst[b], b))
        dinc(vq, "lnv")
        emit_ln_s("lnv", "lns", ln1st[b][:, :], xst[b][:, :], b)
        for half in range(2):
            wait(tq, "lns", C["lns"])
            if tt or half:
                wait(tq, "tr1", C["tr1"])

            def f_tr(E, b=b, half=half):
                for j in range(8):
                    dc = 8 * half + j
                    i2 = E.transpose(ptr_b1[:, 128 * j:128 * (j + 1)],
                                     ln1st[b][:, 128 * dc:128 * (dc + 1)], identb[:, :])
                    if j == 7:
                        i2.then_inc(s["tr1"], 1)
            tq.append(f_tr)
            bump("tr1")
            wait(vq, "tr1", C["tr1"])

            def f_ev(E, half=half, tt=tt):
                for j in range(8):
                    dc = 8 * half + j
                    i2 = E.tensor_copy(out=ln1T[dc][:, 128 * tt:128 * (tt + 1)],
                                       in_=ptr_b1[:, 128 * j:128 * (j + 1)])
                    if j == 7:
                        i2.then_inc(s["tr1"], 1)
            vq.append(f_ev)
            bump("tr1")
    TR1_ALL = C["tr1"]

    cutpoint('P1')
    for wi, dsrc in enumerate([d_wq, d_wk, d_wv]):
        for hl in range(HPC):
            if wi > 0 or hl > 0:
                wait(qq, "prep_t", C["prep_t"])
            dma(qq, "wload", wqst[hl][:, :], dsrc[128 * hl:128 * (hl + 1), :])
        wait(vq, "wload", C["wload"])
        if wi == 0:
            wait(vq, "cload", CLOAD)

        def f_wcast(E, wi=wi):
            for hl in range(HPC):
                E.tensor_copy(out=wotb[hl][:, :], in_=wqst[hl][:, :])
            if wi == 0:
                E.tensor_copy(out=wupt[:, 0:RK], in_=wdnf[:, :])
        vq.append(f_wcast)
        dinc(vq, "prep_v")
        wait(tq, "prep_v", C["prep_v"])
        if wi == 0:
            wait(tq, "cload", CLOAD)
        for dc in range(16):
            def f_amm(E, dc=dc):
                for hl in range(HPC):
                    i2 = E.matmul(pA[:, 64 * (dc % 8) + 32 * hl: 64 * (dc % 8) + 32 * hl + 32],
                                  wotb[hl][:, 128 * dc:128 * (dc + 1)], wupt[:, 0:RK],
                                  start=True, stop=True, skip_group_check=True)
                    if dc % 8 == 7 and hl == HPC - 1:
                        i2.then_inc(s["prep_t"], 1)
            tq.append(f_amm)
            if dc % 8 == 7:
                bump("prep_t")
                wait(vq, "prep_t", C["prep_t"])
                if dc == 7:
                    wait(vq, "prep_dma", C["prep_dma"])

                def f_aev(E, dc=dc):
                    E.tensor_copy(
                        out=prep_st[:, 512 * ((dc // 8) % 2):512 * ((dc // 8) % 2) + 512],
                        in_=pA[:, 0:512]).then_inc(s["prep_v"], 1)
                vq.append(f_aev)
                bump("prep_v")
                wait(tq, "prep_v", C["prep_v"])
        wait(qq, "prep_v", C["prep_v"])
        for half in range(2):
            base = wi * 131072 + half * 8 * 128 * 64
            outap = AP(tensor=d_prep_sh.ap().tensor, offset=base,
                       ap=[[64, 128], [128 * 64, 8], [1, 64]])
            inap = AP(tensor=prep_st.tensor, offset=prep_st.offset + 512 * half,
                      ap=[[prep_st.ap[0][0], 128], [64, 8], [1, 64]])
            dma(qq, "prep_dma", outap, inap)

    wait(tq, "cload", CLOAD)
    tq.append(lambda E: E.transpose(pA[0:128, 0:32], wupf[:, :], identf[0:RK, 0:RK])
              .then_inc(s["prep_t"], 1))
    bump("prep_t")
    wait(vq, "prep_t", C["prep_t"])
    vq.append(lambda E: E.tensor_copy(out=wupt[:, :], in_=pA[0:128, 0:32])
              .then_inc(s["prep_v"], 1))
    bump("prep_v")
    wait(tq, "prep_v", C["prep_v"])

    for hl in range(HPC):
        wo_r = AP(tensor=d_wo.tensor, offset=128 * hl,
                  ap=[[256, 128], [128 * 256, 16], [1, 128]])
        if hl > 0:
            wait(qq, "prep_t", C["prep_t"])
        dma(qq, "wload", wost[:, :], wo_r)
        wait(tq, "wload", C["wload"])
        for quad in range(4):
            def f_wotr(E, quad=quad):
                for j in range(4):
                    dc = 4 * quad + j
                    i2 = E.transpose(pAB[:, 128 * j:128 * (j + 1)],
                                     wost[:, 128 * dc:128 * (dc + 1)], identf[:, :])
                    if j == 3:
                        i2.then_inc(s["prep_t"], 1)
            tq.append(f_wotr)
            bump("prep_t")
            wait(vq, "prep_t", C["prep_t"])

            def f_woev(E, quad=quad, hl=hl):
                E.tensor_copy(out=wotb[hl][:, 512 * quad:512 * (quad + 1)],
                              in_=pAB[:, :]).then_inc(s["prep_v"], 1)
            vq.append(f_woev)
            bump("prep_v")
            wait(tq, "prep_v", C["prep_v"])
        for cc in range(4):
            def f_woemm(E, cc=cc, hl=hl):
                E.matmul(pA[0:32, 0:512], wupt[:, :],
                         wotb[hl][:, 512 * cc:512 * (cc + 1)],
                         start=True, stop=True, skip_group_check=True
                         ).then_inc(s["prep_t"], 1)
            tq.append(f_woemm)
            bump("prep_t")
            wait(vq, "prep_t", C["prep_t"])

            def f_woeev(E, cc=cc, hl=hl):
                E.tensor_copy(out=woest_h[cc // 2][32 * hl:32 * hl + 32,
                                                   512 * (cc % 2):512 * (cc % 2) + 512],
                              in_=pA[0:32, 0:512]).then_inc(s["prep_v"], 1)
            vq.append(f_woeev)
            bump("prep_v")
            wait(tq, "prep_v", C["prep_v"])
    wait(qq, "prep_v", C["prep_v"])
    for half in range(2):
        outap = AP(tensor=d_prep_sh.ap().tensor, offset=3 * 131072 + half * 1024,
                   ap=[[2048, 64], [1, 1024]])
        dma(qq, "prep_dma", outap, woest_h[half][0:64, 0:1024])

    wait(gq, "prep_dma", C["prep_dma"])
    gq.append(lambda E: E.collective_compute(
        "AllGather", OP.bypass, replica_groups=[list(range(NCORES))],
        ins=[d_prep_sh.ap().opt()], outs=[d_prep_g.ap().opt()]).then_inc(s["ag1"], 1))
    bump("ag1")
    wait(gq, "ag1", C["ag1"])
    for wi, Adst in enumerate([A_q, A_k, A_v]):
        for dc in range(16):
            inap = AP(tensor=d_prep_g.ap().tensor,
                      offset=wi * 131072 + 128 * dc * 64,
                      ap=[[64, 128], [PREP_SZ, 8], [1, 64]])
            dma(gq, "aload", Adst[dc][:, :], inap)
    ALOAD = C["aload"]

    cutpoint('P3')
    for g in range(N_G):
        if g == 0:
            wait(tq, "aload", ALOAD)
            wait(tq, "tr1", TR1_ALL)
        for th in range(2):
            if g or th:
                wait(tq, "qkv_v", C["qkv_v"])
            if g == 0 and th == 0:
                wait(vq, "prep_dma", C["prep_dma"])
            if g:
                wait(vq, "shdma", C["shdma"])

            def f_kmm(E, g=g, th=th):
                for dc in range(16):
                    i2 = E.matmul(pq2[th][:, :], A_k[dc][:, 128 * g:128 * (g + 1)],
                                  ln1T[dc][:, 512 * th:512 * (th + 1)],
                                  start=(dc == 0), stop=(dc == 15))
                    if dc == 15:
                        i2.then_inc(s["qkv_t"], 1)
            tq.append(f_kmm)
            bump("qkv_t")
            wait(vq, "qkv_t", C["qkv_t"])

            def f_kev(E, th=th):
                E.tensor_copy(out=kst[:, 512 * th:512 * (th + 1)],
                              in_=pq2[th][:, :]).then_inc(s["qkv_v"], 1)
            vq.append(f_kev)
            bump("qkv_v")
        wait(qq, "qkv_v", C["qkv_v"])
        kshout = AP(tensor=d_ksh.ap().tensor, offset=128 * g * TC,
                    ap=[[TC, 128], [1, TC]])
        dma(qq, "shdma", kshout, kst[:, 0:1024])
        for th in range(2):
            def f_ksq(E, th=th):
                E.tensor_tensor(out=sqst[:, :], in0=kst[:, 512 * th:512 * (th + 1)],
                                in1=kst[:, 512 * th:512 * (th + 1)], op=OP.mult)
            vq.append(f_ksq)
            dinc(vq, "qkv_v")
            wait(tq, "qkv_v", C["qkv_v"])
            tq.append(lambda E: E.matmul(pqn, bdiag[:, :], sqst[:, :], start=True,
                                         stop=True, skip_group_check=True)
                      .then_inc(s["qkv_t"], 1))
            bump("qkv_t")
            wait(vq, "qkv_t", C["qkv_t"])

            def f_knred(E, g=g):
                E.tensor_reduce(out=nrm4[0:4, 0:1], in_=pqn, axis=AX, op=OP.max)
                E.drain()
                E.tensor_tensor(out=nrm[0:4, 1:2], in0=nrm[0:4, 1:2],
                                in1=nrm4[0:4, 0:1], op=OP.max)
            vq.append(f_knred)
            dinc(vq, "qkv_v")
            wait(tq, "qkv_v", C["qkv_v"])
    wait(tq, "qkv_v", C["qkv_v"])
    tq.append(lambda E: E.transpose(pqn[0:1, 0:4], nrm[0:4, 1:2], identf[0:4, 0:4])
              .then_inc(s["qkv_t"], 1))
    bump("qkv_t")
    wait(vq, "qkv_t", C["qkv_t"])

    def f_knfin(E):
        E.tensor_reduce(out=nrm[0:1, 1:2], in_=pqn[0:1, 0:4], axis=AX, op=OP.max)
        E.drain()
        E.tensor_copy(out=nrmb[0:1, 0:1], in_=nrm[0:1, 1:2])
    vq.append(f_knfin)
    dinc(vq, "qkv_v")
    wait(qq, "qkv_v", C["qkv_v"])
    knap = AP(tensor=d_ksh.ap().tensor, offset=HR * TC, ap=[[1, 16], [1, 1]])
    dma(qq, "shdma", knap, nrmb[0:16, 0:1], allow_slow_non_contiguous=True)

    for tt in range(8):
        wait(tq, "shdma", C["shdma"])

        def f_vmm(E, tt=tt):
            for dc in range(16):
                i2 = E.matmul(pq2[0][:, :], ln1T[dc][:, 128 * tt:128 * (tt + 1)],
                              A_v[dc][:, :], start=(dc == 0), stop=(dc == 15))
                if dc == 15:
                    i2.then_inc(s["qkv_t"], 1)
        tq.append(f_vmm)
        bump("qkv_t")
        wait(vq, "qkv_t", C["qkv_t"])
        wait(vq, "shdma", C["shdma"])
        vq.append(lambda E: E.tensor_copy(out=vstg[:, :], in_=pq2[0][:, :])
                  .then_inc(s["qkv_v"], 1))
        bump("qkv_v")
        wait(qq, "qkv_v", C["qkv_v"])
        dma(qq, "shdma", d_vsh[128 * tt:128 * (tt + 1), :], vstg[:, :])

    wait(gq, "shdma", C["shdma"])
    gq.append(lambda E: E.collective_compute(
        "AllGather", OP.bypass, replica_groups=[list(range(NCORES))],
        ins=[d_ksh.ap().opt()], outs=[d_kg.ap().opt()]).then_inc(s["ag2"], 1))
    bump("ag2")
    gq.append(lambda E: E.collective_compute(
        "AllGather", OP.bypass, replica_groups=[list(range(NCORES))],
        ins=[d_vsh.ap().opt()], outs=[d_vg.ap().opt()]).then_inc(s["ag2"], 1))
    bump("ag2")

    for g in range(N_G):
        for th in range(2):
            wait(tq, "qkv_v", C["qkv_v"])

            def f_qmm(E, g=g, th=th):
                for dc in range(16):
                    i2 = E.matmul(pq2[1][:, :], A_q[dc][:, 128 * g:128 * (g + 1)],
                                  ln1T[dc][:, 512 * th:512 * (th + 1)],
                                  start=(dc == 0), stop=(dc == 15))
                    if dc == 15:
                        i2.then_inc(s["qkv_t"], 1)
            tq.append(f_qmm)
            bump("qkv_t")
            wait(vq, "qkv_t", C["qkv_t"])

            def f_qev(E, g=g, th=th):
                E.tensor_copy(out=q_sb[g].ap()[:, 512 * th:512 * (th + 1)],
                              in_=pq2[1][:, :])
                E.drain()
                E.tensor_tensor(out=sqst[:, :],
                                in0=q_sb[g].ap()[:, 512 * th:512 * (th + 1)],
                                in1=q_sb[g].ap()[:, 512 * th:512 * (th + 1)], op=OP.mult)
            vq.append(f_qev)
            dinc(vq, "qkv_v")
            wait(tq, "qkv_v", C["qkv_v"])
            tq.append(lambda E: E.matmul(pqn, bdiag[:, :], sqst[:, :], start=True,
                                         stop=True, skip_group_check=True)
                      .then_inc(s["qkv_t"], 1))
            bump("qkv_t")
            wait(vq, "qkv_t", C["qkv_t"])

            def f_qnred(E, g=g):
                E.tensor_reduce(out=nrm4[0:4, 0:1], in_=pqn, axis=AX, op=OP.max)
                E.drain()
                E.tensor_tensor(out=nrm[0:4, 0:1], in0=nrm[0:4, 0:1],
                                in1=nrm4[0:4, 0:1], op=OP.max)
            vq.append(f_qnred)
            dinc(vq, "qkv_v")
            wait(tq, "qkv_v", C["qkv_v"])
    QKV_T_ALL = C["qkv_t"]
    QKV_V_ALL = C["qkv_v"]

    cutpoint('P4')
    wait(gq, "ag2", C["ag2"])
    wait(gq, "qkv_t", QKV_T_ALL)
    wait(gq, "qkv_v", QKV_V_ALL)
    knin = AP(tensor=d_kg.ap().tensor, offset=HR * TC,
              ap=[[1, 1], [HR * TC + 16, 8]])
    dma(gq, "kvn", kn8[0:1, 0:NCORES], knin, allow_slow_non_contiguous=True)
    for r in range(NCORES):
        for g in range(N_G):
            inap = AP(tensor=d_kg.ap().tensor,
                      offset=r * (HR * TC + 16) + 128 * g * TC,
                      ap=[[TC, 128], [1, TC]])
            outap = AP(tensor=k_sb[g].ap().tensor, offset=r * TC,
                       ap=[[T, 128], [1, TC]])
            dma(gq, "kvload", outap, inap)
        inap = AP(tensor=d_vg.ap().tensor, offset=r * TC * HR,
                  ap=[[HR, 128], [128 * HR, 8], [1, HR]])
        outap = AP(tensor=v_sb.ap().tensor, offset=r * 8 * 512,
                   ap=[[N_KC * HR, 128], [512, 8], [1, 512]])
        dma(gq, "kvload", outap, inap)
        wait(gq, "kvload", C["kvload"])
    KVLOAD = C["kvload"]

    def kv_need(kc):
        return 16 * 5 * (kc // 8 + 1)

    wait(vq, "kvn", 16)
    wait(tq, "qkv_v", QKV_V_ALL)
    tq.append(lambda E: E.transpose(pqn[0:1, 8:12], nrm[0:4, 0:1],
                                    identf[0:4, 0:4]).then_inc(s["mh_t"], 1))
    bump("mh_t")
    wait(vq, "mh_t", C["mh_t"])

    def f_mh1(E):
        E.tensor_reduce(out=nrm[0:1, 0:1], in_=pqn[0:1, 8:12], axis=AX, op=OP.max)
        E.drain()
        E.tensor_reduce(out=nrm[0:1, 1:2], in_=kn8[0:1, 0:NCORES], axis=AX, op=OP.max)
    vq.append(f_mh1)
    dinc(vq, "mh_v")
    wait(sq, "mh_v", C["mh_v"])

    def f_mh2(E):
        E.tensor_tensor(out=nrm[0:1, 2:3], in0=nrm[0:1, 0:1], in1=nrm[0:1, 1:2],
                        op=OP.mult)
    vq.append(f_mh2)
    dinc(vq, "mh_v")
    wait(sq, "mh_v", C["mh_v"])

    def f_mhs(E):
        E.activation(out=nrm[0:1, 2:3], in_=nrm[0:1, 2:3], func=AF.Ln, scale=1.0)
        E.drain()
        E.activation(out=nrm[0:1, 2:3], in_=nrm[0:1, 2:3], func=AF.Exp, scale=0.5)
    sq.append(f_mhs)
    dinc(sq, "mh_s")
    wait(vq, "mh_s", C["mh_s"])

    def f_mh3(E):
        E.tensor_scalar(out=grow[0:1, 33:34], in0=nrm[0:1, 2:3],
                        scalar1=float(-SCALE * MSAFE), scalar2=None, op0=OP.mult)
    vq.append(f_mh3)
    dinc(vq, "mh_v")
    wait(tq, "mh_v", C["mh_v"])
    tq.append(lambda E: E.matmul(pg[0:128, 16:17], grow[0:1, 128:256],
                                 grow[0:1, 33:34], start=True, stop=True,
                                 skip_group_check=True).then_inc(s["mh_t"], 1))
    bump("mh_t")
    wait(vq, "mh_t", C["mh_t"])
    vq.append(lambda E: E.tensor_copy(out=negMb[:, :], in_=pg[0:128, 16:17])
              .then_inc(s["mh_v"], 1))
    bump("mh_v")
    MH_DONE = C["mh_v"]

    cutpoint('P5')
    def qt_sg_kc(i):
        return i // (N_G * N_KC), (i // N_KC) % N_G, i % N_KC

    kv_hi = [0]

    def emit_unit(u):
        i = u // 2
        qt, sg, kc = qt_sg_kc(i)
        if u >= 2:
            wait(tq, "act", u - 1)
        if u == 0:
            wait(tq, "mh_t", C["mh_t"])
        if u % 2 == 0 and kv_need(kc) > kv_hi[0]:
            kv_hi[0] = kv_need(kc)
            wait(tq, "kvload", kv_hi[0])
        hls = (0, 1) if u % 2 == 0 else (2, 3)

        def f_sc(E, i=i, qt=qt, sg=sg, kc=kc, hls=hls):
            for hl in hls:
                bank = (4 * i + hl) % 5
                i2 = E.matmul(
                    p6.ap()[:, 512 * bank: 512 * bank + 512],
                    k_sb[sg].ap()[32 * hl:32 * hl + 32, 128 * kc:128 * (kc + 1)],
                    q_sb[sg].ap()[32 * hl:32 * hl + 32, QW * qt:QW * qt + QW],
                    start=True, stop=True, tile_position=(32 * hl, 0))
                if hl % 2 == 1:
                    i2.then_inc(s["tS"], 1)
        tq.append(f_sc)
        bump("tS")

    def emit_act(u):
        wait(sq, "tS", u + 1)
        if u >= 8:
            wait(sq, "cons", (u - 8) // 2 + 1)
        if u == 0:
            wait(sq, "mh_v", MH_DONE)
        slot = u % 8
        b0 = (2 * u) % 5
        if b0 != 4:
            inap = p6.ap()[:, 512 * b0:512 * b0 + 1024]
        else:
            inap = AP(tensor=p6.ap().tensor, offset=2048,
                      ap=[[3072, 128], [-2048, 2], [1, 512]])

        def f_act(E, inap=inap, slot=slot):
            E.activation(out=exp_sb.ap()[:, 1024 * slot:1024 * (slot + 1)],
                         in_=inap, func=AF.Exp, bias=negMb[:, 0:1],
                         scale=float(SCALE)).then_inc(s["act"], 1)
        sq.append(f_act)
        bump("act")

    def emit_avd(j):
        qt, sg, kc = qt_sg_kc(j)
        opens = (kc == 0)
        close = (kc == N_KC - 1)
        wait(tq, "act", 2 * j + 2)
        if opens:
            if j > 0:
                wait(tq, "sgv1", C["sgv1"])

                def f_bc(E):
                    for hl in range(4):
                        i2 = E.matmul(pd.ap()[32 * hl:32 * hl + 32, :],
                                      grow[0:1, 128:160], dsb[0:1, QW * hl:QW * hl + QW],
                                      start=True, stop=True, skip_group_check=True,
                                      tile_position=(0, 32 * hl))
                        if hl == 3:
                            i2.then_inc(s["sgt2"], 1)
                tq.append(f_bc)
                bump("sgt2")
            wait(tq, "sgv2", C["sgv2"])

            def f_open(E):
                E.matmul(pav.ap()[:, :], zerob[:, :], k_sb[0].ap()[:, 0:512],
                         start=True, stop=False, skip_group_check=True)
                E.matmul(pd.ap()[:, :], zerob[:, :], k_sb[0].ap()[:, 0:512],
                         start=True, stop=False, skip_group_check=True)
            tq.append(f_open)
        su = 2 * j
        slotA, slotB = su % 8, (su + 1) % 8

        def f_avd(E, sg=sg, kc=kc, slotA=slotA, slotB=slotB, close=close):
            for hl in range(4):
                slot = slotA if hl < 2 else slotB
                rhs = exp_sb.ap()[:, 1024 * slot + 512 * (hl % 2):
                                  1024 * slot + 512 * (hl % 2) + 512]
                h = 4 * sg + hl
                E.matmul(pav.ap()[32 * hl:32 * hl + 32, :],
                         v_sb.ap()[:, 512 * kc + 32 * h: 512 * kc + 32 * h + 32],
                         rhs, start=False, stop=close,
                         tile_position=(0, 32 * hl), skip_group_check=True)
            for hl in range(4):
                slot = slotA if hl < 2 else slotB
                rhs = exp_sb.ap()[:, 1024 * slot + 512 * (hl % 2):
                                  1024 * slot + 512 * (hl % 2) + 512]
                i2 = E.matmul(pd.ap()[32 * hl:32 * hl + 1, :],
                              onesb[:, 0:1], rhs, start=False, stop=close,
                              tile_position=(0, 32 * hl), skip_group_check=True)
                if hl == 3:
                    i2.then_inc(s["cons"], 1)
        tq.append(f_avd)
        bump("cons")

    def emit_sg_end(j):
        qt, sg, kc = qt_sg_kc(j)
        wait(vq, "cons", j + 1)

        def f_e1(E, qt=qt, sg=sg):
            E.tensor_copy(out=y_sb[sg].ap()[:, QW * qt:QW * qt + QW],
                          in_=pav.ap()[:, :])
            for hl in range(4):
                E.tensor_copy(out=dsb[0:1, QW * hl:QW * hl + QW],
                              in_=pd.ap()[32 * hl:32 * hl + 1, :])
        vq.append(f_e1)
        dinc(vq, "sgv1")
        wait(vq, "sgt2", C["sgt2"] + 1)

        def f_e3(E, qt=qt, sg=sg):
            E.tensor_copy(out=bcsb[:, :], in_=pd.ap()[:, :])
        vq.append(f_e3)
        dinc(vq, "sgv2")

        def f_e4(E, qt=qt, sg=sg):
            E.reciprocal(out=bcsb[:, :], in_=bcsb[:, :])
            E.drain()
            E.tensor_tensor(out=y_sb[sg].ap()[:, QW * qt:QW * qt + QW],
                            in0=y_sb[sg].ap()[:, QW * qt:QW * qt + QW],
                            in1=bcsb[:, :], op=OP.mult).then_inc(s["sgy"], 1)
        vq.append(f_e4)
        bump("sgy")

    pend = 0
    for i in range(GKC):
        emit_unit(2 * i)
        emit_unit(2 * i + 1)
        emit_act(2 * i)
        emit_act(2 * i + 1)
        while pend < GKC and pend + (4 if pend % N_KC < 2 else 2) <= i:
            emit_avd(pend)
            if pend % N_KC == N_KC - 1:
                emit_sg_end(pend)
            pend += 1
    while pend < GKC:
        emit_avd(pend)
        if pend % N_KC == N_KC - 1:
            emit_sg_end(pend)
        pend += 1
    wait(tq, "sgv1", C["sgv1"])

    def f_bc_last(E):
        for hl in range(4):
            i2 = E.matmul(pd.ap()[32 * hl:32 * hl + 32, :],
                          grow[0:1, 128:160], dsb[0:1, QW * hl:QW * hl + QW],
                          start=True, stop=True, skip_group_check=True,
                          tile_position=(0, 32 * hl))
            if hl == 3:
                i2.then_inc(s["sgt2"], 1)
    tq.append(f_bc_last)
    bump("sgt2")
    CONS_ALL = C["cons"]
    SGY_ALL = 8

    cutpoint('P6')
    wait(gq, "cons", CONS_ALL - 48)
    for g in range(N_G):
        for cp in range(2):
            core = 2 * g + cp
            inap = AP(tensor=d_prep_g.ap().tensor,
                      offset=core * PREP_SZ + 3 * 131072,
                      ap=[[2048, 64], [1, 2048]])
            dma(gq, "woeload", woe_v[g][64 * cp:64 * cp + 64, :], inap)
    WOELOAD = C["woeload"]
    wait(tq, "woeload", WOELOAD)
    wait(tq, "sgy", SGY_ALL)
    for qt in range(N_QT):
        for tl in range(4):
            ttg = 4 * qt + tl
            for cc in range(4):
                pb = cc % 2
                if ttg * 4 + cc >= 2:
                    wait(tq, "op_v", C["op_v"] - 1)

                def f_op(E, qt=qt, tl=tl, cc=cc, pb=pb):
                    for g in range(N_G):
                        i2 = E.matmul(pq2[pb][:, :],
                                      y_sb[g].ap()[:, QW * qt + 128 * tl: QW * qt + 128 * tl + 128],
                                      woe_v[g][:, 512 * cc:512 * (cc + 1)],
                                      start=(g == 0), stop=(g == 3))
                        if g == 3:
                            i2.then_inc(s["op_t"], 1)
                tq.append(f_op)
                bump("op_t")
                wait(vq, "op_t", C["op_t"])

                def f_opev(E, ttg=ttg, cc=cc, pb=pb):
                    E.tensor_copy(out=ln2n[ttg][:, 512 * cc:512 * (cc + 1)],
                                  in_=pq2[pb][:, :]).then_inc(s["op_v"], 1)
                vq.append(f_opev)
                bump("op_v")

    cutpoint('P7')
    xn4 = [xnst[0], xnst[1], xrest[0], xrest[1]]
    p8_dcnt = {}

    def p8_A(ttg):
        b = ttg % 4
        if ttg == 0:
            wait(qq, "op_v", 4)
        if ttg >= 4:
            wait(qq, "p8s", p8_dcnt[ttg - 4])
        dma(qq, "xre" + str(b), xn4[b][:, :], d_xs[128 * ttg:128 * (ttg + 1), :])
        wait(vq, "op_v", 4 * (ttg + 1))
        wait(vq, "xre" + str(b), C["xre" + str(b)])
        vq.append(lambda E, ttg=ttg, b=b: ln_stats(E, ln2n[ttg], b))
        dinc(vq, "p8v")

    def p8_B(ttg):
        b = ttg % 4
        emit_ln_s("p8v", "p8s", ln2n[ttg][:, :], ln2n[ttg][:, :], b)

    def p8_C(ttg):
        b = ttg % 4
        wait(vq, "p8s", C["p8s"])

        def f(E, ttg=ttg, b=b):
            E.tensor_tensor(out=xn4[b][:, :], in0=xn4[b][:, :],
                            in1=ln2n[ttg][:, :], op=OP.add)
            E.drain()
            ln_stats(E, xn4[b], b)
        vq.append(f)
        dinc(vq, "p8v")

    def p8_D(ttg):
        b = ttg % 4
        if ttg >= 2:
            wait(sq, "seq_t", C["seq_t"])
        emit_ln_s("p8v", "p8s", ln3stg[ttg % 2][:, :], xn4[b][:, :], b)
        p8_dcnt[ttg] = C["p8s"]

    def p8_EF(ttg):
        b = ttg % 2
        for half in range(2):
            wait(tq, "p8s", C["p8s"])
            if ttg or half:
                wait(tq, "tr3", C["tr3"])

            def f_tr3(E, b=b, half=half):
                for j in range(8):
                    dc = 8 * half + j
                    i2 = E.transpose(ptr_b8[:, 128 * j:128 * (j + 1)],
                                     ln3stg[b][:, 128 * dc:128 * (dc + 1)], identb[:, :])
                    if j == 7:
                        i2.then_inc(s["tr3"], 1)
            tq.append(f_tr3)
            bump("tr3")
            wait(vq, "tr3", C["tr3"])

            def f_tr3e(E, half=half, ttg=ttg):
                for j in range(8):
                    dc = 8 * half + j
                    i2 = E.tensor_copy(out=ln3T[dc][:, 128 * ttg:128 * (ttg + 1)],
                                       in_=ptr_b8[:, 128 * j:128 * (j + 1)])
                    if j == 7:
                        i2.then_inc(s["tr3"], 1)
            vq.append(f_tr3e)
            bump("tr3")
        wait(tq, "p8s", C["p8s"])

        def f_seq(E, ttg=ttg, b=b):
            for dcq in range(4):
                i2 = E.matmul(p6.ap()[0:1, 512 * dcq:512 * (dcq + 1)],
                              onesb[:, 0:1],
                              ln3stg[b][:, 512 * dcq:512 * (dcq + 1)],
                              start=(ttg == 0), stop=(ttg == 7),
                              skip_group_check=True)
                if dcq == 3:
                    i2.then_inc(s["seq_t"], 1)
        tq.append(f_seq)
        bump("seq_t")
        wait(vq, "seq_t", C["seq_t"])

    p8_A(0); p8_B(0); p8_A(1); p8_B(1)
    for t in range(8):
        p8_C(t); p8_D(t)
        if t + 2 < 8:
            p8_A(t + 2); p8_B(t + 2)
        if t >= 1:
            p8_EF(t - 1)
    p8_EF(7)
    wait(vq, "seq_t", C["seq_t"])
    vq.append(lambda E: E.tensor_copy(out=seqsb[0:1, :], in_=p6.ap()[0:1, 0:2048])
              .then_inc(s["seq_v"], 1))
    bump("seq_v")
    wait(qq, "seq_v", C["seq_v"])
    dma(qq, "seqdma", d_seqp.ap(), seqsb[0:1, :])
    wait(gq, "seqdma", C["seqdma"])
    gq.append(lambda E: E.collective_compute(
        "AllReduce", OP.add, replica_groups=[list(range(NCORES))],
        ins=[d_seqp.ap().opt()], outs=[d_seqs.ap().opt()]).then_inc(s["ar"], 1))
    bump("ar")
    wait(gq, "ar", C["ar"])
    dma(gq, "seqload", seqsb[0:1, :], d_seqs.ap())
    for dc in range(16):
        dma(gq, "simload", simsb[:, 32 * dc:32 * (dc + 1)],
            d_sim[128 * dc:128 * (dc + 1), :])
    SIMLOAD = C["simload"]
    wait(tq, "seqload", C["seqload"])
    wait(tq, "tr3", C["tr3"])

    def f_sct(E):
        for j in range(16):
            i2 = E.transpose(pg[0:128, 20 + j:21 + j],
                             seqsb[0:1, 128 * j:128 * (j + 1)], identf[0:1, 0:1],
                             )
            if j == 15:
                i2.then_inc(s["g_t"], 1)
    tq.append(f_sct)
    bump("g_t")
    wait(vq, "g_t", C["g_t"])
    vq.append(lambda E: E.tensor_copy(out=seqcol[:, :], in_=pg[0:128, 20:36])
              .then_inc(s["g_v"], 1))
    bump("g_v")

    cutpoint('P8')
    wait(vq, "simload", SIMLOAD)
    vq.append(lambda E: E.tensor_tensor(out=sim2s[:, :], in0=simsb[:, :],
                                        in1=simsb[:, :], op=OP.mult))
    dinc(vq, "g_v")
    wait(tq, "g_v", C["g_v"])
    wait(tq, "seq_v", C["seq_v"])

    def f_g2(E):
        for dc in range(16):
            E.matmul(pg[0:1, 0:32], onesf[:, 0:1], sim2s[:, 32 * dc:32 * (dc + 1)],
                     start=(dc == 0), stop=(dc == 15), skip_group_check=True)
        for dc in range(16):
            E.matmul(pg[0:1, 64:96], seqcol[:, dc:dc + 1], simsb[:, 32 * dc:32 * (dc + 1)],
                     start=(dc == 0), stop=(dc == 15), skip_group_check=True)
        for dc in range(16):
            i2 = E.matmul(pg[0:1, 96:97], seqcol[:, dc:dc + 1], seqcol[:, dc:dc + 1],
                          start=(dc == 0), stop=(dc == 15), skip_group_check=True)
            if dc == 15:
                i2.then_inc(s["g_t"], 1)
    tq.append(f_g2)
    bump("g_t")
    wait(sq, "g_t", C["g_t"])

    def f_g3(E):
        E.activation(out=grow[0:1, 0:32], in_=pg[0:1, 0:32], func=AF.Ln, scale=1.0)
        E.drain()
        E.activation(out=grow[0:1, 0:32], in_=grow[0:1, 0:32], func=AF.Exp, scale=-0.5)
        E.drain()
        E.activation(out=grow[0:1, 34:35], in_=pg[0:1, 96:97], func=AF.Ln, scale=1.0)
        E.drain()
        E.activation(out=grow[0:1, 34:35], in_=grow[0:1, 34:35], func=AF.Exp, scale=-0.5)
        E.drain()
        E.activation(out=grow[0:1, 64:96], in_=gatesb[0:1, :], func=AF.Exp, scale=-1.0)
        E.drain()
        E.activation(out=grow[0:1, 96:128], in_=pg[0:1, 64:96], func=AF.Copy, scale=1.0)
    sq.append(f_g3)
    dinc(sq, "g_s")
    wait(vq, "g_s", C["g_s"])

    def f_g4(E):
        E.tensor_scalar(out=grow[0:1, 64:96], in0=grow[0:1, 64:96],
                        scalar1=1.0, scalar2=None, op0=OP.add)
        E.drain()
        E.reciprocal(out=grow[0:1, 64:96], in_=grow[0:1, 64:96])
        E.drain()
        E.tensor_scalar(out=grow[0:1, 96:128], in0=grow[0:1, 96:128],
                        scalar1=grow[0:1, 34:35], scalar2=None, op0=OP.mult)
        E.drain()
        E.tensor_tensor(out=grow[0:1, 96:128], in0=grow[0:1, 96:128],
                        in1=grow[0:1, 0:32], op=OP.mult)
        E.drain()
        E.tensor_tensor(out=grow[0:1, 96:128], in0=grow[0:1, 96:128],
                        in1=grow[0:1, 64:96], op=OP.subtract)
    vq.append(f_g4)
    dinc(vq, "g_v")
    wait(tq, "g_v", C["g_v"])

    def f_g5(E):
        E.transpose(pg[0:EE, 37:38], grow[0:1, 96:128], identf[0:1, 0:1])
        E.matmul(pg[0:EE, 128:160], grow[0:1, 128:160], grow[0:1, 96:128],
                 start=True, stop=True, skip_group_check=True)
        E.matmul(pg[0:EE, 160:192], grow[0:1, 96:128], grow[0:1, 128:160],
                 start=True, stop=True, skip_group_check=True).then_inc(s["g_t"], 1)
    tq.append(f_g5)
    bump("g_t")
    wait(vq, "g_t", C["g_t"])

    def f_g6(E):
        E.tensor_copy(out=gsc[0:EE, 0:1], in_=pg[0:EE, 37:38])
        E.tensor_copy(out=gmx[0:EE, :], in_=pg[0:EE, 128:160])
        E.tensor_copy(out=gmx2[0:EE, :], in_=pg[0:EE, 160:192])
        E.drain()
        E.tensor_scalar(out=gsc[0:EE, 1:2], in0=gsc[0:EE, 0:1],
                        scalar1=0.0, scalar2=None, op0=OP.max)
        E.drain()
        E.tensor_scalar(out=gsc[0:EE, 2:3], in0=gsc[0:EE, 1:2],
                        scalar1=0.0, scalar2=None, op0=OP.is_gt)
        E.drain()
        E.tensor_tensor(out=gtmp[0:EE, :], in0=gmx[0:EE, :], in1=gmx2[0:EE, :],
                        op=OP.is_equal)
        E.drain()
        E.tensor_tensor(out=gtmp[0:EE, :], in0=gtmp[0:EE, :], in1=ltm[0:EE, :],
                        op=OP.mult)
        E.drain()
        E.tensor_tensor(out=gmx2[0:EE, :], in0=gmx[0:EE, :], in1=gmx2[0:EE, :],
                        op=OP.is_gt)
        E.drain()
        E.tensor_tensor(out=gmx2[0:EE, :], in0=gmx2[0:EE, :], in1=gtmp[0:EE, :],
                        op=OP.add)
        E.drain()
        E.tensor_reduce(out=gsc[0:EE, 3:4], in_=gmx2[0:EE, :], axis=AX, op=OP.add)
        E.drain()
        E.tensor_scalar(out=gsc[0:EE, 4:5], in0=gsc[0:EE, 3:4],
                        scalar1=16.0, scalar2=None, op0=OP.is_lt)
    vq.append(f_g6)
    dinc(vq, "g_v")
    wait(tq, "g_v", C["g_v"])
    tq.append(lambda E: E.transpose(pg[0:1, 38:70], gsc[0:EE, 2:3],
                                    identf[0:EE, 0:EE]).then_inc(s["g_t"], 1))
    bump("g_t")
    wait(vq, "g_t", C["g_t"])

    def f_g8(E):
        E.tensor_reduce(out=grow[0:1, 200:201], in_=pg[0:1, 38:70], axis=AX, op=OP.add)
        E.drain()
        E.tensor_scalar(out=grow[0:1, 200:201], in0=grow[0:1, 200:201],
                        scalar1=0.0, scalar2=None, op0=OP.is_equal)
    vq.append(f_g8)
    dinc(vq, "g_v")
    wait(tq, "g_v", C["g_v"])
    tq.append(lambda E: E.matmul(pg[0:EE, 36:37], grow[0:1, 128:160],
                                 grow[0:1, 200:201], start=True, stop=True,
                                 skip_group_check=True).then_inc(s["g_t"], 1))
    bump("g_t")
    wait(vq, "g_t", C["g_t"])

    def f_g10(E):
        E.tensor_copy(out=gsc[0:EE, 5:6], in_=pg[0:EE, 36:37])
        E.drain()
        E.tensor_tensor(out=gsc[0:EE, 6:7], in0=gsc[0:EE, 2:3], in1=gsc[0:EE, 4:5],
                        op=OP.max)
        E.drain()
        E.tensor_tensor(out=gsc[0:EE, 6:7], in0=gsc[0:EE, 6:7], in1=gsc[0:EE, 2:3],
                        op=OP.subtract)
        E.drain()
        E.tensor_tensor(out=gsc[0:EE, 6:7], in0=gsc[0:EE, 6:7], in1=gsc[0:EE, 5:6],
                        op=OP.mult)
        E.drain()
        E.tensor_tensor(out=gsc[0:EE, 6:7], in0=gsc[0:EE, 6:7], in1=gsc[0:EE, 2:3],
                        op=OP.add)
        E.drain()
        E.tensor_tensor(out=gsc[0:EE, 7:8], in0=gsc[0:EE, 1:2], in1=gsc[0:EE, 6:7],
                        op=OP.mult)
        E.drain()
        E.tensor_scalar(out=gsc[0:EE, 8:9], in0=gsc[0:EE, 6:7],
                        scalar1=1.0, scalar2=BIG, op0=OP.subtract, op1=OP.mult)
        E.drain()
        E.tensor_tensor(out=gsc[0:EE, 7:8], in0=gsc[0:EE, 7:8], in1=gsc[0:EE, 8:9],
                        op=OP.add)
    vq.append(f_g10)
    dinc(vq, "g_v")
    wait(tq, "g_v", C["g_v"])
    tq.append(lambda E: E.transpose(pg[0:1, 70:102], gsc[0:EE, 7:8],
                                    identf[0:EE, 0:EE]).then_inc(s["g_t"], 1))
    bump("g_t")
    wait(vq, "g_t", C["g_t"])

    def f_g12(E):
        E.tensor_copy(out=grow[0:1, 224:256], in_=pg[0:1, 70:102])
        E.drain()
        E.tensor_reduce(out=grow[0:1, 201:202], in_=grow[0:1, 224:256], axis=AX, op=OP.max)
        E.drain()
        E.tensor_scalar(out=grow[0:1, 202:203], in0=grow[0:1, 201:202],
                        scalar1=-1.0, scalar2=None, op0=OP.mult)
    vq.append(f_g12)
    dinc(vq, "g_v")
    wait(sq, "g_v", C["g_v"])
    sq.append(lambda E: E.activation(out=grow[0:1, 224:256], in_=grow[0:1, 224:256],
                                     func=AF.Exp, bias=grow[0:1, 202:203], scale=1.0))
    dinc(sq, "g_s")
    wait(vq, "g_s", C["g_s"])

    def f_g14(E):
        E.tensor_reduce(out=grow[0:1, 203:204], in_=grow[0:1, 224:256], axis=AX, op=OP.add)
        E.drain()
        E.reciprocal(out=grow[0:1, 203:204], in_=grow[0:1, 203:204])
        E.drain()
        E.tensor_scalar(out=grow[0:1, 224:256], in0=grow[0:1, 224:256],
                        scalar1=grow[0:1, 203:204], scalar2=None, op0=OP.mult)
    vq.append(f_g14)
    dinc(vq, "g_v")
    wait(tq, "g_v", C["g_v"])
    tq.append(lambda E: E.transpose(pg[0:EE, 102:103], grow[0:1, 224:256],
                                    identf[0:1, 0:1]).then_inc(s["g_t"], 1))
    bump("g_t")
    wait(vq, "g_t", C["g_t"])
    vq.append(lambda E: E.tensor_copy(out=routb[0:EE, :], in_=pg[0:EE, 102:103])
              .then_inc(s["g_v"], 1))
    bump("g_v")
    ROUT_DONE = C["g_v"]

    cutpoint('P9')
    for ch in range(8):
        if ch == 0:
            wait(qq, "seqdma", C["seqdma"])
            wait(qq, "op_v", 32)
        else:
            wait(qq, "ffw_v", C["ffw_v"])
        inap = AP(tensor=d_wvf.tensor, offset=ch * D, ap=[[8 * D, 32], [1, D]])
        dma(qq, "ffwload", wvstage[0:32, :], inap)
        wait(vq, "ffwload", C["ffwload"])
        if ch == 0:
            wait(vq, "seqdma", C["seqdma"])

        def f_cast(E, ch=ch):
            for j in range(4):
                E.tensor_copy(out=wvb[4 * ch + j][0:32, :],
                              in_=wvstage[0:32, 512 * j:512 * (j + 1)])
        vq.append(f_cast)
        dinc(vq, "ffw_v")
    WV_CAST = C["ffw_v"]
    wait(tq, "ffw_v", WV_CAST)
    wait(tq, "g_v", ROUT_DONE)

    def f_fvm(E):
        for hh in range(8):
            for cc in range(16):
                fi = (2048 * hh + 128 * cc) // 512
                co = (2048 * hh + 128 * cc) % 512
                i2 = E.matmul(pg[0:128, 128 + 16 * hh + cc:129 + 16 * hh + cc],
                              wvb[fi][0:32, co:co + 128], routb[0:EE, :],
                              start=True, stop=True, skip_group_check=True)
                if hh == 7 and cc == 15:
                    i2.then_inc(s["ffw_t"], 1)
    tq.append(f_fvm)
    bump("ffw_t")
    wait(vq, "ffw_t", C["ffw_t"])
    vq.append(lambda E: E.tensor_copy(out=ffst[:, 0:128], in_=pg[0:128, 128:256])
              .then_inc(s["ffw_e"], 1))
    bump("ffw_e")
    wait(tq, "ffw_e", C["ffw_e"])
    tq.append(lambda E: E.transpose(ptr_b8[:, 0:128], ffst[:, 0:128], identb[:, :])
              .then_inc(s["fftr_t"], 1))
    bump("fftr_t")
    wait(vq, "fftr_t", C["fftr_t"])
    vq.append(lambda E: E.tensor_copy(out=ffst[:, 128:256], in_=ptr_b8[:, 0:128])
              .then_inc(s["ffw_e"], 1))
    bump("ffw_e")
    wait(qq, "ffw_e", C["ffw_e"])
    fvout = AP(tensor=d_ffnp.ap().tensor, offset=0, ap=[[128, 128], [1, 128]])
    dma(qq, "ffpdma", fvout, ffst[:, 128:256])
    for ch in range(8):
        wait(qq, "ffw_v", C["ffw_v"])
        if ch == 0:
            wait(qq, "ffw_t", C["ffw_t"])
        inap = AP(tensor=d_owf.tensor, offset=ch * D, ap=[[8 * D, 32], [1, D]])
        dma(qq, "ffwload", wvstage[0:32, :], inap)
        wait(vq, "ffwload", C["ffwload"])
        if ch == 0:
            wait(vq, "ffw_t", C["ffw_t"])

        def f_cast2(E, ch=ch):
            for j in range(4):
                E.tensor_copy(out=wvb[4 * ch + j][0:32, :],
                              in_=wvstage[0:32, 512 * j:512 * (j + 1)])
        vq.append(f_cast2)
        dinc(vq, "ffw_v")
    wait(tq, "ffw_v", C["ffw_v"])
    wait(tq, "ffw_e", C["ffw_e"])

    def f_fom(E):
        for hh in range(8):
            for cc in range(16):
                fi = (2048 * hh + 128 * cc) // 512
                co = (2048 * hh + 128 * cc) % 512
                i2 = E.matmul(pg[0:128, 256 + 16 * hh + cc:257 + 16 * hh + cc],
                              wvb[fi][0:32, co:co + 128], routb[0:EE, :],
                              start=True, stop=True, skip_group_check=True)
                if hh == 7 and cc == 15:
                    i2.then_inc(s["ffw_t"], 1)
    tq.append(f_fom)
    bump("ffw_t")
    wait(vq, "ffw_t", C["ffw_t"])
    vq.append(lambda E: E.tensor_copy(out=ffst[:, 256:384], in_=pg[0:128, 256:384])
              .then_inc(s["ffw_e"], 1))
    bump("ffw_e")
    wait(tq, "ffw_e", C["ffw_e"])
    tq.append(lambda E: E.transpose(ptr_b8[:, 0:128], ffst[:, 256:384], identb[:, :])
              .then_inc(s["fftr_t"], 1))
    bump("fftr_t")
    wait(vq, "fftr_t", C["fftr_t"])
    vq.append(lambda E: E.tensor_copy(out=ffst[:, 384:512], in_=ptr_b8[:, 0:128])
              .then_inc(s["ffw_e"], 1))
    bump("ffw_e")
    wait(qq, "ffw_e", C["ffw_e"])
    foout = AP(tensor=d_ffnp.ap().tensor, offset=8 * D, ap=[[128, 128], [1, 128]])
    dma(qq, "ffpdma", foout, ffst[:, 384:512])

    wait(gq, "ffpdma", C["ffpdma"])
    gq.append(lambda E: E.collective_compute(
        "AllGather", OP.bypass, replica_groups=[list(range(NCORES))],
        ins=[d_ffnp.ap().opt()], outs=[d_ffng.ap().opt()]).then_inc(s["ag3"], 1))
    bump("ag3")
    wait(gq, "ag3", C["ag3"])
    for r in range(NCORES):
        inap = AP(tensor=d_ffng.ap().tensor, offset=r * 16 * D, ap=[[D, 8], [1, D]])
        dma(gq, "ffgload", wvgsb[8 * r:8 * r + 8, :], inap)
        inap2 = AP(tensor=d_ffng.ap().tensor, offset=r * 16 * D + 8 * D,
                   ap=[[D, 8], [1, D]])
        dma(gq, "ffgload", wodsb[8 * r:8 * r + 8, :], inap2)
    FFG = C["ffgload"]
    wait(tq, "ffgload", FFG)
    for dc in range(16):
        if dc:
            wait(tq, "fftr_v", C["fftr_v"])
        tq.append(lambda E, dc=dc: E.transpose(ptr_b8[:, 0:64],
                                               wvgsb[0:64, 128 * dc:128 * (dc + 1)],
                                               identb[0:64, 0:64]).then_inc(s["fftr_t"], 1))
        bump("fftr_t")
        wait(vq, "fftr_t", C["fftr_t"])
        vq.append(lambda E, dc=dc: E.tensor_copy(out=wveffT[dc][:, :],
                                                 in_=ptr_b8[:, 0:64]).then_inc(s["fftr_v"], 1))
        bump("fftr_v")

    cutpoint('P10')
    for qt in range(N_QT):
        if qt == 0:
            wait(tq, "fftr_v", C["fftr_v"])
        else:
            wait(tq, "ffc_v", C["ffc_v"])

        def f_cmm(E, qt=qt):
            for dc in range(16):
                i2 = E.matmul(pq2[0][0:64, :], wveffT[dc][:, :],
                              ln3T[dc][:, QW * qt:QW * qt + QW],
                              start=(dc == 0), stop=(dc == 15))
                if dc == 15:
                    i2.then_inc(s["ffc_t"], 1)
        tq.append(f_cmm)
        bump("ffc_t")
        wait(vq, "ffc_t", C["ffc_t"])
        vq.append(lambda E, qt=qt: E.tensor_copy(out=combT[0:64, QW * qt:QW * qt + QW],
                                                 in_=pq2[0][0:64, :]).then_inc(s["ffc_v"], 1))
        bump("ffc_v")
    wait(tq, "ffc_v", C["ffc_v"])
    wait(tq, "fftr_v", C["fftr_v"])
    ffo_banks = [[p6.ap()[:, 512 * dcq:512 * (dcq + 1)] for dcq in range(4)],
                 [p6.ap()[:, 2048:2560], p6.ap()[:, 2560:3072],
                  pav.ap()[:, :], pd.ap()[:, :]]]

    def ffo_evac_ap(sel):
        if sel == 0:
            return [p6.ap()[0:128, 0:2048]]
        return [p6.ap()[0:128, 2048:3072], pav.ap()[:, :], pd.ap()[:, :]]

    for ttg in range(8):
        sel = ttg % 2
        if ttg >= 2:
            wait(tq, "ffo_v", C["ffo_v"] - 1)

        def f_fmm(E, ttg=ttg, sel=sel):
            for dcq in range(4):
                i2 = E.matmul(ffo_banks[sel][dcq],
                              combT[0:64, 128 * ttg:128 * (ttg + 1)],
                              wodsb[0:64, 512 * dcq:512 * (dcq + 1)],
                              start=True, stop=True, skip_group_check=True)
                if dcq == 3:
                    i2.then_inc(s["ffo_t"], 1)
        tq.append(f_fmm)
        bump("ffo_t")
        wait(vq, "ffo_t", C["ffo_t"])

        def f_fev(E, ttg=ttg, sel=sel):
            if sel == 0:
                E.tensor_copy(out=ffn_t[ttg][:, :], in_=p6.ap()[0:128, 0:2048])
                E.drain().then_inc(s["ffo_v"], 1)
            else:
                E.tensor_copy(out=ffn_t[ttg][:, 0:1024], in_=p6.ap()[0:128, 2048:3072])
                E.tensor_copy(out=ffn_t[ttg][:, 1024:1536], in_=pav.ap()[:, :])
                E.tensor_copy(out=ffn_t[ttg][:, 1536:2048], in_=pd.ap()[:, :])
                E.drain().then_inc(s["ffo_v"], 1)
        vq.append(f_fev)
        bump("ffo_v")
    FFO_ALL = C["ffo_v"]

    p11_ccnt = {}

    def p11_A(ttg):
        b = ttg % 4
        if ttg == 0:
            wait(qq, "ffw_t", C["ffw_t"])
            wait(qq, "ffw_v", C["ffw_v"])
            wait(qq, "fftr_v", C["fftr_v"])
            wait(qq, "p8s", C["p8s"])
        if ttg >= 4:
            wait(qq, "p11v", p11_ccnt[ttg - 4])
        dma(qq, "xre" + str(b), xn4[b][:, :], d_xs[128 * ttg:128 * (ttg + 1), :])
        wait(vq, "ffo_v", min(FFO_ALL, ttg + 1))
        wait(vq, "xre" + str(b), C["xre" + str(b)])
        vq.append(lambda E, ttg=ttg, b=b: ln_stats(E, ffn_t[ttg], b))
        dinc(vq, "p11v")

    def p11_B(ttg):
        b = ttg % 4
        emit_ln_s("p11v", "p11s", ffn_t[ttg][:, :], ffn_t[ttg][:, :], b)

    def p11_C(ttg):
        b = ttg % 4
        wait(vq, "p11s", C["p11s"])

        def f(E, ttg=ttg, b=b):
            E.tensor_tensor(out=ffn_t[ttg][:, :], in0=ffn_t[ttg][:, :],
                            in1=xn4[b][:, :], op=OP.add)
            E.drain()
            E.tensor_tensor(out=ffn_t[ttg][:, :], in0=ffn_t[ttg][:, :],
                            in1=ln2n[ttg][:, :], op=OP.add)
        vq.append(f)
        dinc(vq, "p11v")
        p11_ccnt[ttg] = C["p11v"]
        wait(qq, "p11v", C["p11v"])
        dma(qq, "store", d_out[128 * ttg:128 * (ttg + 1), :], ffn_t[ttg][:, :])

    p11_A(0); p11_B(0); p11_A(1); p11_B(1)
    for t in range(8):
        p11_C(t)
        if t + 2 < 8:
            p11_A(t + 2); p11_B(t + 2)
    wait(gq, "store", C["store"])

    with nc.Block() as blk:
        @blk.tensor
        def _(E):
            for f in tq:
                f(E)

        @blk.vector
        def _(E):
            for f in vq:
                f(E)

        @blk.scalar
        def _(E):
            for f in sq:
                f(E)

        @blk.sync
        def _(E):
            for f in qq:
                f(E)

        @blk.gpsimd
        def _(E):
            for f in gq:
                f(E)

    stack.close()
    return nc


def shard_inputs(inputs):
    bf = ml_dtypes.bfloat16
    X = np.ascontiguousarray(np.asarray(inputs["L0_attn_x"], np.float32).reshape(T, D))
    Wq = np.asarray(inputs["L0_attn_Wq"], np.float32)
    Wk = np.asarray(inputs["L0_attn_Wk"], np.float32)
    Wv = np.asarray(inputs["L0_attn_Wv"], np.float32)
    Wo = np.asarray(inputs["L0_attn_Wo"], np.float32)
    Wdn = np.ascontiguousarray(np.asarray(inputs["L0_attn_Wdown"], np.float32))
    Wup = np.ascontiguousarray(np.asarray(inputs["L0_attn_Wup"], np.float32))
    sim = np.ascontiguousarray(np.asarray(inputs["L0_ffn_sim_matrix"], np.float32))
    gates = np.ascontiguousarray(np.asarray(inputs["L0_ffn_gates"], np.float32).reshape(1, EE))
    w_v = np.asarray(inputs["L0_ffn_w_v"], np.float32).reshape(EE, 64, D)
    o_w = np.asarray(inputs["L0_ffn_o_weights"], np.float32).reshape(EE, 64, D)

    ident = np.eye(128, dtype=np.float32)
    ltmask = np.tril(np.ones((EE, EE), np.float32), k=-1)
    expander = np.zeros((4, 128), np.float32)
    for hl in range(4):
        expander[hl, 32 * hl:32 * hl + 32] = 1.0
    bdg = np.zeros((128, 4), np.float32)
    for hl in range(4):
        bdg[32 * hl:32 * hl + 32, hl] = 1.0

    maps = []
    for c in range(NCORES):
        maps.append({
            "xs": np.ascontiguousarray(X[TC * c:TC * (c + 1)]),
            "wq_r": np.ascontiguousarray(Wq[256 * c:256 * (c + 1)]),
            "wk_r": np.ascontiguousarray(Wk[256 * c:256 * (c + 1)]),
            "wv_r": np.ascontiguousarray(Wv[256 * c:256 * (c + 1)]),
            "wo_c": np.ascontiguousarray(Wo[:, 256 * c:256 * (c + 1)]),
            "wdown": Wdn, "wup": Wup, "sim": sim, "gates": gates,
            "wvf": np.ascontiguousarray(w_v[:, 8 * c:8 * (c + 1), :].reshape(EE, 8 * D)),
            "owf": np.ascontiguousarray(o_w[:, 8 * c:8 * (c + 1), :].reshape(EE, 8 * D)),
            "identb": ident.astype(bf), "identf": ident,
            "ltmask": ltmask, "expander": expander,
            "bdiag": bdg.astype(bf),
        })
    return maps


def assemble(results):
    return np.concatenate([r["out"] for r in results], axis=0).reshape(1, T, D)



def kernel(**inputs):
    from concourse.bass_utils import run_bass_kernel_spmd
    nc = build(debug=False)
    maps = shard_inputs(inputs)
    res = run_bass_kernel_spmd(nc, maps, core_ids=list(range(NCORES)))
    return assemble(res.results)


# version 16 (final)
# speedup vs baseline: 1.1325x; 1.0595x over previous
import os
import numpy as np
import ml_dtypes
import concourse.bass as bass
import concourse.mybir as mybir
from concourse.bass_types import AP

F32 = mybir.dt.float32
BF16 = mybir.dt.bfloat16
AF = mybir.ActivationFunctionType
OP = mybir.AluOpType
AX = mybir.AxisListType.X

NCORES = 8
TC = 1024
T = 8192
D = 2048
NH = 16
RK = 32
HR = NH * RK
HPC = 2
EE = 32
SCALE = 1.0 / float(np.sqrt(RK))
EPS = 1e-5
BIG = float(np.finfo(np.float32).max)
MSAFE = 1.03

N_G = 4
N_QT = 2
QW = 512
N_KC = 64
GKC = N_QT * N_G * N_KC
PREP_SZ = 3 * D * 64 + 64 * D


def build(debug=False):
    nc = bass.Bass("TRN2", target_bir_lowering=False, debug=debug)

    din = lambda name, shape, dt: nc.dram_tensor(name, shape, dt, kind="ExternalInput").ap()
    d_xs = din("xs", [TC, D], F32)
    d_wq = din("wq_r", [HPC * 128, D], F32)
    d_wk = din("wk_r", [HPC * 128, D], F32)
    d_wv = din("wv_r", [HPC * 128, D], F32)
    d_wo = din("wo_c", [D, HPC * 128], F32)
    d_wdn = din("wdown", [128, RK], F32)
    d_wup = din("wup", [RK, 128], F32)
    d_sim = din("sim", [D, EE], F32)
    d_gates = din("gates", [1, EE], F32)
    d_wvf = din("wvf", [EE, 8 * D], F32)
    d_owf = din("owf", [EE, 8 * D], F32)
    d_idb = din("identb", [128, 128], BF16)
    d_idf = din("identf", [128, 128], F32)
    d_lt = din("ltmask", [EE, EE], F32)
    d_exp4 = din("expander", [4, 128], F32)
    d_bdg = din("bdiag", [128, 4], BF16)
    d_out = nc.dram_tensor("out", [TC, D], F32, kind="ExternalOutput").ap()

    d_prep_sh = nc.dram_tensor("prep_sh", [PREP_SZ], BF16)
    d_prep_g = nc.dram_tensor("prep_g", [NCORES * PREP_SZ], BF16)
    d_ksh = nc.dram_tensor("ksh", [HR * TC + 16], BF16)
    d_kg = nc.dram_tensor("kg", [NCORES * (HR * TC + 16)], BF16)
    d_vsh = nc.dram_tensor("vsh", [TC, HR], BF16)
    d_vg = nc.dram_tensor("vg", [NCORES * TC, HR], BF16)
    d_seqp = nc.dram_tensor("seqp", [1, D], F32)
    d_seqs = nc.dram_tensor("seqs", [1, D], F32)
    d_ffnp = nc.dram_tensor("ffnp", [16, D], BF16)
    d_ffng = nc.dram_tensor("ffng", [NCORES * 16, D], BF16)

    from contextlib import ExitStack
    stack = ExitStack()
    SB = lambda name, shape, dt: stack.enter_context(nc.sbuf_tensor(name, shape, dt))
    PST = lambda name, shape, dt: stack.enter_context(nc.psum_tensor(name, shape, dt))
    SEM = lambda name: stack.enter_context(nc.semaphore(name))

    k_sb = [SB(f"k{g}", [128, T], BF16) for g in range(N_G)]
    v_sb = SB("v", [128, N_KC * HR], BF16)
    q_sb = [SB(f"q{g}", [128, TC], BF16) for g in range(N_G)]
    y_sb = [SB(f"y{g}", [128, TC], BF16) for g in range(N_G)]
    exp_sb = SB("exps", [128, 8 * 1024], BF16)
    identb = SB("identb_s", [128, 128], BF16)
    identf = SB("identf_s", [128, 128], F32)
    onesb = SB("onesb", [128, 1], BF16)
    onesf = SB("onesf", [128, 1], F32)
    zerob = SB("zerob", [128, 128], BF16)
    epsb = SB("epsb", [128, 1], F32)
    negMb = SB("negMb", [128, 1], F32)
    bdiag = SB("bdiag_s", [128, 4], BF16)
    exp4 = SB("exp4_s", [4, 128], F32)
    ltm = SB("ltm_s", [EE, EE], F32)
    wdnf = SB("wdnf", [128, RK], F32)
    wupf = SB("wupf", [RK, 128], F32)
    nrm = SB("nrm", [NH, 4], F32)
    nrmb = SB("nrmb", [NH, 1], BF16)
    kn8 = SB("kn8", [NH, NCORES], BF16)
    bn6 = SB("bn6", [128, 24], F32)
    mv2 = SB("mv2", [128, 8], F32)
    rstd = SB("rstd", [128, 4], F32)
    negm = SB("negm", [128, 4], F32)
    dsb = SB("dsb", [1, 4 * QW], F32)
    nrm4 = SB("nrm4", [4, 2], F32)
    bcsb = SB("bcsb", [128, QW], F32)
    gsc = SB("gsc", [EE, 16], F32)
    gmx = SB("gmx", [EE, EE], F32)
    gmx2 = SB("gmx2", [EE, EE], F32)
    gtmp = SB("gtmp", [EE, EE], F32)
    grow = SB("grow", [1, 256], F32)
    routb = SB("routb", [EE, 1], BF16)
    simsb = SB("simsb", [128, 512], F32)
    sim2s = SB("sim2s", [128, 512], F32)
    seqcol = SB("seqcol", [128, 16], F32)
    gatesb = SB("gatesb", [1, EE], F32)
    ffst = SB("ffst", [128, 512], BF16)

    def view(t, dt, byte_off, part, cols):
        a = t.ap().bitcast(dt)
        esz = 4 if dt == F32 else 2
        assert byte_off % esz == 0
        return a[0:part, byte_off // esz: byte_off // esz + cols]

    ln1T = ([view(k_sb[0], BF16, 2048 * i, 128, TC) for i in range(8)]
            + [view(k_sb[1], BF16, 2048 * i, 128, TC) for i in range(8)])
    A_q = [view(k_sb[2], BF16, 1024 * i, 128, 512) for i in range(16)]
    A_k = [view(k_sb[3], BF16, 1024 * i, 128, 512) for i in range(16)]
    A_v = [view(v_sb, BF16, 1024 * i, 128, 512) for i in range(16)]
    xst = [view(v_sb, F32, 16384 + 8192 * i, 128, D) for i in range(2)]
    ln1st = [view(v_sb, BF16, 32768 + 4096 * i, 128, D) for i in range(2)]
    wqst = [view(v_sb, F32, 40960 + 8192 * i, 128, D) for i in range(2)]
    wost = view(v_sb, F32, 57344, 128, D)
    wotb = [view(exp_sb, BF16, 4096 * i, 128, D) for i in range(2)]
    wupt = view(exp_sb, BF16, 8192, 128, RK)
    prep_st = view(exp_sb, BF16, 8320, 128, 1024)
    kst = prep_st
    sqst = view(exp_sb, BF16, 10368, 128, 512)
    vstg = view(y_sb[2], BF16, 0, 128, 512)
    woest_h = [view(y_sb[i], BF16, 0, 64, 1024) for i in range(2)]

    ln2n = [view(k_sb[tt // 4], BF16, 4096 * (tt % 4), 128, D) for tt in range(8)]
    ln3T = ([view(k_sb[2], BF16, 2048 * i, 128, TC) for i in range(8)]
            + [view(k_sb[3], BF16, 2048 * i, 128, TC) for i in range(8)])
    woe_v = [view(v_sb, BF16, 4096 * g, 128, D) for g in range(N_G)]
    xnst = [view(v_sb, F32, 16384 + 8192 * i, 128, D) for i in range(2)]
    ln3stg = [view(v_sb, BF16, 32768 + 4096 * i, 128, D) for i in range(2)]
    wvstage = view(v_sb, F32, 57344, 32, D)
    seqsb = view(exp_sb, F32, 0, 1, D)
    wvb = ([view(exp_sb, BF16, 1024 * i, 32, 512) for i in range(16)]
           + [view(v_sb, BF16, 40960 + 1024 * i, 32, 512) for i in range(16)])
    assert len(wvb) == 32
    wvgsb = view(v_sb, BF16, 32768, 64, D)
    wodsb = view(v_sb, BF16, 36864, 64, D)
    wveffT = [view(exp_sb, BF16, 128 * i, 128, 64) for i in range(16)]
    combT = view(exp_sb, BF16, 2048, 64, TC)
    xrest = [view(exp_sb, F32, 4096, 128, D), view(v_sb, F32, 57344, 128, D)]
    ffn_t = [view(v_sb, F32, 0, 128, D), view(v_sb, F32, 8192, 128, D),
             view(v_sb, F32, 40960, 128, D), view(v_sb, F32, 49152, 128, D),
             view(k_sb[2], F32, 0, 128, D), view(k_sb[2], F32, 8192, 128, D),
             view(k_sb[3], F32, 0, 128, D), view(k_sb[3], F32, 8192, 128, D)]

    p6 = PST("p6", [128, 3072], F32)
    pav = PST("pav", [128, 512], F32)
    pd = PST("pd", [128, 512], F32)

    def psview(ps, dt, byte_off, part, cols):
        a = ps.ap().bitcast(dt)
        esz = 4 if dt == F32 else 2
        return a[0:part, byte_off // esz: byte_off // esz + cols]

    pA = p6.ap()[:, 0:512]
    pAB = p6.ap()[:, 512:1024]
    pqn = p6.ap()[0:4, 1536:2048]
    ptr_b1 = psview(p6, BF16, 8192, 128, 1024)
    pq2 = [p6.ap()[:, 2048 + 512 * i: 2048 + 512 * (i + 1)] for i in range(2)]
    ptr_b8 = psview(pav, BF16, 0, 128, 1024)
    pg = pd.ap()

    s, C = {}, {}
    for n in ["cload", "xload0", "xload1", "xre0", "xre1", "xre2", "xre3", "lnv", "lns", "tr1", "wload", "prep_t", "prep_v",
              "prep_dma", "ag1", "aload", "qkv_t", "qkv_v", "shdma", "ag2",
              "kvload", "kvn", "mh_t", "mh_v", "mh_s", "tS", "act", "cons",
              "sgv1", "sgt2", "sgv2", "sgy", "woeload", "op_t", "op_v",
              "p8v", "p8s", "tr3", "seq_t", "seq_v", "seqdma", "ar",
              "seqload", "g_t", "g_v", "g_s", "simload", "ffwload",
              "ffw_v", "ffw_t", "ffw_e", "ffpdma", "ag3", "ffgload",
              "fftr_t", "fftr_v", "ffc_t", "ffc_v", "ffo_t", "ffo_v",
              "p11v", "p11s", "store"]:
        s[n] = SEM("s_" + n)
        C[n] = 0

    _CUT = [False]

    class _Q(list):
        def append(self, x):
            if not _CUT[0]:
                super().append(x)

    ops = {e: _Q() for e in "tvsqg"}
    tq, vq, sq, qq, gq = (ops[e] for e in "tvsqg")
    MK_CUT = os.environ.get("MK_CUT", "")

    def cutpoint(name):
        if MK_CUT == name and not _CUT[0]:
            for ttg in range(8):
                bump("store", 16)
                qq.append(lambda E, ttg=ttg: E.dma_start(
                    out=d_out[128 * ttg:128 * (ttg + 1), 0:128],
                    in_=identf[:, :]).then_inc(s["store"], 16))
            v = C["store"]
            gq.append(lambda E, v=v: E.wait_ge(s["store"], v))
            _CUT[0] = True

    def bump(name, amt=1):
        C[name] += amt
        return C[name]

    def dma(eq, sem_name, out, in_, **kw):
        bump(sem_name, 16)
        eq.append(lambda E, out=out, in_=in_, kw=kw, sn=sem_name:
                  E.dma_start(out=out, in_=in_, **kw).then_inc(s[sn], 16))

    def wait(eq, sem_name, val=None):
        v = C[sem_name] if val is None else val
        eq.append(lambda E, sn=sem_name, v=v: E.wait_ge(s[sn], v))

    def dinc(eq, sem_name):
        bump(sem_name)
        eq.append(lambda E, sn=sem_name: E.drain().then_inc(s[sn], 1))

    dma(qq, "cload", identb[:, :], d_idb)
    dma(qq, "cload", identf[:, :], d_idf)
    dma(qq, "cload", ltm[:, :], d_lt)
    dma(qq, "cload", exp4[:, :], d_exp4)
    dma(qq, "cload", bdiag[:, :], d_bdg)
    dma(qq, "cload", gatesb[:, :], d_gates)
    dma(qq, "cload", wdnf[:, :], d_wdn)
    dma(qq, "cload", wupf[:, :], d_wup)

    def f_memset(E):
        E.memset(onesb[:, :], 1.0)
        E.memset(onesf[:, :], 1.0)
        E.memset(zerob[:, :], 0.0)
        E.memset(epsb[:, :], EPS)
        E.memset(grow[:, :], 1.0)
        E.memset(nrm[:, :], 0.0)
        E.memset(nrmb[:, :], 0.0)
    vq.append(f_memset)
    dinc(vq, "cload")
    CLOAD = C["cload"]

    def ln_stats(E, src, b):
        for ii in range(4):
            E.bn_stats(out=bn6[:, 6 * ii:6 * ii + 6], in_=src[:, 512 * ii:512 * (ii + 1)])
        E.drain()
        E.bn_aggr(out=mv2[:, 2 * b:2 * b + 2], in_=bn6[:, :])
        E.drain()
        E.tensor_scalar(out=negm[:, b:b + 1], in0=mv2[:, 2 * b:2 * b + 1],
                        scalar1=-1.0, scalar2=None, op0=OP.mult)

    def emit_ln_s(sem_wait_v, sem_s, dst, src_, b):
        wait(sq, sem_wait_v, C[sem_wait_v])

        def f1(E, dst=dst, src_=src_, b=b):
            E.activation(out=rstd[:, b:b + 1], in_=mv2[:, 2 * b + 1:2 * b + 2],
                         func=AF.Ln, bias=epsb[:, 0:1], scale=1.0)
            E.drain()
            E.activation(out=rstd[:, b:b + 1], in_=rstd[:, b:b + 1],
                         func=AF.Exp, scale=-0.5)
            E.drain()
            E.activation(out=dst, in_=src_, func=AF.Identity,
                         bias=negm[:, b:b + 1], scale=1.0)
            E.drain()
            E.activation(out=dst, in_=dst, func=AF.Identity,
                         scale=rstd[:, b:b + 1])
        sq.append(f1)
        dinc(sq, sem_s)

    for tt in range(8):
        b = tt % 2
        if tt >= 2:
            wait(qq, "tr1", C["tr1"])
        dma(qq, "xload" + str(b), xst[b][:, :], d_xs[128 * tt:128 * (tt + 1), :])
        wait(vq, "xload" + str(b), C["xload" + str(b)])
        if tt == 0:
            wait(vq, "cload", CLOAD)
        vq.append(lambda E, b=b: ln_stats(E, xst[b], b))
        dinc(vq, "lnv")
        emit_ln_s("lnv", "lns", ln1st[b][:, :], xst[b][:, :], b)
        for half in range(2):
            wait(tq, "lns", C["lns"])
            if tt or half:
                wait(tq, "tr1", C["tr1"])

            def f_tr(E, b=b, half=half):
                for j in range(8):
                    dc = 8 * half + j
                    i2 = E.transpose(ptr_b1[:, 128 * j:128 * (j + 1)],
                                     ln1st[b][:, 128 * dc:128 * (dc + 1)], identb[:, :])
                    if j == 7:
                        i2.then_inc(s["tr1"], 1)
            tq.append(f_tr)
            bump("tr1")
            wait(vq, "tr1", C["tr1"])

            def f_ev(E, half=half, tt=tt):
                for j in range(8):
                    dc = 8 * half + j
                    i2 = E.tensor_copy(out=ln1T[dc][:, 128 * tt:128 * (tt + 1)],
                                       in_=ptr_b1[:, 128 * j:128 * (j + 1)])
                    if j == 7:
                        i2.then_inc(s["tr1"], 1)
            vq.append(f_ev)
            bump("tr1")
    TR1_ALL = C["tr1"]

    cutpoint('P1')
    for wi, dsrc in enumerate([d_wq, d_wk, d_wv]):
        for hl in range(HPC):
            if wi > 0 or hl > 0:
                wait(qq, "prep_t", C["prep_t"])
            dma(qq, "wload", wqst[hl][:, :], dsrc[128 * hl:128 * (hl + 1), :])
        wait(vq, "wload", C["wload"])
        if wi == 0:
            wait(vq, "cload", CLOAD)

        def f_wcast(E, wi=wi):
            for hl in range(HPC):
                E.tensor_copy(out=wotb[hl][:, :], in_=wqst[hl][:, :])
            if wi == 0:
                E.tensor_copy(out=wupt[:, 0:RK], in_=wdnf[:, :])
        vq.append(f_wcast)
        dinc(vq, "prep_v")
        wait(tq, "prep_v", C["prep_v"])
        if wi == 0:
            wait(tq, "cload", CLOAD)
        for dc in range(16):
            def f_amm(E, dc=dc):
                for hl in range(HPC):
                    i2 = E.matmul(pA[:, 64 * (dc % 8) + 32 * hl: 64 * (dc % 8) + 32 * hl + 32],
                                  wotb[hl][:, 128 * dc:128 * (dc + 1)], wupt[:, 0:RK],
                                  start=True, stop=True, skip_group_check=True)
                    if dc % 8 == 7 and hl == HPC - 1:
                        i2.then_inc(s["prep_t"], 1)
            tq.append(f_amm)
            if dc % 8 == 7:
                bump("prep_t")
                wait(vq, "prep_t", C["prep_t"])
                if dc == 7:
                    wait(vq, "prep_dma", C["prep_dma"])

                def f_aev(E, dc=dc):
                    E.tensor_copy(
                        out=prep_st[:, 512 * ((dc // 8) % 2):512 * ((dc // 8) % 2) + 512],
                        in_=pA[:, 0:512]).then_inc(s["prep_v"], 1)
                vq.append(f_aev)
                bump("prep_v")
                wait(tq, "prep_v", C["prep_v"])
        wait(qq, "prep_v", C["prep_v"])
        for half in range(2):
            base = wi * 131072 + half * 8 * 128 * 64
            outap = AP(tensor=d_prep_sh.ap().tensor, offset=base,
                       ap=[[64, 128], [128 * 64, 8], [1, 64]])
            inap = AP(tensor=prep_st.tensor, offset=prep_st.offset + 512 * half,
                      ap=[[prep_st.ap[0][0], 128], [64, 8], [1, 64]])
            dma(qq, "prep_dma", outap, inap)

    wait(tq, "cload", CLOAD)
    tq.append(lambda E: E.transpose(pA[0:128, 0:32], wupf[:, :], identf[0:RK, 0:RK])
              .then_inc(s["prep_t"], 1))
    bump("prep_t")
    wait(vq, "prep_t", C["prep_t"])
    vq.append(lambda E: E.tensor_copy(out=wupt[:, :], in_=pA[0:128, 0:32])
              .then_inc(s["prep_v"], 1))
    bump("prep_v")
    wait(tq, "prep_v", C["prep_v"])

    for hl in range(HPC):
        wo_r = AP(tensor=d_wo.tensor, offset=128 * hl,
                  ap=[[256, 128], [128 * 256, 16], [1, 128]])
        if hl > 0:
            wait(qq, "prep_t", C["prep_t"])
        dma(qq, "wload", wost[:, :], wo_r)
        wait(tq, "wload", C["wload"])
        for quad in range(4):
            def f_wotr(E, quad=quad):
                for j in range(4):
                    dc = 4 * quad + j
                    i2 = E.transpose(pAB[:, 128 * j:128 * (j + 1)],
                                     wost[:, 128 * dc:128 * (dc + 1)], identf[:, :])
                    if j == 3:
                        i2.then_inc(s["prep_t"], 1)
            tq.append(f_wotr)
            bump("prep_t")
            wait(vq, "prep_t", C["prep_t"])

            def f_woev(E, quad=quad, hl=hl):
                E.tensor_copy(out=wotb[hl][:, 512 * quad:512 * (quad + 1)],
                              in_=pAB[:, :]).then_inc(s["prep_v"], 1)
            vq.append(f_woev)
            bump("prep_v")
            wait(tq, "prep_v", C["prep_v"])
        for cc in range(4):
            def f_woemm(E, cc=cc, hl=hl):
                E.matmul(pA[0:32, 0:512], wupt[:, :],
                         wotb[hl][:, 512 * cc:512 * (cc + 1)],
                         start=True, stop=True, skip_group_check=True
                         ).then_inc(s["prep_t"], 1)
            tq.append(f_woemm)
            bump("prep_t")
            wait(vq, "prep_t", C["prep_t"])

            def f_woeev(E, cc=cc, hl=hl):
                E.tensor_copy(out=woest_h[cc // 2][32 * hl:32 * hl + 32,
                                                   512 * (cc % 2):512 * (cc % 2) + 512],
                              in_=pA[0:32, 0:512]).then_inc(s["prep_v"], 1)
            vq.append(f_woeev)
            bump("prep_v")
            wait(tq, "prep_v", C["prep_v"])
    wait(qq, "prep_v", C["prep_v"])
    for half in range(2):
        outap = AP(tensor=d_prep_sh.ap().tensor, offset=3 * 131072 + half * 1024,
                   ap=[[2048, 64], [1, 1024]])
        dma(qq, "prep_dma", outap, woest_h[half][0:64, 0:1024])

    wait(gq, "prep_dma", C["prep_dma"])
    gq.append(lambda E: E.collective_compute(
        "AllGather", OP.bypass, replica_groups=[list(range(NCORES))],
        ins=[d_prep_sh.ap().opt()], outs=[d_prep_g.ap().opt()]).then_inc(s["ag1"], 1))
    bump("ag1")
    wait(gq, "ag1", C["ag1"])
    for wi, Adst in enumerate([A_q, A_k, A_v]):
        for dc in range(16):
            inap = AP(tensor=d_prep_g.ap().tensor,
                      offset=wi * 131072 + 128 * dc * 64,
                      ap=[[64, 128], [PREP_SZ, 8], [1, 64]])
            dma(gq, "aload", Adst[dc][:, :], inap)
    ALOAD = C["aload"]

    cutpoint('P3')
    for g in range(N_G):
        if g == 0:
            wait(tq, "aload", ALOAD)
            wait(tq, "tr1", TR1_ALL)
        for th in range(2):
            if g or th:
                wait(tq, "qkv_v", C["qkv_v"])
            if g == 0 and th == 0:
                wait(vq, "prep_dma", C["prep_dma"])
            if g:
                wait(vq, "shdma", C["shdma"])

            def f_kmm(E, g=g, th=th):
                for dc in range(16):
                    i2 = E.matmul(pq2[th][:, :], A_k[dc][:, 128 * g:128 * (g + 1)],
                                  ln1T[dc][:, 512 * th:512 * (th + 1)],
                                  start=(dc == 0), stop=(dc == 15))
                    if dc == 15:
                        i2.then_inc(s["qkv_t"], 1)
            tq.append(f_kmm)
            bump("qkv_t")
            wait(vq, "qkv_t", C["qkv_t"])

            def f_kev(E, th=th):
                E.tensor_copy(out=kst[:, 512 * th:512 * (th + 1)],
                              in_=pq2[th][:, :]).then_inc(s["qkv_v"], 1)
            vq.append(f_kev)
            bump("qkv_v")
        wait(qq, "qkv_v", C["qkv_v"])
        kshout = AP(tensor=d_ksh.ap().tensor, offset=128 * g * TC,
                    ap=[[TC, 128], [1, TC]])
        dma(qq, "shdma", kshout, kst[:, 0:1024])
        for th in range(2):
            def f_ksq(E, th=th):
                E.tensor_tensor(out=sqst[:, :], in0=kst[:, 512 * th:512 * (th + 1)],
                                in1=kst[:, 512 * th:512 * (th + 1)], op=OP.mult)
            vq.append(f_ksq)
            dinc(vq, "qkv_v")
            wait(tq, "qkv_v", C["qkv_v"])
            tq.append(lambda E: E.matmul(pqn, bdiag[:, :], sqst[:, :], start=True,
                                         stop=True, skip_group_check=True)
                      .then_inc(s["qkv_t"], 1))
            bump("qkv_t")
            wait(vq, "qkv_t", C["qkv_t"])

            def f_knred(E, g=g):
                E.tensor_reduce(out=nrm4[0:4, 0:1], in_=pqn, axis=AX, op=OP.max)
                E.drain()
                E.tensor_tensor(out=nrm[0:4, 1:2], in0=nrm[0:4, 1:2],
                                in1=nrm4[0:4, 0:1], op=OP.max)
            vq.append(f_knred)
            dinc(vq, "qkv_v")
            wait(tq, "qkv_v", C["qkv_v"])
    wait(tq, "qkv_v", C["qkv_v"])
    tq.append(lambda E: E.transpose(pqn[0:1, 0:4], nrm[0:4, 1:2], identf[0:4, 0:4])
              .then_inc(s["qkv_t"], 1))
    bump("qkv_t")
    wait(vq, "qkv_t", C["qkv_t"])

    def f_knfin(E):
        E.tensor_reduce(out=nrm[0:1, 1:2], in_=pqn[0:1, 0:4], axis=AX, op=OP.max)
        E.drain()
        E.tensor_copy(out=nrmb[0:1, 0:1], in_=nrm[0:1, 1:2])
    vq.append(f_knfin)
    dinc(vq, "qkv_v")
    wait(qq, "qkv_v", C["qkv_v"])
    knap = AP(tensor=d_ksh.ap().tensor, offset=HR * TC, ap=[[1, 16], [1, 1]])
    dma(qq, "shdma", knap, nrmb[0:16, 0:1], allow_slow_non_contiguous=True)

    wait(gq, "shdma", C["shdma"])
    gq.append(lambda E: E.collective_compute(
        "AllGather", OP.bypass, replica_groups=[list(range(NCORES))],
        ins=[d_ksh.ap().opt()], outs=[d_kg.ap().opt()]).then_inc(s["ag2"], 1))
    bump("ag2")

    for tt in range(8):
        wait(tq, "shdma", C["shdma"])

        def f_vmm(E, tt=tt):
            for dc in range(16):
                i2 = E.matmul(pq2[0][:, :], ln1T[dc][:, 128 * tt:128 * (tt + 1)],
                              A_v[dc][:, :], start=(dc == 0), stop=(dc == 15))
                if dc == 15:
                    i2.then_inc(s["qkv_t"], 1)
        tq.append(f_vmm)
        bump("qkv_t")
        wait(vq, "qkv_t", C["qkv_t"])
        wait(vq, "shdma", C["shdma"])
        vq.append(lambda E: E.tensor_copy(out=vstg[:, :], in_=pq2[0][:, :])
                  .then_inc(s["qkv_v"], 1))
        bump("qkv_v")
        wait(qq, "qkv_v", C["qkv_v"])
        dma(qq, "shdma", d_vsh[128 * tt:128 * (tt + 1), :], vstg[:, :])

    wait(gq, "shdma", C["shdma"])
    gq.append(lambda E: E.collective_compute(
        "AllGather", OP.bypass, replica_groups=[list(range(NCORES))],
        ins=[d_vsh.ap().opt()], outs=[d_vg.ap().opt()]).then_inc(s["ag2"], 1))
    bump("ag2")

    for g in range(N_G):
        for th in range(2):
            wait(tq, "qkv_v", C["qkv_v"])

            def f_qmm(E, g=g, th=th):
                for dc in range(16):
                    i2 = E.matmul(pq2[1][:, :], A_q[dc][:, 128 * g:128 * (g + 1)],
                                  ln1T[dc][:, 512 * th:512 * (th + 1)],
                                  start=(dc == 0), stop=(dc == 15))
                    if dc == 15:
                        i2.then_inc(s["qkv_t"], 1)
            tq.append(f_qmm)
            bump("qkv_t")
            wait(vq, "qkv_t", C["qkv_t"])

            def f_qev(E, g=g, th=th):
                E.tensor_copy(out=q_sb[g].ap()[:, 512 * th:512 * (th + 1)],
                              in_=pq2[1][:, :])
                E.drain()
                E.tensor_tensor(out=sqst[:, :],
                                in0=q_sb[g].ap()[:, 512 * th:512 * (th + 1)],
                                in1=q_sb[g].ap()[:, 512 * th:512 * (th + 1)], op=OP.mult)
            vq.append(f_qev)
            dinc(vq, "qkv_v")
            wait(tq, "qkv_v", C["qkv_v"])
            tq.append(lambda E: E.matmul(pqn, bdiag[:, :], sqst[:, :], start=True,
                                         stop=True, skip_group_check=True)
                      .then_inc(s["qkv_t"], 1))
            bump("qkv_t")
            wait(vq, "qkv_t", C["qkv_t"])

            def f_qnred(E, g=g):
                E.tensor_reduce(out=nrm4[0:4, 0:1], in_=pqn, axis=AX, op=OP.max)
                E.drain()
                E.tensor_tensor(out=nrm[0:4, 0:1], in0=nrm[0:4, 0:1],
                                in1=nrm4[0:4, 0:1], op=OP.max)
            vq.append(f_qnred)
            dinc(vq, "qkv_v")
            wait(tq, "qkv_v", C["qkv_v"])
    QKV_T_ALL = C["qkv_t"]
    QKV_V_ALL = C["qkv_v"]

    cutpoint('P4')
    wait(gq, "ag2", C["ag2"])
    wait(gq, "qkv_t", QKV_T_ALL)
    wait(gq, "qkv_v", QKV_V_ALL)
    knin = AP(tensor=d_kg.ap().tensor, offset=HR * TC,
              ap=[[1, 1], [HR * TC + 16, 8]])
    dma(gq, "kvn", kn8[0:1, 0:NCORES], knin, allow_slow_non_contiguous=True)
    for r in range(NCORES):
        for g in range(N_G):
            inap = AP(tensor=d_kg.ap().tensor,
                      offset=r * (HR * TC + 16) + 128 * g * TC,
                      ap=[[TC, 128], [1, TC]])
            outap = AP(tensor=k_sb[g].ap().tensor, offset=r * TC,
                       ap=[[T, 128], [1, TC]])
            dma(gq, "kvload", outap, inap)
        inap = AP(tensor=d_vg.ap().tensor, offset=r * TC * HR,
                  ap=[[HR, 128], [128 * HR, 8], [1, HR]])
        outap = AP(tensor=v_sb.ap().tensor, offset=r * 8 * 512,
                   ap=[[N_KC * HR, 128], [512, 8], [1, 512]])
        dma(gq, "kvload", outap, inap)
        wait(gq, "kvload", C["kvload"])
    KVLOAD = C["kvload"]

    def kv_need(kc):
        return 16 * 5 * (kc // 8 + 1)

    wait(vq, "kvn", 16)
    wait(tq, "qkv_v", QKV_V_ALL)
    tq.append(lambda E: E.transpose(pqn[0:1, 8:12], nrm[0:4, 0:1],
                                    identf[0:4, 0:4]).then_inc(s["mh_t"], 1))
    bump("mh_t")
    wait(vq, "mh_t", C["mh_t"])

    def f_mh1(E):
        E.tensor_reduce(out=nrm[0:1, 0:1], in_=pqn[0:1, 8:12], axis=AX, op=OP.max)
        E.drain()
        E.tensor_reduce(out=nrm[0:1, 1:2], in_=kn8[0:1, 0:NCORES], axis=AX, op=OP.max)
    vq.append(f_mh1)
    dinc(vq, "mh_v")
    wait(sq, "mh_v", C["mh_v"])

    def f_mh2(E):
        E.tensor_tensor(out=nrm[0:1, 2:3], in0=nrm[0:1, 0:1], in1=nrm[0:1, 1:2],
                        op=OP.mult)
    vq.append(f_mh2)
    dinc(vq, "mh_v")
    wait(sq, "mh_v", C["mh_v"])

    def f_mhs(E):
        E.activation(out=nrm[0:1, 2:3], in_=nrm[0:1, 2:3], func=AF.Ln, scale=1.0)
        E.drain()
        E.activation(out=nrm[0:1, 2:3], in_=nrm[0:1, 2:3], func=AF.Exp, scale=0.5)
    sq.append(f_mhs)
    dinc(sq, "mh_s")
    wait(vq, "mh_s", C["mh_s"])

    def f_mh3(E):
        E.tensor_scalar(out=grow[0:1, 33:34], in0=nrm[0:1, 2:3],
                        scalar1=float(-SCALE * MSAFE), scalar2=None, op0=OP.mult)
    vq.append(f_mh3)
    dinc(vq, "mh_v")
    wait(tq, "mh_v", C["mh_v"])
    tq.append(lambda E: E.matmul(pg[0:128, 16:17], grow[0:1, 128:256],
                                 grow[0:1, 33:34], start=True, stop=True,
                                 skip_group_check=True).then_inc(s["mh_t"], 1))
    bump("mh_t")
    wait(vq, "mh_t", C["mh_t"])
    vq.append(lambda E: E.tensor_copy(out=negMb[:, :], in_=pg[0:128, 16:17])
              .then_inc(s["mh_v"], 1))
    bump("mh_v")
    MH_DONE = C["mh_v"]

    cutpoint('P5')
    def qt_sg_kc(i):
        return i // (N_G * N_KC), (i // N_KC) % N_G, i % N_KC

    kv_hi = [0]

    def emit_unit(u):
        i = u // 2
        qt, sg, kc = qt_sg_kc(i)
        if u >= 3:
            wait(tq, "act", u - 2)
        if u == 0:
            wait(tq, "mh_t", C["mh_t"])
        if kv_need(kc) > kv_hi[0]:
            kv_hi[0] = kv_need(kc)
            wait(tq, "kvload", kv_hi[0])
        p = u % 3
        hls = (0, 1) if u % 2 == 0 else (2, 3)

        def f_sc(E, qt=qt, sg=sg, kc=kc, p=p, hls=hls):
            for hl in hls:
                i2 = E.matmul(
                    p6.ap()[:, 1024 * p + 512 * (hl % 2): 1024 * p + 512 * (hl % 2) + 512],
                    k_sb[sg].ap()[32 * hl:32 * hl + 32, 128 * kc:128 * (kc + 1)],
                    q_sb[sg].ap()[32 * hl:32 * hl + 32, QW * qt:QW * qt + QW],
                    start=True, stop=True, tile_position=(32 * hl, 0))
                if hl % 2 == 1:
                    i2.then_inc(s["tS"], 1)
        tq.append(f_sc)
        bump("tS")

    def emit_act(u):
        wait(sq, "tS", u + 1)
        if u >= 8:
            wait(sq, "cons", (u - 8) // 2 + 1)
        if u == 0:
            wait(sq, "mh_v", MH_DONE)
        p, slot = u % 3, u % 8

        def f_act(E, p=p, slot=slot):
            E.activation(out=exp_sb.ap()[:, 1024 * slot:1024 * (slot + 1)],
                         in_=p6.ap()[:, 1024 * p:1024 * (p + 1)],
                         func=AF.Exp, bias=negMb[:, 0:1],
                         scale=float(SCALE)).then_inc(s["act"], 1)
        sq.append(f_act)
        bump("act")

    def emit_avd(j):
        qt, sg, kc = qt_sg_kc(j)
        opens = (kc == 0)
        close = (kc == N_KC - 1)
        wait(tq, "act", 2 * j + 2)
        if opens:
            if j > 0:
                wait(tq, "sgv1", C["sgv1"])

                def f_bc(E):
                    for hl in range(4):
                        i2 = E.matmul(pd.ap()[32 * hl:32 * hl + 32, :],
                                      grow[0:1, 128:160], dsb[0:1, QW * hl:QW * hl + QW],
                                      start=True, stop=True, skip_group_check=True,
                                      tile_position=(0, 32 * hl))
                        if hl == 3:
                            i2.then_inc(s["sgt2"], 1)
                tq.append(f_bc)
                bump("sgt2")
            wait(tq, "sgv2", C["sgv2"])

            def f_open(E):
                E.matmul(pav.ap()[:, :], zerob[:, :], k_sb[0].ap()[:, 0:512],
                         start=True, stop=False, skip_group_check=True)
                E.matmul(pd.ap()[:, :], zerob[:, :], k_sb[0].ap()[:, 0:512],
                         start=True, stop=False, skip_group_check=True)
            tq.append(f_open)
        su = 2 * j
        slotA, slotB = su % 8, (su + 1) % 8

        def f_avd(E, sg=sg, kc=kc, slotA=slotA, slotB=slotB, close=close):
            for hl in range(4):
                slot = slotA if hl < 2 else slotB
                rhs = exp_sb.ap()[:, 1024 * slot + 512 * (hl % 2):
                                  1024 * slot + 512 * (hl % 2) + 512]
                h = 4 * sg + hl
                E.matmul(pav.ap()[32 * hl:32 * hl + 32, :],
                         v_sb.ap()[:, 512 * kc + 32 * h: 512 * kc + 32 * h + 32],
                         rhs, start=False, stop=close,
                         tile_position=(0, 32 * hl), skip_group_check=True)
            for hl in range(4):
                slot = slotA if hl < 2 else slotB
                rhs = exp_sb.ap()[:, 1024 * slot + 512 * (hl % 2):
                                  1024 * slot + 512 * (hl % 2) + 512]
                i2 = E.matmul(pd.ap()[32 * hl:32 * hl + 1, :],
                              onesb[:, 0:1], rhs, start=False, stop=close,
                              tile_position=(0, 32 * hl), skip_group_check=True)
                if hl == 3:
                    i2.then_inc(s["cons"], 1)
        tq.append(f_avd)
        bump("cons")

    def emit_sg_end(j):
        qt, sg, kc = qt_sg_kc(j)
        wait(vq, "cons", j + 1)

        def f_e1(E, qt=qt, sg=sg):
            E.tensor_copy(out=y_sb[sg].ap()[:, QW * qt:QW * qt + QW],
                          in_=pav.ap()[:, :])
            for hl in range(4):
                E.tensor_copy(out=dsb[0:1, QW * hl:QW * hl + QW],
                              in_=pd.ap()[32 * hl:32 * hl + 1, :])
        vq.append(f_e1)
        dinc(vq, "sgv1")
        wait(vq, "sgt2", C["sgt2"] + 1)

        def f_e3(E, qt=qt, sg=sg):
            E.tensor_copy(out=bcsb[:, :], in_=pd.ap()[:, :])
        vq.append(f_e3)
        dinc(vq, "sgv2")

        def f_e4(E, qt=qt, sg=sg):
            E.reciprocal(out=bcsb[:, :], in_=bcsb[:, :])
            E.drain()
            E.tensor_tensor(out=y_sb[sg].ap()[:, QW * qt:QW * qt + QW],
                            in0=y_sb[sg].ap()[:, QW * qt:QW * qt + QW],
                            in1=bcsb[:, :], op=OP.mult).then_inc(s["sgy"], 1)
        vq.append(f_e4)
        bump("sgy")

    pend = 0
    for i in range(GKC):
        emit_unit(2 * i)
        emit_unit(2 * i + 1)
        emit_act(2 * i)
        emit_act(2 * i + 1)
        while pend < GKC and pend + (4 if pend % N_KC < 2 else 2) <= i:
            emit_avd(pend)
            if pend % N_KC == N_KC - 1:
                emit_sg_end(pend)
            pend += 1
    while pend < GKC:
        emit_avd(pend)
        if pend % N_KC == N_KC - 1:
            emit_sg_end(pend)
        pend += 1
    wait(tq, "sgv1", C["sgv1"])

    def f_bc_last(E):
        for hl in range(4):
            i2 = E.matmul(pd.ap()[32 * hl:32 * hl + 32, :],
                          grow[0:1, 128:160], dsb[0:1, QW * hl:QW * hl + QW],
                          start=True, stop=True, skip_group_check=True,
                          tile_position=(0, 32 * hl))
            if hl == 3:
                i2.then_inc(s["sgt2"], 1)
    tq.append(f_bc_last)
    bump("sgt2")
    CONS_ALL = C["cons"]
    SGY_ALL = 8

    cutpoint('P6')
    wait(gq, "cons", CONS_ALL - 48)
    for g in range(N_G):
        for cp in range(2):
            core = 2 * g + cp
            inap = AP(tensor=d_prep_g.ap().tensor,
                      offset=core * PREP_SZ + 3 * 131072,
                      ap=[[2048, 64], [1, 2048]])
            dma(gq, "woeload", woe_v[g][64 * cp:64 * cp + 64, :], inap)
    WOELOAD = C["woeload"]
    wait(tq, "woeload", WOELOAD)
    wait(tq, "sgy", SGY_ALL)
    for qt in range(N_QT):
        for tl in range(4):
            ttg = 4 * qt + tl
            for cc in range(4):
                pb = cc % 2
                if ttg * 4 + cc >= 2:
                    wait(tq, "op_v", C["op_v"] - 1)

                def f_op(E, qt=qt, tl=tl, cc=cc, pb=pb):
                    for g in range(N_G):
                        i2 = E.matmul(pq2[pb][:, :],
                                      y_sb[g].ap()[:, QW * qt + 128 * tl: QW * qt + 128 * tl + 128],
                                      woe_v[g][:, 512 * cc:512 * (cc + 1)],
                                      start=(g == 0), stop=(g == 3))
                        if g == 3:
                            i2.then_inc(s["op_t"], 1)
                tq.append(f_op)
                bump("op_t")
                wait(vq, "op_t", C["op_t"])

                def f_opev(E, ttg=ttg, cc=cc, pb=pb):
                    E.tensor_copy(out=ln2n[ttg][:, 512 * cc:512 * (cc + 1)],
                                  in_=pq2[pb][:, :]).then_inc(s["op_v"], 1)
                vq.append(f_opev)
                bump("op_v")

    cutpoint('P7')
    xn4 = [xnst[0], xnst[1], xrest[0], xrest[1]]
    p8_dcnt = {}

    def p8_A(ttg):
        b = ttg % 4
        if ttg == 0:
            wait(qq, "op_v", 4)
        if ttg >= 4:
            wait(qq, "p8s", p8_dcnt[ttg - 4])
        dma(qq, "xre" + str(b), xn4[b][:, :], d_xs[128 * ttg:128 * (ttg + 1), :])
        wait(vq, "op_v", 4 * (ttg + 1))
        wait(vq, "xre" + str(b), C["xre" + str(b)])
        vq.append(lambda E, ttg=ttg, b=b: ln_stats(E, ln2n[ttg], b))
        dinc(vq, "p8v")

    def p8_B(ttg):
        b = ttg % 4
        emit_ln_s("p8v", "p8s", ln2n[ttg][:, :], ln2n[ttg][:, :], b)

    def p8_C(ttg):
        b = ttg % 4
        wait(vq, "p8s", C["p8s"])

        def f(E, ttg=ttg, b=b):
            E.tensor_tensor(out=xn4[b][:, :], in0=xn4[b][:, :],
                            in1=ln2n[ttg][:, :], op=OP.add)
            E.drain()
            ln_stats(E, xn4[b], b)
        vq.append(f)
        dinc(vq, "p8v")

    def p8_D(ttg):
        b = ttg % 4
        if ttg >= 2:
            wait(sq, "seq_t", C["seq_t"])
        emit_ln_s("p8v", "p8s", ln3stg[ttg % 2][:, :], xn4[b][:, :], b)
        p8_dcnt[ttg] = C["p8s"]

    def p8_EF(ttg):
        b = ttg % 2
        for half in range(2):
            wait(tq, "p8s", C["p8s"])
            if ttg or half:
                wait(tq, "tr3", C["tr3"])

            def f_tr3(E, b=b, half=half):
                for j in range(8):
                    dc = 8 * half + j
                    i2 = E.transpose(ptr_b8[:, 128 * j:128 * (j + 1)],
                                     ln3stg[b][:, 128 * dc:128 * (dc + 1)], identb[:, :])
                    if j == 7:
                        i2.then_inc(s["tr3"], 1)
            tq.append(f_tr3)
            bump("tr3")
            wait(vq, "tr3", C["tr3"])

            def f_tr3e(E, half=half, ttg=ttg):
                for j in range(8):
                    dc = 8 * half + j
                    i2 = E.tensor_copy(out=ln3T[dc][:, 128 * ttg:128 * (ttg + 1)],
                                       in_=ptr_b8[:, 128 * j:128 * (j + 1)])
                    if j == 7:
                        i2.then_inc(s["tr3"], 1)
            vq.append(f_tr3e)
            bump("tr3")
        wait(tq, "p8s", C["p8s"])

        def f_seq(E, ttg=ttg, b=b):
            for dcq in range(4):
                i2 = E.matmul(p6.ap()[0:1, 512 * dcq:512 * (dcq + 1)],
                              onesb[:, 0:1],
                              ln3stg[b][:, 512 * dcq:512 * (dcq + 1)],
                              start=(ttg == 0), stop=(ttg == 7),
                              skip_group_check=True)
                if dcq == 3:
                    i2.then_inc(s["seq_t"], 1)
        tq.append(f_seq)
        bump("seq_t")
        wait(vq, "seq_t", C["seq_t"])

    p8_A(0); p8_B(0); p8_A(1); p8_B(1)
    for t in range(8):
        p8_C(t); p8_D(t)
        if t + 2 < 8:
            p8_A(t + 2); p8_B(t + 2)
        if t >= 1:
            p8_EF(t - 1)
    p8_EF(7)
    wait(vq, "seq_t", C["seq_t"])
    vq.append(lambda E: E.tensor_copy(out=seqsb[0:1, :], in_=p6.ap()[0:1, 0:2048])
              .then_inc(s["seq_v"], 1))
    bump("seq_v")
    wait(qq, "seq_v", C["seq_v"])
    dma(qq, "seqdma", d_seqp.ap(), seqsb[0:1, :])
    wait(gq, "seqdma", C["seqdma"])
    gq.append(lambda E: E.collective_compute(
        "AllReduce", OP.add, replica_groups=[list(range(NCORES))],
        ins=[d_seqp.ap().opt()], outs=[d_seqs.ap().opt()]).then_inc(s["ar"], 1))
    bump("ar")
    wait(gq, "ar", C["ar"])
    dma(gq, "seqload", seqsb[0:1, :], d_seqs.ap())
    for dc in range(16):
        dma(gq, "simload", simsb[:, 32 * dc:32 * (dc + 1)],
            d_sim[128 * dc:128 * (dc + 1), :])
    SIMLOAD = C["simload"]
    wait(tq, "seqload", C["seqload"])
    wait(tq, "tr3", C["tr3"])

    def f_sct(E):
        for j in range(16):
            i2 = E.transpose(pg[0:128, 20 + j:21 + j],
                             seqsb[0:1, 128 * j:128 * (j + 1)], identf[0:1, 0:1],
                             )
            if j == 15:
                i2.then_inc(s["g_t"], 1)
    tq.append(f_sct)
    bump("g_t")
    wait(vq, "g_t", C["g_t"])
    vq.append(lambda E: E.tensor_copy(out=seqcol[:, :], in_=pg[0:128, 20:36])
              .then_inc(s["g_v"], 1))
    bump("g_v")

    cutpoint('P8')
    wait(vq, "simload", SIMLOAD)
    vq.append(lambda E: E.tensor_tensor(out=sim2s[:, :], in0=simsb[:, :],
                                        in1=simsb[:, :], op=OP.mult))
    dinc(vq, "g_v")
    wait(tq, "g_v", C["g_v"])
    wait(tq, "seq_v", C["seq_v"])

    def f_g2(E):
        for dc in range(16):
            E.matmul(pg[0:1, 0:32], onesf[:, 0:1], sim2s[:, 32 * dc:32 * (dc + 1)],
                     start=(dc == 0), stop=(dc == 15), skip_group_check=True)
        for dc in range(16):
            E.matmul(pg[0:1, 64:96], seqcol[:, dc:dc + 1], simsb[:, 32 * dc:32 * (dc + 1)],
                     start=(dc == 0), stop=(dc == 15), skip_group_check=True)
        for dc in range(16):
            i2 = E.matmul(pg[0:1, 96:97], seqcol[:, dc:dc + 1], seqcol[:, dc:dc + 1],
                          start=(dc == 0), stop=(dc == 15), skip_group_check=True)
            if dc == 15:
                i2.then_inc(s["g_t"], 1)
    tq.append(f_g2)
    bump("g_t")
    wait(sq, "g_t", C["g_t"])

    def f_g3(E):
        E.activation(out=grow[0:1, 0:32], in_=pg[0:1, 0:32], func=AF.Ln, scale=1.0)
        E.drain()
        E.activation(out=grow[0:1, 0:32], in_=grow[0:1, 0:32], func=AF.Exp, scale=-0.5)
        E.drain()
        E.activation(out=grow[0:1, 34:35], in_=pg[0:1, 96:97], func=AF.Ln, scale=1.0)
        E.drain()
        E.activation(out=grow[0:1, 34:35], in_=grow[0:1, 34:35], func=AF.Exp, scale=-0.5)
        E.drain()
        E.activation(out=grow[0:1, 64:96], in_=gatesb[0:1, :], func=AF.Exp, scale=-1.0)
        E.drain()
        E.activation(out=grow[0:1, 96:128], in_=pg[0:1, 64:96], func=AF.Copy, scale=1.0)
    sq.append(f_g3)
    dinc(sq, "g_s")
    wait(vq, "g_s", C["g_s"])

    def f_g4(E):
        E.tensor_scalar(out=grow[0:1, 64:96], in0=grow[0:1, 64:96],
                        scalar1=1.0, scalar2=None, op0=OP.add)
        E.drain()
        E.reciprocal(out=grow[0:1, 64:96], in_=grow[0:1, 64:96])
        E.drain()
        E.tensor_scalar(out=grow[0:1, 96:128], in0=grow[0:1, 96:128],
                        scalar1=grow[0:1, 34:35], scalar2=None, op0=OP.mult)
        E.drain()
        E.tensor_tensor(out=grow[0:1, 96:128], in0=grow[0:1, 96:128],
                        in1=grow[0:1, 0:32], op=OP.mult)
        E.drain()
        E.tensor_tensor(out=grow[0:1, 96:128], in0=grow[0:1, 96:128],
                        in1=grow[0:1, 64:96], op=OP.subtract)
    vq.append(f_g4)
    dinc(vq, "g_v")
    wait(tq, "g_v", C["g_v"])

    def f_g5(E):
        E.transpose(pg[0:EE, 37:38], grow[0:1, 96:128], identf[0:1, 0:1])
        E.matmul(pg[0:EE, 128:160], grow[0:1, 128:160], grow[0:1, 96:128],
                 start=True, stop=True, skip_group_check=True)
        E.matmul(pg[0:EE, 160:192], grow[0:1, 96:128], grow[0:1, 128:160],
                 start=True, stop=True, skip_group_check=True).then_inc(s["g_t"], 1)
    tq.append(f_g5)
    bump("g_t")
    wait(vq, "g_t", C["g_t"])

    def f_g6(E):
        E.tensor_copy(out=gsc[0:EE, 0:1], in_=pg[0:EE, 37:38])
        E.tensor_copy(out=gmx[0:EE, :], in_=pg[0:EE, 128:160])
        E.tensor_copy(out=gmx2[0:EE, :], in_=pg[0:EE, 160:192])
        E.drain()
        E.tensor_scalar(out=gsc[0:EE, 1:2], in0=gsc[0:EE, 0:1],
                        scalar1=0.0, scalar2=None, op0=OP.max)
        E.drain()
        E.tensor_scalar(out=gsc[0:EE, 2:3], in0=gsc[0:EE, 1:2],
                        scalar1=0.0, scalar2=None, op0=OP.is_gt)
        E.drain()
        E.tensor_tensor(out=gtmp[0:EE, :], in0=gmx[0:EE, :], in1=gmx2[0:EE, :],
                        op=OP.is_equal)
        E.drain()
        E.tensor_tensor(out=gtmp[0:EE, :], in0=gtmp[0:EE, :], in1=ltm[0:EE, :],
                        op=OP.mult)
        E.drain()
        E.tensor_tensor(out=gmx2[0:EE, :], in0=gmx[0:EE, :], in1=gmx2[0:EE, :],
                        op=OP.is_gt)
        E.drain()
        E.tensor_tensor(out=gmx2[0:EE, :], in0=gmx2[0:EE, :], in1=gtmp[0:EE, :],
                        op=OP.add)
        E.drain()
        E.tensor_reduce(out=gsc[0:EE, 3:4], in_=gmx2[0:EE, :], axis=AX, op=OP.add)
        E.drain()
        E.tensor_scalar(out=gsc[0:EE, 4:5], in0=gsc[0:EE, 3:4],
                        scalar1=16.0, scalar2=None, op0=OP.is_lt)
    vq.append(f_g6)
    dinc(vq, "g_v")
    wait(tq, "g_v", C["g_v"])
    tq.append(lambda E: E.transpose(pg[0:1, 38:70], gsc[0:EE, 2:3],
                                    identf[0:EE, 0:EE]).then_inc(s["g_t"], 1))
    bump("g_t")
    wait(vq, "g_t", C["g_t"])

    def f_g8(E):
        E.tensor_reduce(out=grow[0:1, 200:201], in_=pg[0:1, 38:70], axis=AX, op=OP.add)
        E.drain()
        E.tensor_scalar(out=grow[0:1, 200:201], in0=grow[0:1, 200:201],
                        scalar1=0.0, scalar2=None, op0=OP.is_equal)
    vq.append(f_g8)
    dinc(vq, "g_v")
    wait(tq, "g_v", C["g_v"])
    tq.append(lambda E: E.matmul(pg[0:EE, 36:37], grow[0:1, 128:160],
                                 grow[0:1, 200:201], start=True, stop=True,
                                 skip_group_check=True).then_inc(s["g_t"], 1))
    bump("g_t")
    wait(vq, "g_t", C["g_t"])

    def f_g10(E):
        E.tensor_copy(out=gsc[0:EE, 5:6], in_=pg[0:EE, 36:37])
        E.drain()
        E.tensor_tensor(out=gsc[0:EE, 6:7], in0=gsc[0:EE, 2:3], in1=gsc[0:EE, 4:5],
                        op=OP.max)
        E.drain()
        E.tensor_tensor(out=gsc[0:EE, 6:7], in0=gsc[0:EE, 6:7], in1=gsc[0:EE, 2:3],
                        op=OP.subtract)
        E.drain()
        E.tensor_tensor(out=gsc[0:EE, 6:7], in0=gsc[0:EE, 6:7], in1=gsc[0:EE, 5:6],
                        op=OP.mult)
        E.drain()
        E.tensor_tensor(out=gsc[0:EE, 6:7], in0=gsc[0:EE, 6:7], in1=gsc[0:EE, 2:3],
                        op=OP.add)
        E.drain()
        E.tensor_tensor(out=gsc[0:EE, 7:8], in0=gsc[0:EE, 1:2], in1=gsc[0:EE, 6:7],
                        op=OP.mult)
        E.drain()
        E.tensor_scalar(out=gsc[0:EE, 8:9], in0=gsc[0:EE, 6:7],
                        scalar1=1.0, scalar2=BIG, op0=OP.subtract, op1=OP.mult)
        E.drain()
        E.tensor_tensor(out=gsc[0:EE, 7:8], in0=gsc[0:EE, 7:8], in1=gsc[0:EE, 8:9],
                        op=OP.add)
    vq.append(f_g10)
    dinc(vq, "g_v")
    wait(tq, "g_v", C["g_v"])
    tq.append(lambda E: E.transpose(pg[0:1, 70:102], gsc[0:EE, 7:8],
                                    identf[0:EE, 0:EE]).then_inc(s["g_t"], 1))
    bump("g_t")
    wait(vq, "g_t", C["g_t"])

    def f_g12(E):
        E.tensor_copy(out=grow[0:1, 224:256], in_=pg[0:1, 70:102])
        E.drain()
        E.tensor_reduce(out=grow[0:1, 201:202], in_=grow[0:1, 224:256], axis=AX, op=OP.max)
        E.drain()
        E.tensor_scalar(out=grow[0:1, 202:203], in0=grow[0:1, 201:202],
                        scalar1=-1.0, scalar2=None, op0=OP.mult)
    vq.append(f_g12)
    dinc(vq, "g_v")
    wait(sq, "g_v", C["g_v"])
    sq.append(lambda E: E.activation(out=grow[0:1, 224:256], in_=grow[0:1, 224:256],
                                     func=AF.Exp, bias=grow[0:1, 202:203], scale=1.0))
    dinc(sq, "g_s")
    wait(vq, "g_s", C["g_s"])

    def f_g14(E):
        E.tensor_reduce(out=grow[0:1, 203:204], in_=grow[0:1, 224:256], axis=AX, op=OP.add)
        E.drain()
        E.reciprocal(out=grow[0:1, 203:204], in_=grow[0:1, 203:204])
        E.drain()
        E.tensor_scalar(out=grow[0:1, 224:256], in0=grow[0:1, 224:256],
                        scalar1=grow[0:1, 203:204], scalar2=None, op0=OP.mult)
    vq.append(f_g14)
    dinc(vq, "g_v")
    wait(tq, "g_v", C["g_v"])
    tq.append(lambda E: E.transpose(pg[0:EE, 102:103], grow[0:1, 224:256],
                                    identf[0:1, 0:1]).then_inc(s["g_t"], 1))
    bump("g_t")
    wait(vq, "g_t", C["g_t"])
    vq.append(lambda E: E.tensor_copy(out=routb[0:EE, :], in_=pg[0:EE, 102:103])
              .then_inc(s["g_v"], 1))
    bump("g_v")
    ROUT_DONE = C["g_v"]

    cutpoint('P9')
    for ch in range(8):
        if ch == 0:
            wait(qq, "seqdma", C["seqdma"])
            wait(qq, "op_v", 32)
        else:
            wait(qq, "ffw_v", C["ffw_v"])
        inap = AP(tensor=d_wvf.tensor, offset=ch * D, ap=[[8 * D, 32], [1, D]])
        dma(qq, "ffwload", wvstage[0:32, :], inap)
        wait(vq, "ffwload", C["ffwload"])
        if ch == 0:
            wait(vq, "seqdma", C["seqdma"])

        def f_cast(E, ch=ch):
            for j in range(4):
                E.tensor_copy(out=wvb[4 * ch + j][0:32, :],
                              in_=wvstage[0:32, 512 * j:512 * (j + 1)])
        vq.append(f_cast)
        dinc(vq, "ffw_v")
    WV_CAST = C["ffw_v"]
    wait(tq, "ffw_v", WV_CAST)
    wait(tq, "g_v", ROUT_DONE)

    def f_fvm(E):
        for hh in range(8):
            for cc in range(16):
                fi = (2048 * hh + 128 * cc) // 512
                co = (2048 * hh + 128 * cc) % 512
                i2 = E.matmul(pg[0:128, 128 + 16 * hh + cc:129 + 16 * hh + cc],
                              wvb[fi][0:32, co:co + 128], routb[0:EE, :],
                              start=True, stop=True, skip_group_check=True)
                if hh == 7 and cc == 15:
                    i2.then_inc(s["ffw_t"], 1)
    tq.append(f_fvm)
    bump("ffw_t")
    wait(vq, "ffw_t", C["ffw_t"])
    vq.append(lambda E: E.tensor_copy(out=ffst[:, 0:128], in_=pg[0:128, 128:256])
              .then_inc(s["ffw_e"], 1))
    bump("ffw_e")
    wait(tq, "ffw_e", C["ffw_e"])
    tq.append(lambda E: E.transpose(ptr_b8[:, 0:128], ffst[:, 0:128], identb[:, :])
              .then_inc(s["fftr_t"], 1))
    bump("fftr_t")
    wait(vq, "fftr_t", C["fftr_t"])
    vq.append(lambda E: E.tensor_copy(out=ffst[:, 128:256], in_=ptr_b8[:, 0:128])
              .then_inc(s["ffw_e"], 1))
    bump("ffw_e")
    wait(qq, "ffw_e", C["ffw_e"])
    fvout = AP(tensor=d_ffnp.ap().tensor, offset=0, ap=[[128, 128], [1, 128]])
    dma(qq, "ffpdma", fvout, ffst[:, 128:256])
    for ch in range(8):
        wait(qq, "ffw_v", C["ffw_v"])
        if ch == 0:
            wait(qq, "ffw_t", C["ffw_t"])
        inap = AP(tensor=d_owf.tensor, offset=ch * D, ap=[[8 * D, 32], [1, D]])
        dma(qq, "ffwload", wvstage[0:32, :], inap)
        wait(vq, "ffwload", C["ffwload"])
        if ch == 0:
            wait(vq, "ffw_t", C["ffw_t"])

        def f_cast2(E, ch=ch):
            for j in range(4):
                E.tensor_copy(out=wvb[4 * ch + j][0:32, :],
                              in_=wvstage[0:32, 512 * j:512 * (j + 1)])
        vq.append(f_cast2)
        dinc(vq, "ffw_v")
    wait(tq, "ffw_v", C["ffw_v"])
    wait(tq, "ffw_e", C["ffw_e"])

    def f_fom(E):
        for hh in range(8):
            for cc in range(16):
                fi = (2048 * hh + 128 * cc) // 512
                co = (2048 * hh + 128 * cc) % 512
                i2 = E.matmul(pg[0:128, 256 + 16 * hh + cc:257 + 16 * hh + cc],
                              wvb[fi][0:32, co:co + 128], routb[0:EE, :],
                              start=True, stop=True, skip_group_check=True)
                if hh == 7 and cc == 15:
                    i2.then_inc(s["ffw_t"], 1)
    tq.append(f_fom)
    bump("ffw_t")
    wait(vq, "ffw_t", C["ffw_t"])
    vq.append(lambda E: E.tensor_copy(out=ffst[:, 256:384], in_=pg[0:128, 256:384])
              .then_inc(s["ffw_e"], 1))
    bump("ffw_e")
    wait(tq, "ffw_e", C["ffw_e"])
    tq.append(lambda E: E.transpose(ptr_b8[:, 0:128], ffst[:, 256:384], identb[:, :])
              .then_inc(s["fftr_t"], 1))
    bump("fftr_t")
    wait(vq, "fftr_t", C["fftr_t"])
    vq.append(lambda E: E.tensor_copy(out=ffst[:, 384:512], in_=ptr_b8[:, 0:128])
              .then_inc(s["ffw_e"], 1))
    bump("ffw_e")
    wait(qq, "ffw_e", C["ffw_e"])
    foout = AP(tensor=d_ffnp.ap().tensor, offset=8 * D, ap=[[128, 128], [1, 128]])
    dma(qq, "ffpdma", foout, ffst[:, 384:512])

    wait(gq, "ffpdma", C["ffpdma"])
    gq.append(lambda E: E.collective_compute(
        "AllGather", OP.bypass, replica_groups=[list(range(NCORES))],
        ins=[d_ffnp.ap().opt()], outs=[d_ffng.ap().opt()]).then_inc(s["ag3"], 1))
    bump("ag3")
    wait(gq, "ag3", C["ag3"])
    for r in range(NCORES):
        inap = AP(tensor=d_ffng.ap().tensor, offset=r * 16 * D, ap=[[D, 8], [1, D]])
        dma(gq, "ffgload", wvgsb[8 * r:8 * r + 8, :], inap)
        inap2 = AP(tensor=d_ffng.ap().tensor, offset=r * 16 * D + 8 * D,
                   ap=[[D, 8], [1, D]])
        dma(gq, "ffgload", wodsb[8 * r:8 * r + 8, :], inap2)
    FFG = C["ffgload"]
    wait(tq, "ffgload", FFG)
    for dc in range(16):
        if dc:
            wait(tq, "fftr_v", C["fftr_v"])
        tq.append(lambda E, dc=dc: E.transpose(ptr_b8[:, 0:64],
                                               wvgsb[0:64, 128 * dc:128 * (dc + 1)],
                                               identb[0:64, 0:64]).then_inc(s["fftr_t"], 1))
        bump("fftr_t")
        wait(vq, "fftr_t", C["fftr_t"])
        vq.append(lambda E, dc=dc: E.tensor_copy(out=wveffT[dc][:, :],
                                                 in_=ptr_b8[:, 0:64]).then_inc(s["fftr_v"], 1))
        bump("fftr_v")

    cutpoint('P10')
    for qt in range(N_QT):
        if qt == 0:
            wait(tq, "fftr_v", C["fftr_v"])
        else:
            wait(tq, "ffc_v", C["ffc_v"])

        def f_cmm(E, qt=qt):
            for dc in range(16):
                i2 = E.matmul(pq2[0][0:64, :], wveffT[dc][:, :],
                              ln3T[dc][:, QW * qt:QW * qt + QW],
                              start=(dc == 0), stop=(dc == 15))
                if dc == 15:
                    i2.then_inc(s["ffc_t"], 1)
        tq.append(f_cmm)
        bump("ffc_t")
        wait(vq, "ffc_t", C["ffc_t"])
        vq.append(lambda E, qt=qt: E.tensor_copy(out=combT[0:64, QW * qt:QW * qt + QW],
                                                 in_=pq2[0][0:64, :]).then_inc(s["ffc_v"], 1))
        bump("ffc_v")
    wait(tq, "ffc_v", C["ffc_v"])
    wait(tq, "fftr_v", C["fftr_v"])
    ffo_banks = [[p6.ap()[:, 512 * dcq:512 * (dcq + 1)] for dcq in range(4)],
                 [p6.ap()[:, 2048:2560], p6.ap()[:, 2560:3072],
                  pav.ap()[:, :], pd.ap()[:, :]]]

    def ffo_evac_ap(sel):
        if sel == 0:
            return [p6.ap()[0:128, 0:2048]]
        return [p6.ap()[0:128, 2048:3072], pav.ap()[:, :], pd.ap()[:, :]]

    for ttg in range(8):
        sel = ttg % 2
        if ttg >= 2:
            wait(tq, "ffo_v", C["ffo_v"] - 1)

        def f_fmm(E, ttg=ttg, sel=sel):
            for dcq in range(4):
                i2 = E.matmul(ffo_banks[sel][dcq],
                              combT[0:64, 128 * ttg:128 * (ttg + 1)],
                              wodsb[0:64, 512 * dcq:512 * (dcq + 1)],
                              start=True, stop=True, skip_group_check=True)
                if dcq == 3:
                    i2.then_inc(s["ffo_t"], 1)
        tq.append(f_fmm)
        bump("ffo_t")
        wait(vq, "ffo_t", C["ffo_t"])

        def f_fev(E, ttg=ttg, sel=sel):
            if sel == 0:
                E.tensor_copy(out=ffn_t[ttg][:, :], in_=p6.ap()[0:128, 0:2048])
                E.drain().then_inc(s["ffo_v"], 1)
            else:
                E.tensor_copy(out=ffn_t[ttg][:, 0:1024], in_=p6.ap()[0:128, 2048:3072])
                E.tensor_copy(out=ffn_t[ttg][:, 1024:1536], in_=pav.ap()[:, :])
                E.tensor_copy(out=ffn_t[ttg][:, 1536:2048], in_=pd.ap()[:, :])
                E.drain().then_inc(s["ffo_v"], 1)
        vq.append(f_fev)
        bump("ffo_v")
    FFO_ALL = C["ffo_v"]

    p11_ccnt = {}

    def p11_A(ttg):
        b = ttg % 4
        if ttg == 0:
            wait(qq, "ffw_t", C["ffw_t"])
            wait(qq, "ffw_v", C["ffw_v"])
            wait(qq, "fftr_v", C["fftr_v"])
            wait(qq, "p8s", C["p8s"])
        if ttg >= 4:
            wait(qq, "p11v", p11_ccnt[ttg - 4])
        dma(qq, "xre" + str(b), xn4[b][:, :], d_xs[128 * ttg:128 * (ttg + 1), :])
        wait(vq, "ffo_v", min(FFO_ALL, ttg + 1))
        wait(vq, "xre" + str(b), C["xre" + str(b)])
        vq.append(lambda E, ttg=ttg, b=b: ln_stats(E, ffn_t[ttg], b))
        dinc(vq, "p11v")

    def p11_B(ttg):
        b = ttg % 4
        emit_ln_s("p11v", "p11s", ffn_t[ttg][:, :], ffn_t[ttg][:, :], b)

    def p11_C(ttg):
        b = ttg % 4
        wait(vq, "p11s", C["p11s"])

        def f(E, ttg=ttg, b=b):
            E.tensor_tensor(out=ffn_t[ttg][:, :], in0=ffn_t[ttg][:, :],
                            in1=xn4[b][:, :], op=OP.add)
            E.drain()
            E.tensor_tensor(out=ffn_t[ttg][:, :], in0=ffn_t[ttg][:, :],
                            in1=ln2n[ttg][:, :], op=OP.add)
        vq.append(f)
        dinc(vq, "p11v")
        p11_ccnt[ttg] = C["p11v"]
        wait(qq, "p11v", C["p11v"])
        dma(qq, "store", d_out[128 * ttg:128 * (ttg + 1), :], ffn_t[ttg][:, :])

    p11_A(0); p11_B(0); p11_A(1); p11_B(1)
    for t in range(8):
        p11_C(t)
        if t + 2 < 8:
            p11_A(t + 2); p11_B(t + 2)
    wait(gq, "store", C["store"])

    with nc.Block() as blk:
        @blk.tensor
        def _(E):
            for f in tq:
                f(E)

        @blk.vector
        def _(E):
            for f in vq:
                f(E)

        @blk.scalar
        def _(E):
            for f in sq:
                f(E)

        @blk.sync
        def _(E):
            for f in qq:
                f(E)

        @blk.gpsimd
        def _(E):
            for f in gq:
                f(E)

    stack.close()
    return nc


def shard_inputs(inputs):
    bf = ml_dtypes.bfloat16
    X = np.ascontiguousarray(np.asarray(inputs["L0_attn_x"], np.float32).reshape(T, D))
    Wq = np.asarray(inputs["L0_attn_Wq"], np.float32)
    Wk = np.asarray(inputs["L0_attn_Wk"], np.float32)
    Wv = np.asarray(inputs["L0_attn_Wv"], np.float32)
    Wo = np.asarray(inputs["L0_attn_Wo"], np.float32)
    Wdn = np.ascontiguousarray(np.asarray(inputs["L0_attn_Wdown"], np.float32))
    Wup = np.ascontiguousarray(np.asarray(inputs["L0_attn_Wup"], np.float32))
    sim = np.ascontiguousarray(np.asarray(inputs["L0_ffn_sim_matrix"], np.float32))
    gates = np.ascontiguousarray(np.asarray(inputs["L0_ffn_gates"], np.float32).reshape(1, EE))
    w_v = np.asarray(inputs["L0_ffn_w_v"], np.float32).reshape(EE, 64, D)
    o_w = np.asarray(inputs["L0_ffn_o_weights"], np.float32).reshape(EE, 64, D)

    ident = np.eye(128, dtype=np.float32)
    ltmask = np.tril(np.ones((EE, EE), np.float32), k=-1)
    expander = np.zeros((4, 128), np.float32)
    for hl in range(4):
        expander[hl, 32 * hl:32 * hl + 32] = 1.0
    bdg = np.zeros((128, 4), np.float32)
    for hl in range(4):
        bdg[32 * hl:32 * hl + 32, hl] = 1.0

    maps = []
    for c in range(NCORES):
        maps.append({
            "xs": np.ascontiguousarray(X[TC * c:TC * (c + 1)]),
            "wq_r": np.ascontiguousarray(Wq[256 * c:256 * (c + 1)]),
            "wk_r": np.ascontiguousarray(Wk[256 * c:256 * (c + 1)]),
            "wv_r": np.ascontiguousarray(Wv[256 * c:256 * (c + 1)]),
            "wo_c": np.ascontiguousarray(Wo[:, 256 * c:256 * (c + 1)]),
            "wdown": Wdn, "wup": Wup, "sim": sim, "gates": gates,
            "wvf": np.ascontiguousarray(w_v[:, 8 * c:8 * (c + 1), :].reshape(EE, 8 * D)),
            "owf": np.ascontiguousarray(o_w[:, 8 * c:8 * (c + 1), :].reshape(EE, 8 * D)),
            "identb": ident.astype(bf), "identf": ident,
            "ltmask": ltmask, "expander": expander,
            "bdiag": bdg.astype(bf),
        })
    return maps


def assemble(results):
    return np.concatenate([r["out"] for r in results], axis=0).reshape(1, T, D)



def kernel(**inputs):
    from concourse.bass_utils import run_bass_kernel_spmd
    nc = build(debug=False)
    maps = shard_inputs(inputs)
    res = run_bass_kernel_spmd(nc, maps, core_ids=list(range(NCORES)))
    return assemble(res.results)
